# version 03
# speedup vs baseline: 1.0535x; 1.0516x over previous
import math
import numpy as np
import ml_dtypes
from contextlib import ExitStack

import concourse.bass as bass
import concourse.mybir as mybir
from concourse.bass_utils import run_bass_kernel_spmd

F32 = mybir.dt.float32
BF16 = mybir.dt.bfloat16
AF = mybir.ActivationFunctionType
ALU = mybir.AluOpType
AX = mybir.AxisListType

NCORES = 8
D = 1024
T = 4096
DT8 = 8
TC8 = 8
NB_F32 = np.float32
NB_BF16 = ml_dtypes.bfloat16

PREFIX = 4096 - 1369
NROPE = 1369


class Ctr:
    def __init__(self, ctx, nc, name):
        self.h = ctx.enter_context(nc.semaphore(name))
        self.v = 0

    def inc(self, inst, n=1):
        inst.then_inc(self.h, n)
        self.v += n
        return self.v


HANDOFF_NAMES = {"x2T", "z2", "x3T", "z3", "x4T", "z4", "x5T", "z5",
                 "x6T", "z6", "x7T", "z7"}


class Builder:
    def __init__(self, name="prog", race_checks=False, fused=False):
        self.nc = bass.Bass("TRN2", target_bir_lowering=False, debug=False,
                            detect_race_conditions=race_checks)
        self.ctx = ExitStack()
        self.name = name
        self.fused = fused
        self.phase = ""
        self.handoff = {}
        self.ctrs = {}
        self.scope = None
        self.block = None

    def finish(self):
        self.ctx.close()
        return self

    def get_block(self):
        if self.block is None:
            self.block = self.ctx.enter_context(self.nc.Block())
        return self.block

    def sb(self, name, shape, dt):
        stack = self.scope if self.scope is not None else self.ctx
        return stack.enter_context(self.nc.sbuf_tensor(self.phase + name, shape, dt))

    def ps(self, name, shape, dt=F32):
        stack = self.scope if self.scope is not None else self.ctx
        return stack.enter_context(self.nc.psum_tensor(self.phase + name, shape, dt))

    def ctr(self, name):
        if self.fused:
            if name not in self.ctrs:
                self.ctrs[name] = Ctr(self.ctx, self.nc, name)
            return self.ctrs[name]
        return Ctr(self.ctx, self.nc, name)

    def din(self, name, shape, dt):
        if self.fused and name in self.handoff:
            return self.handoff[name]
        return self.nc.dram_tensor(self.phase + name, shape, dt, kind="ExternalInput").ap()

    def dout(self, name, shape, dt):
        if self.fused and name in HANDOFF_NAMES:
            ap = self.nc.dram_tensor(name, shape, dt, kind="Internal").ap()
            self.handoff[name] = ap
            return ap
        return self.nc.dram_tensor(self.phase + name, shape, dt, kind="ExternalOutput").ap()

    def dint(self, name, shape, dt):
        return self.nc.dram_tensor(self.phase + name, shape, dt, kind="Internal").ap()


def bfc(x):
    return np.ascontiguousarray(np.asarray(x), dtype=NB_BF16)


def f32c(x):
    return np.ascontiguousarray(np.asarray(x), dtype=NB_F32)


def col_layout(bvec, ntile):
    return f32c(np.asarray(bvec, dtype=np.float32).reshape(ntile, 128).T)



def build_launch_A(debug=False, npairs=8, do_tail=True, proj_only=False):
    standalone = True
    b = Builder("A")
    nc = b.nc
    xT0_d = b.din("xT0", [D, T], F32)
    wqkv_d = b.din("wqkv", [D, 3 * D], F32)
    bq_col_d = b.din("bq_col", [128, 24], F32)
    wproj_d = b.din("wproj", [D, D], F32)
    bp_col_d = b.din("bp_col", [128, 8], F32)
    cosP_d = b.din("cosP", [128, NROPE], F32)
    sinP_d = b.din("sinP", [128, NROPE], F32)
    wfT_d = b.din("wfT", [D, D], F32)
    wiT_d = b.din("wiT", [D, D], F32)
    wcT_d = b.din("wcT", [D, D], F32)
    ones_b_d = b.din("ones_b", [128, 128], BF16)
    bh2_d = b.din("bh2", [128, 2], BF16)
    ident_f_d = b.din("ident_f", [128, 128], F32)

    x1T_d = b.dout("x1T", [D, T], F32)
    z1_d = b.dout("z1", [D, T], BF16)
    gates_d = b.dout("gatesT", [D, 3 * T], BF16)
    oT_d = (b.dout if debug else b.dint)("oT_scratch", [D, T], BF16)
    if debug:
        z0_dbg = b.dout("z0_dbg", [D, T], BF16)
        qt_dbg = b.dout("qt_dbg", [128, T], BF16)
        kt_dbg = b.dout("kt_dbg", [128, T], BF16)
        nc_dbg = b.dout("nc_dbg", [128, 2], F32)
        csb_dbg = b.dout("csb_dbg", [2, 8], F32)
        csr_dbg = b.dout("csr_dbg", [1, 2], F32)
        cmx_dbg = b.dout("cmx_dbg", [2, 32], F32)

    xs = b.sb("xs", [128, 2 * T], F32)
    z0 = b.sb("z0", [128, DT8 * T], BF16)
    xb0 = b.sb("xb0", [128, T // 2], BF16)
    xsq = b.sb("xsq", [128, T // 2], BF16)
    Mt = b.sb("Mt", [128, 512], F32)
    Rt = b.sb("Rt", [128, 512], F32)
    stat_sc = b.sb("stat_sc", [128, 512], F32)
    zsc = b.sb("zsc", [128, 512], F32)
    wpair = b.sb("wpair", [128, 2 * 24 * 128], BF16)
    QT = b.sb("QT", [128, T], BF16)
    KT = b.sb("KT", [128, T], BF16)
    VTt = b.sb("VTt", [128, 2 * 512], F32)
    vst = b.sb("vst", [128, 32 * 130], BF16)
    probs = b.sb("probs", [128, 2 * 1024], BF16)
    cosP = b.sb("cosP_s", [128, NROPE], BF16)
    sinN = b.sb("sinN", [128, NROPE], BF16)
    ropeA = b.sb("ropeA", [128, NROPE], BF16)
    qsw = b.sb("qsw", [128, NROPE], BF16)
    ones_bt = b.sb("ones_bt", [128, 128], BF16)
    ident_f = b.sb("ident_f_s", [128, 128], F32)
    onesf = b.sb("onesf", [1, 128], F32)
    fc = b.sb("fc", [128, 4], F32)
    bq_col = b.sb("bq_col_s", [128, 24], F32)
    bp_col = b.sb("bp_col_s", [128, 8], F32)
    oTs = b.sb("oTs", [128, 2 * 1024], BF16)
    recipR = b.sb("recipR", [1, 1024], F32)
    rB = b.sb("rB", [64, 1024], F32)
    negC = b.sb("negC", [128, 2], F32)
    cmx = b.sb("cmx", [2, 32], F32)
    csb = b.sb("csb", [2, 8], F32)
    csr = b.sb("csr", [1, 2], F32)
    sqb = b.sb("sqb", [128, 2048], BF16)
    bh2 = b.sb("bh2_s", [128, 2], BF16)
    x1s = b.sb("x1s", [128, 2 * 512], F32)
    evb = b.sb("evb", [128, 2 * 512], BF16)
    wpj = b.sb("wpj", [128, DT8 * DT8 * 128], BF16)

    psA = b.ps("psA", [128, 1024], F32)
    psB = b.ps("psB", [128, 1024], F32)
    psO = b.ps("psO", [128, 1024], F32)
    psP = b.ps("psP", [128, 1024], F32)

    cLd = b.ctr("cLd")
    cTab = b.ctr("cTab")
    cXs = [b.ctr("cXs0"), b.ctr("cXs1")]
    cWp = [b.ctr("cWp0"), b.ctr("cWp1")]
    cWpj = b.ctr("cWpj")
    cWg = [b.ctr("cWg0"), b.ctr("cWg1")]
    cOtr = [b.ctr("cOtr0"), b.ctr("cOtr1")]
    cX0t = [b.ctr("cX0t0"), b.ctr("cX0t1")]
    cOtd = [b.ctr("cOtd0"), b.ctr("cOtd1")]
    cX1d = b.ctr("cX1d")
    cOut = b.ctr("cOut")
    cPE = b.ctr("cPE")
    cACT = b.ctr("cACT")
    cDVE = b.ctr("cDVE")

    pe, dv, ac, dq = {}, {}, {}, {}

    block = b.get_block()

    @block.sync
    def _(sy):
        for ap_s, ap_d in ((bq_col, bq_col_d), (bp_col, bp_col_d),
                           (ones_bt, ones_b_d), (ident_f, ident_f_d), (bh2, bh2_d)):
            cLd.inc(sy.dma_start(ap_s[:], ap_d[:, :]), 16)
        for c in range(2):
            for j in range(DT8):
                cXs[c].inc(sy.dma_start(xs[:, c * T + j * 512: c * T + (j + 1) * 512],
                                        xT0_d[j * 128:(j + 1) * 128, c * 512:(c + 1) * 512]), 16)
            dq[("xs", c)] = cXs[c].v

    @block.gpsimd
    def _(g):
        cTab.inc(g.dma_start(cosP[:], cosP_d[:, :]), 16)
        cTab.inc(g.dma_start(sinN[:], sinN_d := sinP_d[:, :]), 16)
        for kk, col in ((0, 0), (1, D), (2, 2 * D)):
            for j in range(DT8):
                cTab.inc(g.dma_start(wpair[:, kk * DT8 * 128 + j * 128: kk * DT8 * 128 + (j + 1) * 128],
                                    wqkv_d[j * 128:(j + 1) * 128, col:col + 128]), 16)
    ld_all = cLd.v
    tab_all = cTab.v

    @block.vector
    def _(v):
        v.wait_ge(cLd.h, ld_all)
        v.wait_ge(cTab.h, tab_all)
        v.memset(onesf[:], 1.0)
        v.memset(fc[:, 0:1], 1e-5)
        v.memset(fc[:, 1:2], -0.5)
        v.memset(fc[:, 2:3], 0.125)
        v.memset(fc[:, 3:4], 0.5)
        vv = vst[:].rearrange("p (k c) -> p k c", c=130)
        v.memset(vv[:, :, 64:65], 1.0)
        inst = v.memset(vv[:, :, 129:130], 1.0)
        cDVE.inc(inst)
        v.wait_ge(cDVE.h, cDVE.v)
        for base, sgn in ((0, -1.0), (32, 1.0), (64, -1.0), (96, 1.0)):
            inst = v.tensor_scalar(sinN[base:base + 32, :], sinN[base:base + 32, :], sgn, None, ALU.mult)
        cDVE.inc(inst)
        dv["consts"] = cDVE.v

    def emit_ln(tag, x_dram, zdst, pre_wait=None):
        for c in range(TC8):
            t0 = c * 512
            rr = (c % 2) * T
            slot = cXs[c % 2]

            if not (tag == "ln0" and c < 2):
                @block.sync
                def _(sy, c=c, t0=t0, rr=rr, slot=slot):
                    if tag != "ln0" and c < 2 and pre_wait is not None:
                        sy.wait_ge(pre_wait[0].h, pre_wait[1])
                    if (tag, "z", c - 2) in dv:
                        sy.wait_ge(cDVE.h, dv[(tag, "z", c - 2)])
                    for j in range(DT8):
                        slot.inc(sy.dma_start(xs[:, rr + j * 512: rr + (j + 1) * 512],
                                              x_dram[j * 128:(j + 1) * 128, t0:t0 + 512]), 16)
                    dq[(tag, "xs", c)] = slot.v
            else:
                dq[(tag, "xs", c)] = dq[("xs", c)]

            for g9 in range(2):
                @block.vector
                def _(v, c=c, rr=rr, g9=g9, slot=slot):
                    if g9 == 0:
                        v.wait_ge(slot.h, dq[(tag, "xs", c)])
                    if c >= 1 or g9 >= 1:
                        pg = (c, g9 - 1) if g9 >= 1 else (c - 1, 1)
                        v.wait_ge(cPE.h, pe[(tag, "st") + pg])
                        v.wait_ge(cDVE.h, dv[(tag, "st") + pg])
                    for jj in range(4):
                        j = g9 * 4 + jj
                        v.tensor_copy(xb0[:, jj * 512:(jj + 1) * 512], xs[:, rr + j * 512: rr + (j + 1) * 512])
                    for jj in range(4):
                        inst = v.tensor_tensor(xsq[:, jj * 512:(jj + 1) * 512],
                                               xb0[:, jj * 512:(jj + 1) * 512],
                                               xb0[:, jj * 512:(jj + 1) * 512], ALU.mult)
                    cDVE.inc(inst)
                    dv[(tag, "st", c, g9)] = cDVE.v

                @block.tensor
                def _(t, c=c, g9=g9):
                    t.wait_ge(cDVE.h, dv[(tag, "st", c, g9)])
                    if g9 == 0:
                        if c >= 1:
                            t.wait_ge(cDVE.h, dv[(tag, "mv", c - 1)])
                        elif tag == "ln0":
                            t.wait_ge(cLd.h, ld_all)
                    for jj in range(4):
                        t.matmul(psA[:, 0:512], ones_bt[:], xb0[:, jj * 512:(jj + 1) * 512],
                                 start=(g9 == 0 and jj == 0), stop=(g9 == 1 and jj == 3))
                    for jj in range(4):
                        inst = t.matmul(psA[:, 512:1024], ones_bt[:], xsq[:, jj * 512:(jj + 1) * 512],
                                        start=(g9 == 0 and jj == 0), stop=(g9 == 1 and jj == 3))
                    cPE.inc(inst)
                    pe[(tag, "st", c, g9)] = cPE.v
            pe[(tag, "st", c)] = pe[(tag, "st", c, 1)]

            @block.vector
            def _(v, c=c):
                v.wait_ge(cPE.h, pe[(tag, "st", c)])
                if c >= 1:
                    v.wait_ge(cACT.h, ac[(tag, "r", c - 1)])
                    v.wait_ge(cDVE.h, dv[(tag, "z", c - 1)])
                v.tensor_scalar(Mt[:], psA[:, 0:512], 1.0 / D, None, ALU.mult)
                v.tensor_tensor(stat_sc[:], Mt[:], Mt[:], ALU.mult)
                inst = v.scalar_tensor_tensor(stat_sc[:], psA[:, 512:1024], 1.0 / D, stat_sc[:],
                                              ALU.mult, ALU.subtract)
                cDVE.inc(inst)
                dv[(tag, "mv", c)] = cDVE.v

            @block.scalar
            def _(s, c=c):
                s.wait_ge(cDVE.h, dv[(tag, "mv", c)])
                if c == 0:
                    s.wait_ge(cDVE.h, dv["consts"])
                s.activation(stat_sc[:], stat_sc[:], AF.Ln, bias=fc[:, 0:1])
                inst = s.activation(Rt[:], stat_sc[:], AF.Exp, scale=fc[:, 1:2])
                cACT.inc(inst)
                ac[(tag, "r", c)] = cACT.v

            @block.vector
            def _(v, c=c, t0=t0, rr=rr):
                v.wait_ge(cACT.h, ac[(tag, "r", c)])
                for j in range(DT8):
                    v.scalar_tensor_tensor(zsc[:], xs[:, rr + j * 512: rr + (j + 1) * 512], 1.0,
                                           Mt[:], ALU.mult, ALU.subtract)
                    inst = v.scalar_tensor_tensor(zdst[:, j * T + t0: j * T + t0 + 512], zsc[:], 1.0,
                                                  Rt[:], ALU.mult, ALU.mult)
                cDVE.inc(inst)
                dv[(tag, "z", c)] = cDVE.v
        dv[(tag, "z")] = dv[(tag, "z", TC8 - 1)]

    emit_ln("ln0", xT0_d, z0)

    if debug:
        @block.sync
        def _(sy):
            sy.wait_ge(cDVE.h, dv[("ln0", "z")])
            for j in range(DT8):
                cOut.inc(sy.dma_start(z0_dbg[j * 128:(j + 1) * 128, :], z0[:, j * T:(j + 1) * T]), 16)

    SCALE = 1.0 / math.sqrt(64.0)

    for hp in range(npairs):
        wrot = (hp % 2) * 24 * 128

        if hp + 1 < npairs:
            @block.gpsimd
            def _(g, hp=hp):
                if hp >= 1:
                    g.wait_ge(cPE.h, pe[("pj", hp - 1, "v", TC8 - 1)])
                nrot = ((hp + 1) % 2) * 24 * 128
                slot = cWp[(hp + 1) % 2]
                for kk, col in ((0, (hp + 1) * 128), (1, D + (hp + 1) * 128), (2, 2 * D + (hp + 1) * 128)):
                    for j in range(DT8):
                        slot.inc(g.dma_start(wpair[:, nrot + kk * DT8 * 128 + j * 128: nrot + kk * DT8 * 128 + (j + 1) * 128],
                                             wqkv_d[j * 128:(j + 1) * 128, col:col + 128]), 16)
                dq[("wp", hp + 1)] = slot.v

        for which, kk in (("q", 0), ("k", 1), ("v", 2)):
            bias_tile = {"q": hp, "k": 8 + hp, "v": 16 + hp}[which]
            for tc in range(TC8):
                psx = psA if (tc % 2 == 0) else psB

                @block.tensor
                def _(t, hp=hp, which=which, kk=kk, wrot=wrot, tc=tc, psx=psx):
                    if tc == 0:
                        if hp == 0 and which == "q":
                            t.wait_ge(cDVE.h, dv[("ln0", "z")])
                            t.wait_ge(cLd.h, ld_all)
                            t.wait_ge(cTab.h, tab_all)
                        elif which == "q":
                            if hp >= 1:
                                t.wait_ge(cWp[hp % 2].h, dq[("wp", hp)])
                            t.wait_ge(cDVE.h, dv[("norm", hp - 1, 1, 3)])
                            t.wait_ge(cDVE.h, dv[("rbev", hp - 1, 1, 3)])
                    if tc >= 2:
                        t.wait_ge(cDVE.h, dv[("pjev", hp, which, tc - 2)])
                    elif which == "k":
                        t.wait_ge(cDVE.h, dv[("pjev", hp, "q", TC8 - 2 + tc)])
                    elif which == "v":
                        t.wait_ge(cDVE.h, dv[("pjev", hp, "k", TC8 - 2 + tc)])
                    elif which == "q" and hp == 0:
                        t.wait_ge(cDVE.h, dv[("ln0", "mv", TC8 - 1)])
                    for j in range(DT8):
                        inst = t.matmul(psx[:, 0:512],
                                        wpair[:, wrot + kk * DT8 * 128 + j * 128: wrot + kk * DT8 * 128 + (j + 1) * 128],
                                        z0[:, j * T + tc * 512: j * T + tc * 512 + 512],
                                        start=(j == 0), stop=(j == DT8 - 1))
                    cPE.inc(inst)
                    pe[("pj", hp, which, tc)] = cPE.v

                if which in ("q", "k"):
                    @block.vector
                    def _(v, hp=hp, which=which, bias_tile=bias_tile, tc=tc, psx=psx):
                        dst = QT if which == "q" else KT
                        v.wait_ge(cPE.h, pe[("pj", hp, which, tc)])
                        inst = v.tensor_scalar(dst[:, tc * 512:(tc + 1) * 512], psx[:, 0:512],
                                               bq_col[:, bias_tile:bias_tile + 1], None, ALU.add)
                        cDVE.inc(inst)
                        dv[("pjev", hp, which, tc)] = cDVE.v
                else:
                    @block.vector
                    def _(v, hp=hp, bias_tile=bias_tile, tc=tc, psx=psx):
                        v.wait_ge(cPE.h, pe[("pj", hp, "v", tc)])
                        if tc >= 2:
                            v.wait_ge(cPE.h, pe[("vt", hp, tc - 2)])
                        inst = v.tensor_scalar(VTt[:, (tc % 2) * 512:(tc % 2) * 512 + 512], psx[:, 0:512],
                                               bq_col[:, bias_tile:bias_tile + 1], None, ALU.add)
                        cDVE.inc(inst)
                        dv[("pjev", hp, "v", tc)] = cDVE.v

                    @block.tensor
                    def _(t, hp=hp, tc=tc):
                        t.wait_ge(cDVE.h, dv[("pjev", hp, "v", tc)])
                        if tc >= 2:
                            t.wait_ge(cDVE.h, dv[("vtev", hp, tc - 2)])
                        elif hp > 0:
                            t.wait_ge(cDVE.h, dv[("vtev", hp - 1, TC8 - 2 + tc)])
                        for k4 in range(4):
                            inst = t.transpose(psO[:, (tc % 2) * 512 + k4 * 128:(tc % 2) * 512 + (k4 + 1) * 128],
                                               VTt[:, (tc % 2) * 512 + k4 * 128:(tc % 2) * 512 + (k4 + 1) * 128],
                                               ident_f[:])
                        cPE.inc(inst)
                        pe[("vt", hp, tc)] = cPE.v

                    @block.vector
                    def _(v, hp=hp, tc=tc):
                        v.wait_ge(cPE.h, pe[("vt", hp, tc)])
                        src = psO[:, (tc % 2) * 512:(tc % 2) * 512 + 512].rearrange(
                            "p (k two dv) -> p k two dv", two=2, dv=64)
                        dstv = vst[:, tc * 4 * 130:(tc + 1) * 4 * 130].rearrange("p (k c) -> p k c", c=130)
                        v.tensor_copy(dstv[:, :, 0:64], src[:, :, 0, :])
                        inst = v.tensor_copy(dstv[:, :, 65:129], src[:, :, 1, :])
                        cDVE.inc(inst)
                        dv[("vtev", hp, tc)] = cDVE.v

        @block.vector
        def _(v, hp=hp):
            if hp == 0:
                v.wait_ge(cDVE.h, dv["consts"])
            for src in (QT, KT):
                w = "q" if src is QT else "k"
                v.wait_ge(cDVE.h, dv[("pjev", hp, w, TC8 - 1)])
                for base in (0, 64):
                    v.tensor_copy(qsw[base:base + 32, :], src[base + 32:base + 64, PREFIX:])
                    v.tensor_copy(qsw[base + 32:base + 64, :], src[base:base + 32, PREFIX:])
                v.tensor_tensor(ropeA[:], src[:, PREFIX:], cosP[:], ALU.mult)
                v.tensor_tensor(qsw[:], qsw[:], sinN[:], ALU.mult)
                inst = v.tensor_tensor(src[:, PREFIX:], ropeA[:], qsw[:], ALU.add)
            cDVE.inc(inst)
            dv[("ropec", hp)] = cDVE.v

        for w8, src in (("q", QT), ("k", KT)):
            @block.scalar
            def _(sc, hp=hp, w8=w8, src=src):
                sc.wait_ge(cDVE.h, dv[("ropec", hp)])
                for half in range(2):
                    inst = sc.activation(sqb[:], src[:, half * 2048:(half + 1) * 2048], AF.Square)
                    cACT.inc(inst)
                    ac[("sq", hp, w8, half)] = cACT.v

            for half in range(2):
                for cc in range(4):
                    @block.tensor
                    def _(t, hp=hp, w8=w8, half=half, cc=cc):
                        if cc == 0:
                            t.wait_ge(cACT.h, ac[("sq", hp, w8, half)])
                        idx = (half * 4 + cc)
                        if idx >= 2 or w8 == "k" or hp > 0:
                            pidx = idx - 2
                            if pidx >= 0:
                                pk = ("nrmred", hp, w8, pidx // 4, pidx % 4)
                            elif w8 == "k":
                                pk = ("nrmred", hp, "q", (6 + idx) // 4, (6 + idx) % 4)
                            else:
                                pk = ("negC", hp - 1)
                            if pk in dv:
                                t.wait_ge(cDVE.h, dv[pk])
                        inst = t.matmul(psB[0:2, (cc % 2) * 512:(cc % 2) * 512 + 512],
                                        bh2[:], sqb[:, cc * 512:(cc + 1) * 512],
                                        start=True, stop=True)
                        cPE.inc(inst)
                        pe[("nrm", hp, w8, half, cc)] = cPE.v

                    @block.vector
                    def _(v, hp=hp, w8=w8, half=half, cc=cc):
                        base = 0 if w8 == "q" else 16
                        v.wait_ge(cPE.h, pe[("nrm", hp, w8, half, cc)])
                        inst = v.tensor_reduce(cmx[:, base + half * 4 + cc: base + half * 4 + cc + 1],
                                               psB[0:2, (cc % 2) * 512:(cc % 2) * 512 + 512],
                                               axis=AX.X, op=ALU.max)
                        cDVE.inc(inst)
                        dv[("nrmred", hp, w8, half, cc)] = cDVE.v
            dv[("nrmev", hp, w8, 1)] = dv[("nrmred", hp, w8, 1, 3)]

        @block.vector
        def _(v, hp=hp):
            v.wait_ge(cDVE.h, dv[("nrmev", hp, "k", 1)])
            v.tensor_reduce(csb[:, 0:1], cmx[:, 0:8], axis=AX.X, op=ALU.max)
            inst = v.tensor_reduce(csb[:, 1:2], cmx[:, 16:24], axis=AX.X, op=ALU.max)
            cDVE.inc(inst)
            v.wait_ge(cDVE.h, cDVE.v)
            inst = v.tensor_tensor(csb[:, 0:1], csb[:, 0:1], csb[:, 1:2], ALU.mult)
            cDVE.inc(inst)
            dv[("csb", hp)] = cDVE.v

        @block.scalar
        def _(sc, hp=hp):
            sc.wait_ge(cDVE.h, dv[("csb", hp)])
            sc.activation(csb[:, 0:1], csb[:, 0:1], AF.Ln)
            inst = sc.activation(csb[:, 0:1], csb[:, 0:1], AF.Exp, scale=fc[0:2, 3:4])
            cACT.inc(inst)
            ac[("csqrt", hp)] = cACT.v

        @block.vector
        def _(v, hp=hp):
            v.wait_ge(cACT.h, ac[("csqrt", hp)])
            inst = v.tensor_scalar(csb[:, 0:1], csb[:, 0:1], -SCALE * 1.05, None, ALU.mult)
            cDVE.inc(inst)
            dv[("csc", hp)] = cDVE.v

        @block.tensor
        def _(t, hp=hp):
            t.wait_ge(cDVE.h, dv[("csc", hp)])
            inst = t.transpose(psB[0:8, 0:2], csb[:, 0:8], ident_f[0:2, 0:2])
            cPE.inc(inst)
            pe[("ct", hp)] = cPE.v

        @block.vector
        def _(v, hp=hp):
            v.wait_ge(cPE.h, pe[("ct", hp)])
            inst = v.tensor_copy(csr[:], psB[0:1, 0:2])
            cDVE.inc(inst)
            dv[("csr", hp)] = cDVE.v

        @block.tensor
        def _(t, hp=hp):
            t.wait_ge(cDVE.h, dv[("csr", hp)])
            inst = t.matmul(psB[:, 0:2], onesf[0:1, :], csr[0:1, 0:2], start=True, stop=True)
            cPE.inc(inst)
            pe[("cbc", hp)] = cPE.v

        @block.vector
        def _(v, hp=hp):
            v.wait_ge(cPE.h, pe[("cbc", hp)])
            inst = v.tensor_copy(negC[:], psB[:, 0:2])
            cDVE.inc(inst)
            dv[("negC", hp)] = cDVE.v

        if debug and hp == 0:
            @block.sync
            def _(sy, hp=hp):
                sy.wait_ge(cDVE.h, dv[("negC", hp)])
                sy.wait_ge(cDVE.h, dv[("ropec", hp)])
                cOut.inc(sy.dma_start(qt_dbg[:, :], QT[:]), 16)
                cOut.inc(sy.dma_start(kt_dbg[:, :], KT[:]), 16)
                cOut.inc(sy.dma_start(nc_dbg[:, :], negC[:]), 16)
                cOut.inc(sy.dma_start(csb_dbg[:, :], csb[:]), 16)
                cOut.inc(sy.dma_start(csr_dbg[:, :], csr[:]), 16)
                cOut.inc(sy.dma_start(cmx_dbg[:, :], cmx[:]), 16)
                sy.wait_ge(cOut.h, cOut.v)

        if proj_only:
            @block.sync
            def _(sy):
                sy.wait_ge(cOut.h, cOut.v)
            break

        for h2 in range(2):
            hb = h2 * 64
            vofs = h2 * 65
            for qc in range(4):
                q0 = qc * 1024
                pso = psO if (qc % 2 == 0) else psP
                prev = (hp, h2, qc - 1) if qc > 0 else ((hp, 0, 3) if h2 > 0 else ((hp - 1, 1, 3) if hp > 0 else None))
                pk = (hp, h2, qc - 2) if qc >= 2 else ((hp, 0, qc + 2) if h2 == 1 else ((hp - 1, 1, qc + 2) if hp > 0 else None))

                for kt in range(32):
                    pss = psA if (kt % 2 == 0) else psB

                    @block.tensor
                    def _(t, hp=hp, h2=h2, hb=hb, qc=qc, q0=q0, pso=pso, vofs=vofs,
                          prev=prev, pk=pk, kt=kt, pss=pss):
                        if kt == 0:
                            if h2 == 0 and qc == 0:
                                t.wait_ge(cDVE.h, dv[("vtev", hp, TC8 - 1)])
                                t.wait_ge(cDVE.h, dv[("negC", hp)])
                            if prev is not None and (("rbev",) + prev) in dv:
                                t.wait_ge(cDVE.h, dv[("rbev",) + prev])
                            if pk is not None and (("norm",) + pk) in dv:
                                t.wait_ge(cDVE.h, dv[("norm",) + pk])
                        if kt >= 2:
                            t.wait_ge(cACT.h, ac[("exp", hp, h2, qc, kt - 2)])
                        for half in range(2):
                            inst = t.matmul(pss[:, half * 512:(half + 1) * 512],
                                            KT[hb:hb + 64, kt * 128:(kt + 1) * 128],
                                            QT[hb:hb + 64, q0 + half * 512: q0 + (half + 1) * 512],
                                            start=True, stop=True)
                        cPE.inc(inst)
                        pe[("s", hp, h2, qc, kt)] = cPE.v
                        if kt >= 1:
                            t.wait_ge(cACT.h, ac[("exp", hp, h2, qc, kt - 1)])
                            pr = probs[:, ((kt - 1) % 2) * 1024:((kt - 1) % 2) * 1024 + 1024]
                            for half in range(2):
                                inst = t.matmul(pso[0:65, half * 512:(half + 1) * 512],
                                                vst[:, (kt - 1) * 130 + vofs:(kt - 1) * 130 + vofs + 65],
                                                pr[:, half * 512:(half + 1) * 512],
                                                start=(kt - 1 == 0), stop=False)
                            cPE.inc(inst)
                            pe[("v", hp, h2, qc, kt - 1)] = cPE.v

                    @block.scalar
                    def _(s, hp=hp, h2=h2, qc=qc, kt=kt, pss=pss):
                        s.wait_ge(cPE.h, pe[("s", hp, h2, qc, kt)])
                        if kt >= 2:
                            s.wait_ge(cPE.h, pe[("v", hp, h2, qc, kt - 2)])
                        inst = s.activation(probs[:, (kt % 2) * 1024:(kt % 2) * 1024 + 1024],
                                            pss[:], AF.Exp, bias=negC[:, h2:h2 + 1], scale=fc[:, 2:3])
                        cACT.inc(inst)
                        ac[("exp", hp, h2, qc, kt)] = cACT.v

                @block.tensor
                def _(t, hp=hp, h2=h2, qc=qc, pso=pso, vofs=vofs):
                    t.wait_ge(cACT.h, ac[("exp", hp, h2, qc, 31)])
                    pr = probs[:, 1024:2048]
                    for half in range(2):
                        inst = t.matmul(pso[0:65, half * 512:(half + 1) * 512],
                                        vst[:, 31 * 130 + vofs:31 * 130 + vofs + 65],
                                        pr[:, half * 512:(half + 1) * 512],
                                        start=False, stop=True)
                    cPE.inc(inst)
                    pe[("v", hp, h2, qc, 31)] = cPE.v

                @block.vector
                def _(v, hp=hp, h2=h2, qc=qc, pso=pso):
                    v.wait_ge(cPE.h, pe[("v", hp, h2, qc, 31)])
                    inst = v.reciprocal(recipR[:], pso[64:65, 0:1024])
                    cDVE.inc(inst)
                    dv[("recip", hp, h2, qc)] = cDVE.v

                @block.tensor
                def _(t, hp=hp, h2=h2, qc=qc):
                    t.wait_ge(cDVE.h, dv[("recip", hp, h2, qc)])
                    t.wait_ge(cACT.h, ac[("exp", hp, h2, qc, 31)])
                    for half in range(2):
                        inst = t.matmul(psA[0:64, half * 512:(half + 1) * 512],
                                        onesf[0:1, 0:64],
                                        recipR[0:1, half * 512:(half + 1) * 512],
                                        start=True, stop=True)
                    cPE.inc(inst)
                    pe[("bc", hp, h2, qc)] = cPE.v

                @block.vector
                def _(v, hp=hp, h2=h2, qc=qc, pso=pso):
                    v.wait_ge(cPE.h, pe[("bc", hp, h2, qc)])
                    head = hp * 2 + h2
                    if ("oTdma", hp, h2, qc - 2) in dq:
                        v.wait_ge(cOtd[qc % 2].h, dq[("oTdma", hp, h2, qc - 2)])
                    elif hp > 0 and qc < 2:
                        v.wait_ge(cOtd[qc % 2].h, dq[("oTdma", hp - 1, h2, qc + 2)])
                    inst = v.tensor_copy(rB[:], psA[0:64, 0:1024])
                    cDVE.inc(inst)
                    dv[("rbev", hp, h2, qc)] = cDVE.v
                    ob = oTs[(head % 2) * 64:(head % 2) * 64 + 64, (qc % 2) * 1024:(qc % 2) * 1024 + 1024]
                    inst = v.tensor_tensor(ob, pso[0:64, 0:1024], rB[:], ALU.mult)
                    cDVE.inc(inst)
                    dv[("norm", hp, h2, qc)] = cDVE.v

                @block.sync
                def _(sy, hp=hp, h2=h2, qc=qc, q0=q0):
                    sy.wait_ge(cDVE.h, dv[("norm", hp, h2, qc)])
                    head = hp * 2 + h2
                    row0 = (head // 2) * 128 + (head % 2) * 64
                    cOtd[qc % 2].inc(sy.dma_start(
                        oT_d[row0:row0 + 64, q0:q0 + 1024],
                        oTs[(head % 2) * 64:(head % 2) * 64 + 64, (qc % 2) * 1024:(qc % 2) * 1024 + 1024]), 16)
                    dq[("oTdma", hp, h2, qc)] = cOtd[qc % 2].v

    if not do_tail:
        @block.sync
        def _(sy):
            sy.wait_ge(cOtd[0].h, cOtd[0].v)
            sy.wait_ge(cOtd[1].h, cOtd[1].v)
            sy.wait_ge(cOut.h, cOut.v)
        return (b.finish() if standalone else b)

    @block.gpsimd
    def _(g):
        g.wait_ge(cDVE.h, dv[("rbev", 7, 1, 3)])
        for et in range(DT8):
            for j in range(DT8):
                cWpj.inc(g.dma_start(wpj[:, et * D + j * 128: et * D + (j + 1) * 128],
                                     wproj_d[j * 128:(j + 1) * 128, et * 128:(et + 1) * 128]), 16)
    wpj_all = cWpj.v

    for tc in range(TC8):
        t0 = tc * 512
        stage = QT if tc % 2 == 0 else KT
        oslot = cOtr[tc % 2]

        @block.sync
        def _(sy, tc=tc, t0=t0, stage=stage, oslot=oslot):
            if tc == 0:
                sy.wait_ge(cDVE.h, dv[("rbev", 7, 1, 3)])
                sy.wait_ge(cOtd[0].h, cOtd[0].v)
                sy.wait_ge(cOtd[1].h, cOtd[1].v)
            if tc >= 2:
                sy.wait_ge(cPE.h, pe[("wp", tc - 2, DT8 - 1)])
            for j in range(DT8):
                oslot.inc(sy.dma_start(stage[:, j * 512:(j + 1) * 512],
                                       oT_d[j * 128:(j + 1) * 128, t0:t0 + 512]), 16)
            dq[("oTr", tc)] = oslot.v

        for et in range(DT8):
            psx = psA if ((tc * DT8 + et) % 2 == 0) else psB
            xslot = cX0t[et % 2]

            @block.sync
            def _(sy, tc=tc, t0=t0, et=et, xslot=xslot):
                if tc >= 1 or et >= 2:
                    prevk = (tc, et - 2) if et >= 2 else (tc - 1, DT8 - 2 + et)
                    sy.wait_ge(cDVE.h, dv[("x1",) + prevk])
                xslot.inc(sy.dma_start(x1s[:, (et % 2) * 512:(et % 2) * 512 + 512],
                                       xT0_d[et * 128:(et + 1) * 128, t0:t0 + 512]), 16)
                dq[("x0t", tc, et)] = xslot.v

            @block.tensor
            def _(t, tc=tc, et=et, psx=psx, stage=stage, oslot=oslot):
                if et == 0:
                    t.wait_ge(oslot.h, dq[("oTr", tc)])
                if tc == 0 and et == 0:
                    t.wait_ge(cWpj.h, wpj_all)
                key = ("x1", tc, et - 2) if et >= 2 else (("x1", tc - 1, DT8 - 2 + et) if tc >= 1 else None)
                if key is not None and key in dv:
                    t.wait_ge(cDVE.h, dv[key])
                for j in range(DT8):
                    inst = t.matmul(psx[:, 0:512], wpj[:, et * D + j * 128: et * D + (j + 1) * 128],
                                    stage[:, j * 512:(j + 1) * 512],
                                    start=(j == 0), stop=(j == DT8 - 1))
                cPE.inc(inst)
                pe[("wp", tc, et)] = cPE.v

            @block.vector
            def _(v, tc=tc, et=et, psx=psx, xslot=xslot):
                v.wait_ge(cPE.h, pe[("wp", tc, et)])
                v.wait_ge(xslot.h, dq[("x0t", tc, et)])
                inst = v.scalar_tensor_tensor(x1s[:, (et % 2) * 512:(et % 2) * 512 + 512],
                                              psx[:, 0:512], bp_col[:, et:et + 1],
                                              x1s[:, (et % 2) * 512:(et % 2) * 512 + 512],
                                              ALU.add, ALU.add)
                cDVE.inc(inst)
                dv[("x1", tc, et)] = cDVE.v

            @block.sync
            def _(sy, tc=tc, et=et, t0=t0):
                sy.wait_ge(cDVE.h, dv[("x1", tc, et)])
                cX1d.inc(sy.dma_start(x1T_d[et * 128:(et + 1) * 128, t0:t0 + 512],
                                      x1s[:, (et % 2) * 512:(et % 2) * 512 + 512]), 16)
                dq[("x1d", tc, et)] = cX1d.v

    emit_ln("ln1", x1T_d, z0, pre_wait=(cX1d, cX1d.v))

    @block.sync
    def _(sy):
        sy.wait_ge(cDVE.h, dv[("ln1", "z")])
        for j in range(DT8):
            cOut.inc(sy.dma_start(z1_d[j * 128:(j + 1) * 128, :], z0[:, j * T:(j + 1) * T]), 16)

    for gi, wsrc in ((0, wfT_d), (1, wiT_d), (2, wcT_d)):
        for mt in range(DT8):
            wofs = ((gi * DT8 + mt) % 2) * D
            gslot = cWg[(gi * DT8 + mt) % 2]

            @block.gpsimd
            def _(g, gi=gi, mt=mt, wsrc=wsrc, wofs=wofs, gslot=gslot):
                if gi == 0 and mt <= 1:
                    g.wait_ge(cPE.h, pe[("wp", TC8 - 1, DT8 - 1)])
                if (gi * DT8 + mt) >= 2:
                    pm = gi * DT8 + mt - 2
                    g.wait_ge(cPE.h, pe[("gp", pm // DT8, pm % DT8, TC8 - 1)])
                for j in range(DT8):
                    gslot.inc(g.dma_start(wpj[:, wofs + j * 128: wofs + (j + 1) * 128],
                                          wsrc[j * 128:(j + 1) * 128, mt * 128:(mt + 1) * 128]), 16)
                dq[("wg", gi, mt)] = gslot.v

            for tc in range(TC8):
                psx = psA if (tc % 2 == 0) else psB

                @block.tensor
                def _(t, gi=gi, mt=mt, wofs=wofs, tc=tc, psx=psx, gslot=gslot):
                    if tc == 0:
                        t.wait_ge(gslot.h, dq[("wg", gi, mt)])
                        if gi == 0 and mt == 0:
                            t.wait_ge(cDVE.h, dv[("ln1", "z")])
                    key = None
                    if tc >= 2:
                        key = ("gpev", gi, mt, tc - 2)
                    elif (gi, mt) != (0, 0):
                        pm = gi * DT8 + mt - 1
                        key = ("gpev", pm // DT8, pm % DT8, TC8 - 2 + tc)
                    else:
                        t.wait_ge(cDVE.h, dv[("ln1", "mv", TC8 - 1)])
                    if key is not None and key in dv:
                        t.wait_ge(cDVE.h, dv[key])
                    for j in range(DT8):
                        inst = t.matmul(psx[:, 0:512], wpj[:, wofs + j * 128: wofs + (j + 1) * 128],
                                        z0[:, j * T + tc * 512: j * T + tc * 512 + 512],
                                        start=(j == 0), stop=(j == DT8 - 1))
                    cPE.inc(inst)
                    pe[("gp", gi, mt, tc)] = cPE.v

                @block.vector
                def _(v, gi=gi, mt=mt, tc=tc, psx=psx):
                    v.wait_ge(cPE.h, pe[("gp", gi, mt, tc)])
                    gk = ("gpd", gi, mt, tc - 2) if tc >= 2 else None
                    if gk is None and (gi, mt) != (0, 0):
                        pm = gi * DT8 + mt - 1
                        gk = ("gpd", pm // DT8, pm % DT8, TC8 - 2 + tc)
                    if gk is not None and gk in dq:
                        v.wait_ge(cOut.h, dq[gk])
                    inst = v.tensor_copy(evb[:, (tc % 2) * 512:(tc % 2) * 512 + 512], psx[:, 0:512])
                    cDVE.inc(inst)
                    dv[("gpev", gi, mt, tc)] = cDVE.v

                @block.sync
                def _(sy, gi=gi, mt=mt, tc=tc):
                    sy.wait_ge(cDVE.h, dv[("gpev", gi, mt, tc)])
                    cOut.inc(sy.dma_start(
                        gates_d[mt * 128:(mt + 1) * 128, gi * T + tc * 512: gi * T + tc * 512 + 512],
                        evb[:, (tc % 2) * 512:(tc % 2) * 512 + 512]), 16)
                    dq[("gpd", gi, mt, tc)] = cOut.v

    @block.sync
    def _(sy):
        for c in (cX1d, cOut, cOtd[0], cOtd[1]):
            sy.wait_ge(c.h, c.v)

    return (b.finish() if standalone else b)



def prep_A(inputs):
    x = np.asarray(inputs["L0_attn_x"])
    cos = np.asarray(inputs["L0_attn_cos"])
    sin = np.asarray(inputs["L0_attn_sin"])
    cosP = f32c(np.concatenate([cos.T, cos.T], axis=0))
    sinP = f32c(np.concatenate([sin.T, sin.T], axis=0))
    shared = {
        "wqkv": f32c(inputs["L0_attn_w_qkv"]),
        "bq_col": col_layout(inputs["L0_attn_b_qkv"], 24),
        "wproj": f32c(inputs["L0_attn_w_proj"]),
        "bp_col": col_layout(inputs["L0_attn_b_proj"], 8),
        "cosP": cosP,
        "sinP": sinP,
        "wfT": f32c(np.asarray(inputs["L0_ffn_wf"]).T),
        "wiT": f32c(np.asarray(inputs["L0_ffn_wi"]).T),
        "wcT": f32c(np.asarray(inputs["L0_ffn_wc"]).T),
        "ones_b": bfc(np.ones((128, 128))),
        "bh2": bfc(np.concatenate([
            np.concatenate([np.ones((64, 1)), np.zeros((64, 1))], axis=0),
            np.concatenate([np.zeros((64, 1)), np.ones((64, 1))], axis=0)], axis=1)),
        "ident_f": f32c(np.eye(128)),
    }
    maps = []
    for t in range(NCORES):
        m = dict(shared)
        m["xT0"] = f32c(x[t].T)
        maps.append(m)
    return maps


def _intile_cols(weights):
    cd = np.zeros((128, 7 * 32), np.float32)
    co = np.zeros((128, 7 * 32), np.float32)
    coP = np.zeros((128, 7 * 32), np.float32)
    gi = np.arange(4096)
    for s in range(7):
        stride = 1 << s
        nb = 4096 // (2 * stride)
        w = np.asarray(weights[s]).reshape(nb, stride, 2, 2)
        nbi = gi // (2 * stride)
        half = (gi // stride) % 2
        si = gi % stride
        cdv = w[nbi, si, half, half]
        cov = w[nbi, si, half, 1 - half]
        coPv = cov[gi ^ stride]
        for t in range(32):
            cd[:, s * 32 + t] = cdv[t * 128:(t + 1) * 128]
            co[:, s * 32 + t] = cov[t * 128:(t + 1) * 128]
            coP[:, s * 32 + t] = coPv[t * 128:(t + 1) * 128]
    return cd, co, coP


def _cross_cols(weights):
    cx = np.zeros((128, 5 * 64), np.float32)
    gi = np.arange(4096)
    for s5 in range(5):
        stride = 1 << (7 + s5)
        nb = 4096 // (2 * stride)
        w = np.asarray(weights[7 + s5]).reshape(nb, stride, 2, 2)
        nbi = gi // (2 * stride)
        half = (gi // stride) % 2
        si = gi % stride
        selfv = w[nbi, si, half, half]
        partv = w[nbi, si, half, 1 - half]
        for t in range(32):
            cx[:, s5 * 64 + t * 2 + 0] = selfv[t * 128:(t + 1) * 128]
            cx[:, s5 * 64 + t * 2 + 1] = partv[t * 128:(t + 1) * 128]
    return cx


def _xs7_mats():
    out = np.zeros((128, 7 * 128), np.float32)
    i = np.arange(128)
    for s in range(7):
        out[i, s * 128 + (i ^ (1 << s))] = 1.0
    return out


def _sin_pe(length, dim):
    pos = np.arange(length, dtype=np.float32)[:, None]
    inv_freq = np.exp(-np.log(np.float32(10000.0)) * (np.arange(0, dim, 2, dtype=np.float32) / dim))
    ang = pos * inv_freq[None, :]
    pe = np.zeros((length, dim), dtype=np.float32)
    pe[:, 0::2] = np.sin(ang)
    pe[:, 1::2] = np.cos(ang)
    return pe


def kernel(**inputs):
    import ml_dtypes
    from concourse.bass_utils import run_bass_kernel_spmd
    cores = list(range(NCORES))
    ones_b = bfc(np.ones((128, 128)))
    ident_f = f32c(np.eye(128))
    identb = bfc(np.eye(128))
    cmask = np.ascontiguousarray(
        (np.arange(1920)[None, :] - 896 >= np.arange(128)[:, None]).astype(ml_dtypes.bfloat16))

    b = build_launch_A()
    res = run_bass_kernel_spmd(b.nc, prep_A(inputs), core_ids=cores)
    x1T = [np.asarray(res.results[t]["x1T"]) for t in range(8)]
    z1 = [np.asarray(res.results[t]["z1"]) for t in range(8)]
    gates = [np.asarray(res.results[t]["gatesT"]) for t in range(8)]
    del res, b

    b = build_launch_B()
    res = run_bass_kernel_spmd(b.nc, prep_B(inputs, gates), core_ids=cores)
    uterm = np.ascontiguousarray(np.concatenate(
        [np.asarray(res.results[j]["uterm"]) for j in range(8)], axis=1))
    del res, b, gates

    b = build_launch_Cmerged()
    maps = prep_Cmerged(inputs, x1T, z1, uterm)
    res = run_bass_kernel_spmd(b.nc, maps, core_ids=cores)
    out = np.stack([np.ascontiguousarray(np.asarray(res.results[t]["c7_x8T"], np.float32).T)
                    for t in range(8)], axis=0)
    return out


if __name__ == "__main__":
    build_launch_A()
    print("launch A built ok")



def build_launch_B(race_checks=False):
    standalone = True
    b = Builder("B", race_checks=race_checks)
    nc = b.nc
    xfT_d = b.din("xfT", [D, 8 * 512], BF16)
    xiT_d = b.din("xiT", [D, 8 * 512], BF16)
    xcT_d = b.din("xcT", [D, 8 * 512], BF16)
    ufT_d = b.din("ufT", [D, D], F32)
    uiT_d = b.din("uiT", [D, D], F32)
    ucT_d = b.din("ucT", [D, D], F32)
    uoT_d = b.din("uoT", [D, D], F32)
    bf_col_d = b.din("bf_col", [128, 8], F32)
    bi_col_d = b.din("bi_col", [128, 8], F32)
    bc_col_d = b.din("bc_col", [128, 8], F32)
    inc_col_d = b.din("inc_col", [128, 8], F32)
    uterm_d = b.dout("uterm", [D, 512], BF16)

    uf = b.sb("uf", [128, DT8 * D], BF16)
    ui = b.sb("ui", [128, DT8 * D], BF16)
    uc = b.sb("uc", [128, DT8 * D], BF16)
    uo = b.sb("uo", [128, DT8 * D], BF16)
    xg = b.sb("xg", [128, 2 * 3 * DT8 * 512], BF16)
    cT = b.sb("cT", [128, DT8 * 512], F32)
    cB = b.sb("cB", [128, DT8 * 512], BF16)
    gf = b.sb("gf", [128, DT8 * 512], F32)
    gi = b.sb("gi", [128, DT8 * 512], F32)
    gc = b.sb("gc", [128, DT8 * 512], F32)
    tt1 = b.sb("tt1", [128, 2 * 512], F32)
    bfc_ = b.sb("bfc", [128, 8], F32)
    bic = b.sb("bic", [128, 8], F32)
    bcc = b.sb("bcc", [128, 8], F32)
    incc = b.sb("incc", [128, 8], F32)
    evB = b.sb("evB", [128, 2 * 512], BF16)

    psA = b.ps("psA", [128, 1024], F32)
    psB = b.ps("psB", [128, 1024], F32)

    cW = b.ctr("cW")
    cXg = [b.ctr("cXg0"), b.ctr("cXg1")]
    cB_ = b.ctr("cBc")
    cUo = b.ctr("cUo")
    cPE = b.ctr("cPE")
    cACT = b.ctr("cACT")
    cDVE = b.ctr("cDVE")
    pe, dv, ac, dq = {}, {}, {}, {}

    block = b.get_block()

    @block.gpsimd
    def _(g):
        for buf, src in ((uf, ufT_d), (ui, uiT_d), (uc, ucT_d), (uo, uoT_d)):
            for j in range(DT8):
                cW.inc(g.dma_start(buf[:, j * D:(j + 1) * D], src[j * 128:(j + 1) * 128, :]), 16)
    w_all = cW.v

    @block.sync
    def _(sy):
        for buf, src in ((bfc_, bf_col_d), (bic, bi_col_d), (bcc, bc_col_d), (incc, inc_col_d)):
            cB_.inc(sy.dma_start(buf[:], src[:, :]), 16)
        for st in range(2):
            for gk, src in ((0, xfT_d), (1, xiT_d), (2, xcT_d)):
                for j in range(DT8):
                    cXg[st].inc(sy.dma_start(
                        xg[:, st * 3 * DT8 * 512 + gk * DT8 * 512 + j * 512:
                           st * 3 * DT8 * 512 + gk * DT8 * 512 + (j + 1) * 512],
                        src[j * 128:(j + 1) * 128, st * 512:(st + 1) * 512]), 16)
            dq[("xg", st)] = cXg[st].v
    b_all = cB_.v

    @block.vector
    def _(v):
        v.wait_ge(cB_.h, b_all)
        for j in range(DT8):
            inst = v.memset(cT[:, j * 512:(j + 1) * 512], 0.0)
        cDVE.inc(inst)
        v.wait_ge(cDVE.h, cDVE.v)
        for j in range(DT8):
            inst = v.tensor_scalar(cT[:, j * 512:(j + 1) * 512], cT[:, j * 512:(j + 1) * 512],
                                   incc[:, j:j + 1], None, ALU.add)
        cDVE.inc(inst)
        v.wait_ge(cDVE.h, cDVE.v)
        for j in range(DT8):
            inst = v.tensor_copy(cB[:, j * 512:(j + 1) * 512], cT[:, j * 512:(j + 1) * 512])
        cDVE.inc(inst)
        dv[("cB", -1)] = cDVE.v

    GATES = ((0, uf, bfc_, gf, AF.Sigmoid), (1, ui, bic, gi, AF.Sigmoid), (2, uc, bcc, gc, AF.Tanh))

    for st in range(8):
        slot = st % 2
        sofs = slot * 3 * DT8 * 512
        for gk, ubuf, bcol, gdst, afun in GATES:
            for mt in range(DT8):
                psx = psA if ((gk * DT8 + mt) % 2 == 0) else psB

                @block.tensor
                def _(t, st=st, gk=gk, mt=mt, ubuf=ubuf, psx=psx, slot=slot):
                    if st == 0 and gk == 0 and mt == 0:
                        t.wait_ge(cW.h, w_all)
                        t.wait_ge(cDVE.h, dv[("cB", -1)])
                    elif gk == 0 and mt == 0:
                        t.wait_ge(cDVE.h, dv[("cB", st - 1)])
                    idx = gk * DT8 + mt
                    pk = None
                    if idx >= 2:
                        pk = ("gate", st, (idx - 2) // DT8, (idx - 2) % DT8)
                    elif st > 0:
                        pidx = 3 * DT8 + idx - 2
                        pk = ("gate", st - 1, pidx // DT8, pidx % DT8) if pidx < 3 * DT8 else None
                        if pk is None and idx == 0:
                            pk = ("uterm_mm",) if False else None
                    if pk is not None and pk in ac:
                        t.wait_ge(cACT.h, ac[pk])
                    for j in range(DT8):
                        inst = t.matmul(psx[:, 0:512], ubuf[:, j * D + mt * 128: j * D + mt * 128 + 128],
                                        cB[:, j * 512:(j + 1) * 512],
                                        start=(j == 0), stop=(j == DT8 - 1))
                    cPE.inc(inst)
                    pe[("gate", st, gk, mt)] = cPE.v

                @block.vector
                def _(v, st=st, gk=gk, mt=mt, psx=psx, slot=slot, sofs=sofs,
                      bcol={0: bfc_, 1: bic, 2: bcc}[gk]):
                    v.wait_ge(cPE.h, pe[("gate", st, gk, mt)])
                    if gk == 0 and mt == 0:
                        v.wait_ge(cXg[slot].h, dq[("xg", st)])
                    idx = gk * DT8 + mt
                    pk2 = None
                    if idx >= 2:
                        pk2 = ("gate", st, (idx - 2) // DT8, (idx - 2) % DT8)
                    elif st > 0:
                        pidx = 3 * DT8 + idx - 2
                        pk2 = ("gate", st - 1, pidx // DT8, pidx % DT8)
                    if pk2 is not None and pk2 in ac:
                        v.wait_ge(cACT.h, ac[pk2])
                    trot = (idx % 2) * 512
                    inst = v.scalar_tensor_tensor(
                        tt1[:, trot:trot + 512], psx[:, 0:512], bcol[:, mt:mt + 1],
                        xg[:, sofs + gk * DT8 * 512 + mt * 512: sofs + gk * DT8 * 512 + (mt + 1) * 512],
                        ALU.add, ALU.add)
                    cDVE.inc(inst)
                    dv[("gpre", st, gk, mt)] = cDVE.v

                @block.scalar
                def _(s, st=st, gk=gk, mt=mt, gdst=gdst, afun=afun):
                    s.wait_ge(cDVE.h, dv[("gpre", st, gk, mt)])
                    trot = ((gk * DT8 + mt) % 2) * 512
                    inst = s.activation(gdst[:, mt * 512:(mt + 1) * 512], tt1[:, trot:trot + 512], afun)
                    cACT.inc(inst)
                    ac[("gate", st, gk, mt)] = cACT.v

        if st + 2 < 8:
            @block.sync
            def _(sy, st=st, slot=slot, sofs=sofs):
                sy.wait_ge(cDVE.h, dv[("gpre", st, 2, DT8 - 1)])
                nst = st + 2
                for gk, src in ((0, xfT_d), (1, xiT_d), (2, xcT_d)):
                    for j in range(DT8):
                        cXg[slot].inc(sy.dma_start(
                            xg[:, sofs + gk * DT8 * 512 + j * 512: sofs + gk * DT8 * 512 + (j + 1) * 512],
                            src[j * 128:(j + 1) * 128, nst * 512:(nst + 1) * 512]), 16)
                dq[("xg", nst)] = cXg[slot].v

        @block.vector
        def _(v, st=st):
            v.wait_ge(cACT.h, ac[("gate", st, 2, DT8 - 1)])
            for mt in range(DT8):
                sl = slice(mt * 512, (mt + 1) * 512)
                v.tensor_tensor(cT[:, sl], gf[:, sl], cT[:, sl], ALU.mult)
                v.tensor_tensor(tt1[:, (mt % 2) * 512:(mt % 2) * 512 + 512], gi[:, sl], gc[:, sl], ALU.mult)
                v.tensor_tensor(cT[:, sl], cT[:, sl], tt1[:, (mt % 2) * 512:(mt % 2) * 512 + 512], ALU.add)
                inst = v.tensor_copy(cB[:, sl], cT[:, sl])
            cDVE.inc(inst)
            dv[("cB", st)] = cDVE.v

    for mt in range(DT8):
        psx = psA if (mt % 2 == 0) else psB

        @block.tensor
        def _(t, mt=mt, psx=psx):
            if mt == 0:
                t.wait_ge(cDVE.h, dv[("cB", 7)])
            if mt >= 2:
                t.wait_ge(cDVE.h, dv[("utev", mt - 2)])
            if mt < 2:
                t.wait_ge(cACT.h, ac[("gate", 7, 2, DT8 - 2 + mt)])
            for j in range(DT8):
                inst = t.matmul(psx[:, 0:512], uo[:, j * D + mt * 128: j * D + mt * 128 + 128],
                                cB[:, j * 512:(j + 1) * 512],
                                start=(j == 0), stop=(j == DT8 - 1))
            cPE.inc(inst)
            pe[("ut", mt)] = cPE.v

        @block.vector
        def _(v, mt=mt, psx=psx):
            v.wait_ge(cPE.h, pe[("ut", mt)])
            if ("utd", mt - 2) in dq:
                v.wait_ge(cUo.h, dq[("utd", mt - 2)])
            inst = v.tensor_copy(evB[:, (mt % 2) * 512:(mt % 2) * 512 + 512], psx[:, 0:512])
            cDVE.inc(inst)
            dv[("utev", mt)] = cDVE.v

        @block.sync
        def _(sy, mt=mt):
            sy.wait_ge(cDVE.h, dv[("utev", mt)])
            cUo.inc(sy.dma_start(uterm_d[mt * 128:(mt + 1) * 128, :],
                                 evB[:, (mt % 2) * 512:(mt % 2) * 512 + 512]), 16)
            dq[("utd", mt)] = cUo.v

    @block.sync
    def _(sy):
        sy.wait_ge(cUo.h, cUo.v)
    return (b.finish() if standalone else b)


def prep_B(inputs, gates_by_core):
    shared = {
        "ufT": f32c(np.asarray(inputs["L0_ffn_uf"]).T),
        "uiT": f32c(np.asarray(inputs["L0_ffn_ui"]).T),
        "ucT": f32c(np.asarray(inputs["L0_ffn_uc"]).T),
        "uoT": f32c(np.asarray(inputs["L0_ffn_uo"]).T),
        "bf_col": col_layout(inputs["L0_ffn_bf"], 8),
        "bi_col": col_layout(inputs["L0_ffn_bi"], 8),
        "bc_col": col_layout(inputs["L0_ffn_bc"], 8),
        "inc_col": col_layout(inputs["L0_ffn_in_c"], 8),
    }
    maps = []
    for j in range(NCORES):
        m = dict(shared)
        r0 = j * 512
        for name, gk in (("xfT", 0), ("xiT", 1), ("xcT", 2)):
            m[name] = np.ascontiguousarray(np.concatenate(
                [gates_by_core[t][:, gk * T + r0: gk * T + r0 + 512] for t in range(8)], axis=1))
        maps.append(m)
    return maps




class LnEnv:
    def __init__(self, block, xs, xb0, xsq, Mt, Rt, stat_sc, zsc, evz, ones_bt,
                 fc, psA, cXs, cOut, cPE, cACT, cDVE, pe, dv, ac, dq):
        self.__dict__.update(locals())


def emit_ln_generic(E, tag, x_dram, z_out_dram, pre_ctr, pre_val, z_sbuf=None):
    block = E.block
    pe, dv, ac, dq = E.pe, E.dv, E.ac, E.dq
    cPE, cACT, cDVE, cOut = E.cPE, E.cACT, E.cDVE, E.cOut
    for c in range(TC8):
        t0 = c * 512
        rr = (c % 2) * T
        slot = E.cXs[c % 2]

        @block.sync
        def _(sy, c=c, t0=t0, rr=rr, slot=slot):
            if c < 2 and pre_ctr is not None:
                sy.wait_ge(pre_ctr.h, pre_val)
            if (tag, "z", c - 2) in dv:
                sy.wait_ge(cDVE.h, dv[(tag, "z", c - 2)])
            for j in range(DT8):
                slot.inc(sy.dma_start(E.xs[:, rr + j * 512: rr + (j + 1) * 512],
                                      x_dram[j * 128:(j + 1) * 128, t0:t0 + 512]), 16)
            dq[(tag, "xs", c)] = slot.v

        for g9 in range(2):
            @block.vector
            def _(v, c=c, rr=rr, g9=g9, slot=slot):
                if g9 == 0:
                    v.wait_ge(slot.h, dq[(tag, "xs", c)])
                    if c == 0 and "consts" in dv:
                        v.wait_ge(cDVE.h, dv["consts"])
                if c >= 1 or g9 >= 1:
                    pg = (c, g9 - 1) if g9 >= 1 else (c - 1, 1)
                    v.wait_ge(cPE.h, pe[(tag, "st") + pg])
                    v.wait_ge(cDVE.h, dv[(tag, "st") + pg])
                for jj in range(4):
                    j = g9 * 4 + jj
                    v.tensor_copy(E.xb0[:, jj * 512:(jj + 1) * 512], E.xs[:, rr + j * 512: rr + (j + 1) * 512])
                for jj in range(4):
                    inst = v.tensor_tensor(E.xsq[:, jj * 512:(jj + 1) * 512],
                                           E.xb0[:, jj * 512:(jj + 1) * 512],
                                           E.xb0[:, jj * 512:(jj + 1) * 512], ALU.mult)
                cDVE.inc(inst)
                dv[(tag, "st", c, g9)] = cDVE.v

            @block.tensor
            def _(t, c=c, g9=g9):
                t.wait_ge(cDVE.h, dv[(tag, "st", c, g9)])
                if g9 == 0 and c >= 1:
                    t.wait_ge(cDVE.h, dv[(tag, "mv", c - 1)])
                if g9 == 0 and (tag, "psA_free") in pe:
                    t.wait_ge(cDVE.h, pe[(tag, "psA_free")]) if False else None
                for jj in range(4):
                    t.matmul(E.psA[:, 0:512], E.ones_bt[:], E.xb0[:, jj * 512:(jj + 1) * 512],
                             start=(g9 == 0 and jj == 0), stop=(g9 == 1 and jj == 3))
                for jj in range(4):
                    inst = t.matmul(E.psA[:, 512:1024], E.ones_bt[:], E.xsq[:, jj * 512:(jj + 1) * 512],
                                    start=(g9 == 0 and jj == 0), stop=(g9 == 1 and jj == 3))
                cPE.inc(inst)
                pe[(tag, "st", c, g9)] = cPE.v
        pe[(tag, "st", c)] = pe[(tag, "st", c, 1)]
        dv[(tag, "st", c)] = dv[(tag, "st", c, 1)]

        @block.vector
        def _(v, c=c):
            v.wait_ge(cPE.h, pe[(tag, "st", c)])
            if c >= 1:
                v.wait_ge(cACT.h, ac[(tag, "r", c - 1)])
                v.wait_ge(cDVE.h, dv[(tag, "z", c - 1)])
            v.tensor_scalar(E.Mt[:], E.psA[:, 0:512], 1.0 / D, None, ALU.mult)
            v.tensor_tensor(E.stat_sc[:], E.Mt[:], E.Mt[:], ALU.mult)
            inst = v.scalar_tensor_tensor(E.stat_sc[:], E.psA[:, 512:1024], 1.0 / D, E.stat_sc[:],
                                          ALU.mult, ALU.subtract)
            cDVE.inc(inst)
            dv[(tag, "mv", c)] = cDVE.v

        @block.scalar
        def _(s, c=c):
            s.wait_ge(cDVE.h, dv[(tag, "mv", c)])
            s.activation(E.stat_sc[:], E.stat_sc[:], AF.Ln, bias=E.fc[:, 0:1])
            inst = s.activation(E.Rt[:], E.stat_sc[:], AF.Exp, scale=E.fc[:, 1:2])
            cACT.inc(inst)
            ac[(tag, "r", c)] = cACT.v

        if z_sbuf is not None:
            @block.vector
            def _(v, c=c, t0=t0, rr=rr):
                v.wait_ge(cACT.h, ac[(tag, "r", c)])
                for j in range(DT8):
                    v.scalar_tensor_tensor(E.zsc[:], E.xs[:, rr + j * 512: rr + (j + 1) * 512], 1.0,
                                           E.Mt[:], ALU.mult, ALU.subtract)
                    inst = v.scalar_tensor_tensor(z_sbuf[:, j * T + t0: j * T + t0 + 512], E.zsc[:], 1.0,
                                                  E.Rt[:], ALU.mult, ALU.mult)
                cDVE.inc(inst)
                dv[(tag, "z", c)] = cDVE.v
        else:
            for j in range(DT8):
                @block.vector
                def _(v, c=c, rr=rr, j=j):
                    if j == 0:
                        v.wait_ge(cACT.h, ac[(tag, "r", c)])
                    pk = ("zdt", tag, c, j - 2) if j >= 2 else (("zdt", tag, c - 1, DT8 - 2 + j) if c >= 1 else None)
                    if pk is not None and pk in dq:
                        v.wait_ge(cOut.h, dq[pk])
                    v.scalar_tensor_tensor(E.zsc[:], E.xs[:, rr + j * 512: rr + (j + 1) * 512], 1.0,
                                           E.Mt[:], ALU.mult, ALU.subtract)
                    inst = v.scalar_tensor_tensor(E.evz[:, (j % 2) * 512:(j % 2) * 512 + 512], E.zsc[:], 1.0,
                                                  E.Rt[:], ALU.mult, ALU.mult)
                    cDVE.inc(inst)
                    dv[(tag, "zt", c, j)] = cDVE.v

                @block.sync
                def _(sy, c=c, t0=t0, j=j):
                    sy.wait_ge(cDVE.h, dv[(tag, "zt", c, j)])
                    cOut.inc(sy.dma_start(z_out_dram[j * 128:(j + 1) * 128, t0:t0 + 512],
                                          E.evz[:, (j % 2) * 512:(j % 2) * 512 + 512]), 16)
                    dq[("zdt", tag, c, j)] = cOut.v
            dv[(tag, "z", c)] = dv[(tag, "zt", c, DT8 - 1)]
    dv[(tag, "z")] = dv[(tag, "z", TC8 - 1)]



def build_launch_C1(b=None):
    standalone = b is None
    if standalone:
        b = Builder("C1")
    nc = b.nc
    x1T_d = b.din("x1T", [D, T], F32)
    z1_d = b.din("z1", [D, T], BF16)
    ut_d = b.din("uterm", [D, T], BF16)
    woT_d = b.din("woT", [D, D], F32)
    bo_col_d = b.din("bo_col", [128, 8], F32)
    ones_b_d = b.din("ones_b", [128, 128], BF16)
    x2T_d = b.dout("x2T", [D, T], F32)
    z2_d = b.dout("z2", [D, T], BF16)

    z1 = b.sb("z1_s", [128, DT8 * T], BF16)
    wo = b.sb("wo_s", [128, DT8 * D], BF16)
    uts = b.sb("uts", [128, 2 * 512], BF16)
    x1s = b.sb("x1s", [128, 2 * 512], F32)
    hs = b.sb("hs", [128, 2 * 512], F32)
    x2s = b.sb("x2s", [128, 2 * 512], F32)
    bo_col = b.sb("bo_col_s", [128, 8], F32)
    ones_bt = b.sb("ones_bt", [128, 128], BF16)
    fc = b.sb("fc", [128, 4], F32)
    xs = b.sb("xs", [128, 2 * T], F32)
    xb0 = b.sb("xb0", [128, T // 2], BF16)
    xsq = b.sb("xsq", [128, T // 2], BF16)
    Mt = b.sb("Mt", [128, 512], F32)
    Rt = b.sb("Rt", [128, 512], F32)
    stat_sc = b.sb("stat_sc", [128, 512], F32)
    zsc = b.sb("zsc", [128, 512], F32)
    evz = b.sb("evz", [128, 2 * 512], BF16)

    psA = b.ps("psA", [128, 1024], F32)
    psB = b.ps("psB", [128, 1024], F32)

    cLd = b.ctr("cLd")
    cW = b.ctr("cW")
    cUt = [b.ctr("cUt0"), b.ctr("cUt1")]
    cX1 = [b.ctr("cX10"), b.ctr("cX11")]
    cXs = [b.ctr("cXs0"), b.ctr("cXs1")]
    cX2d = b.ctr("cX2d")
    cOut = b.ctr("cOut")
    cPE = b.ctr("cPE")
    cACT = b.ctr("cACT")
    cDVE = b.ctr("cDVE")
    pe, dv, ac, dq = {}, {}, {}, {}

    block = b.get_block()

    @block.sync
    def _(sy):
        cLd.inc(sy.dma_start(bo_col[:], bo_col_d[:, :]), 16)
        cLd.inc(sy.dma_start(ones_bt[:], ones_b_d[:, :]), 16)
    ld_all = cLd.v

    @block.gpsimd
    def _(g):
        for j in range(DT8):
            cW.inc(g.dma_start(z1[:, j * T:(j + 1) * T], z1_d[j * 128:(j + 1) * 128, :]), 16)
        for j in range(DT8):
            cW.inc(g.dma_start(wo[:, j * D:(j + 1) * D], woT_d[j * 128:(j + 1) * 128, :]), 16)
    w_all = cW.v

    @block.vector
    def _(v):
        v.wait_ge(cLd.h, ld_all)
        v.memset(fc[:, 0:1], 1e-5)
        inst = v.memset(fc[:, 1:2], -0.5)
        cDVE.inc(inst)
        v.wait_ge(cDVE.h, cDVE.v)
        inst = v.tensor_copy(ones_bt[:], ones_bt[:])
        cDVE.inc(inst)
        dv["consts"] = cDVE.v

    for tc in range(TC8):
        t0 = tc * 512
        for ot in range(DT8):
            idx = tc * DT8 + ot
            psx = psA if (idx % 2 == 0) else psB
            uslot = cUt[ot % 2]
            xslot = cX1[ot % 2]

            @block.sync
            def _(sy, tc=tc, t0=t0, ot=ot, uslot=uslot, xslot=xslot, idx=idx):
                if idx >= 2:
                    sy.wait_ge(cDVE.h, dv[("x2", idx - 2)])
                uslot.inc(sy.dma_start(uts[:, (ot % 2) * 512:(ot % 2) * 512 + 512],
                                       ut_d[ot * 128:(ot + 1) * 128, t0:t0 + 512]), 16)
                dq[("ut", idx)] = uslot.v
                xslot.inc(sy.dma_start(x1s[:, (ot % 2) * 512:(ot % 2) * 512 + 512],
                                       x1T_d[ot * 128:(ot + 1) * 128, t0:t0 + 512]), 16)
                dq[("x1", idx)] = xslot.v

            @block.tensor
            def _(t, tc=tc, ot=ot, psx=psx, idx=idx):
                if idx == 0:
                    t.wait_ge(cW.h, w_all)
                if idx >= 2:
                    t.wait_ge(cDVE.h, dv[("hpre", idx - 2)])
                for j in range(DT8):
                    inst = t.matmul(psx[:, 0:512], wo[:, j * D + ot * 128: j * D + ot * 128 + 128],
                                    z1[:, j * T + tc * 512: j * T + tc * 512 + 512],
                                    start=(j == 0), stop=(j == DT8 - 1))
                cPE.inc(inst)
                pe[("wo", idx)] = cPE.v

            @block.vector
            def _(v, ot=ot, psx=psx, idx=idx, uslot=uslot):
                v.wait_ge(cPE.h, pe[("wo", idx)])
                v.wait_ge(uslot.h, dq[("ut", idx)])
                if idx >= 2:
                    v.wait_ge(cACT.h, ac[("h", idx - 2)])
                inst = v.scalar_tensor_tensor(hs[:, (idx % 2) * 512:(idx % 2) * 512 + 512],
                                              psx[:, 0:512], bo_col[:, ot:ot + 1],
                                              uts[:, (ot % 2) * 512:(ot % 2) * 512 + 512],
                                              ALU.add, ALU.add)
                cDVE.inc(inst)
                dv[("hpre", idx)] = cDVE.v

            @block.scalar
            def _(s, idx=idx):
                s.wait_ge(cDVE.h, dv[("hpre", idx)])
                if idx >= 2:
                    s.wait_ge(cDVE.h, dv[("x2", idx - 2)])
                inst = s.activation(hs[:, (idx % 2) * 512:(idx % 2) * 512 + 512],
                                    hs[:, (idx % 2) * 512:(idx % 2) * 512 + 512], AF.Tanh)
                cACT.inc(inst)
                ac[("h", idx)] = cACT.v

            @block.vector
            def _(v, idx=idx, xslot=xslot):
                v.wait_ge(cACT.h, ac[("h", idx)])
                v.wait_ge(xslot.h, dq[("x1", idx)])
                if idx >= 2:
                    v.wait_ge(cX2d.h, dq[("x2d", idx - 2)])
                inst = v.tensor_tensor(x2s[:, (idx % 2) * 512:(idx % 2) * 512 + 512],
                                       x1s[:, (idx % 2) * 512:(idx % 2) * 512 + 512],
                                       hs[:, (idx % 2) * 512:(idx % 2) * 512 + 512], ALU.add)
                cDVE.inc(inst)
                dv[("x2", idx)] = cDVE.v

            @block.sync
            def _(sy, tc=tc, t0=t0, ot=ot, idx=idx):
                sy.wait_ge(cDVE.h, dv[("x2", idx)])
                cX2d.inc(sy.dma_start(x2T_d[ot * 128:(ot + 1) * 128, t0:t0 + 512],
                                      x2s[:, (idx % 2) * 512:(idx % 2) * 512 + 512]), 16)
                dq[("x2d", idx)] = cX2d.v

    def emit_ln_c(tag, x_dram, z_out_dram, pre_ctr, pre_val):
        for c in range(TC8):
            t0 = c * 512
            rr = (c % 2) * T
            slot = cXs[c % 2]

            @block.sync
            def _(sy, c=c, t0=t0, rr=rr, slot=slot):
                if c < 2:
                    sy.wait_ge(pre_ctr.h, pre_val)
                if (tag, "zd", c - 2) in dq:
                    sy.wait_ge(cOut.h, dq[(tag, "zd", c - 2)])
                for j in range(DT8):
                    slot.inc(sy.dma_start(xs[:, rr + j * 512: rr + (j + 1) * 512],
                                          x_dram[j * 128:(j + 1) * 128, t0:t0 + 512]), 16)
                dq[(tag, "xs", c)] = slot.v

            for g9 in range(2):
                @block.vector
                def _(v, c=c, rr=rr, g9=g9, slot=slot):
                    if g9 == 0:
                        v.wait_ge(slot.h, dq[(tag, "xs", c)])
                        if c == 0:
                            v.wait_ge(cDVE.h, dv["consts"])
                    if c >= 1 or g9 >= 1:
                        pg = (c, g9 - 1) if g9 >= 1 else (c - 1, 1)
                        v.wait_ge(cPE.h, pe[(tag, "st") + pg])
                        v.wait_ge(cDVE.h, dv[(tag, "st") + pg])
                    for jj in range(4):
                        j = g9 * 4 + jj
                        v.tensor_copy(xb0[:, jj * 512:(jj + 1) * 512], xs[:, rr + j * 512: rr + (j + 1) * 512])
                    for jj in range(4):
                        inst = v.tensor_tensor(xsq[:, jj * 512:(jj + 1) * 512],
                                               xb0[:, jj * 512:(jj + 1) * 512],
                                               xb0[:, jj * 512:(jj + 1) * 512], ALU.mult)
                    cDVE.inc(inst)
                    dv[(tag, "st", c, g9)] = cDVE.v

                @block.tensor
                def _(t, c=c, g9=g9):
                    t.wait_ge(cDVE.h, dv[(tag, "st", c, g9)])
                    if g9 == 0 and c >= 1:
                        t.wait_ge(cDVE.h, dv[(tag, "mv", c - 1)])
                    for jj in range(4):
                        t.matmul(psA[:, 0:512], ones_bt[:], xb0[:, jj * 512:(jj + 1) * 512],
                                 start=(g9 == 0 and jj == 0), stop=(g9 == 1 and jj == 3))
                    for jj in range(4):
                        inst = t.matmul(psA[:, 512:1024], ones_bt[:], xsq[:, jj * 512:(jj + 1) * 512],
                                        start=(g9 == 0 and jj == 0), stop=(g9 == 1 and jj == 3))
                    cPE.inc(inst)
                    pe[(tag, "st", c, g9)] = cPE.v
            pe[(tag, "st", c)] = pe[(tag, "st", c, 1)]
            dv[(tag, "st", c)] = dv[(tag, "st", c, 1)]

            @block.vector
            def _(v, c=c):
                v.wait_ge(cPE.h, pe[(tag, "st", c)])
                if c >= 1:
                    v.wait_ge(cACT.h, ac[(tag, "r", c - 1)])
                    v.wait_ge(cDVE.h, dv[(tag, "z", c - 1)])
                v.tensor_scalar(Mt[:], psA[:, 0:512], 1.0 / D, None, ALU.mult)
                v.tensor_tensor(stat_sc[:], Mt[:], Mt[:], ALU.mult)
                inst = v.scalar_tensor_tensor(stat_sc[:], psA[:, 512:1024], 1.0 / D, stat_sc[:],
                                              ALU.mult, ALU.subtract)
                cDVE.inc(inst)
                dv[(tag, "mv", c)] = cDVE.v

            @block.scalar
            def _(s, c=c):
                s.wait_ge(cDVE.h, dv[(tag, "mv", c)])
                s.activation(stat_sc[:], stat_sc[:], AF.Ln, bias=fc[:, 0:1])
                inst = s.activation(Rt[:], stat_sc[:], AF.Exp, scale=fc[:, 1:2])
                cACT.inc(inst)
                ac[(tag, "r", c)] = cACT.v

            for j in range(DT8):
                @block.vector
                def _(v, c=c, rr=rr, j=j):
                    if j == 0:
                        v.wait_ge(cACT.h, ac[(tag, "r", c)])
                    pk = ("zdt", c, j - 2) if j >= 2 else (("zdt", c - 1, DT8 - 2 + j) if c >= 1 else None)
                    if pk is not None and pk in dq:
                        v.wait_ge(cOut.h, dq[pk])
                    v.scalar_tensor_tensor(zsc[:], xs[:, rr + j * 512: rr + (j + 1) * 512], 1.0,
                                           Mt[:], ALU.mult, ALU.subtract)
                    inst = v.scalar_tensor_tensor(evz[:, (j % 2) * 512:(j % 2) * 512 + 512], zsc[:], 1.0,
                                                  Rt[:], ALU.mult, ALU.mult)
                    cDVE.inc(inst)
                    dv[(tag, "zt", c, j)] = cDVE.v

                @block.sync
                def _(sy, c=c, t0=t0, j=j):
                    sy.wait_ge(cDVE.h, dv[(tag, "zt", c, j)])
                    cOut.inc(sy.dma_start(z_out_dram[j * 128:(j + 1) * 128, t0:t0 + 512],
                                          evz[:, (j % 2) * 512:(j % 2) * 512 + 512]), 16)
                    dq[("zdt", c, j)] = cOut.v
            dv[(tag, "z", c)] = dv[(tag, "zt", c, DT8 - 1)]
            dq[(tag, "zd", c)] = dq[("zdt", c, DT8 - 1)]

    emit_ln_c("ln2", x2T_d, z2_d, cX2d, cX2d.v)

    @block.sync
    def _(sy):
        sy.wait_ge(cOut.h, cOut.v)
        sy.wait_ge(cX2d.h, cX2d.v)
    return (b.finish() if standalone else b)



def build_launch_C2(b=None):
    standalone = b is None
    if standalone:
        b = Builder("C2")
    nc = b.nc
    x2T_d = b.din("x2T", [D, T], F32)
    z2_d = b.din("z2", [D, T], BF16)
    wq_d = b.din("Wq", [D, D], F32)
    wk_d = b.din("Wk", [D, D], F32)
    wv_d = b.din("Wv", [D, D], F32)
    wo_d = b.din("Wo", [D, D], F32)
    ones_b_d = b.din("ones_b", [128, 128], BF16)
    ident_f_d = b.din("ident_f", [128, 128], F32)
    bh2_d = b.din("bh2", [128, 2], BF16)
    cmask_d = b.din("cmask", [128, 1920], BF16)
    x3T_d = b.dout("x3T", [D, T], F32)
    z3_d = b.dout("z3", [D, T], BF16)
    oT_d = b.dint("oT_scratch", [D, T], BF16)

    z2 = b.sb("z2_s", [128, DT8 * T], BF16)
    wpair = b.sb("wpair", [128, 2 * 24 * 128], BF16)
    QT = b.sb("QT", [128, T], BF16)
    KT = b.sb("KT", [128, T], BF16)
    VTt = b.sb("VTt", [128, 2 * 512], F32)
    vst = b.sb("vst", [128, 32 * 130], BF16)
    probs = b.sb("probs", [128, 2 * 1024], BF16)
    cmask = b.sb("cmask_s", [128, 1920], BF16)
    ones_bt = b.sb("ones_bt", [128, 128], BF16)
    ident_f = b.sb("ident_f_s", [128, 128], F32)
    onesf = b.sb("onesf", [1, 128], F32)
    fc = b.sb("fc", [128, 4], F32)
    oTs = b.sb("oTs", [128, 2 * 1024], BF16)
    recipR = b.sb("recipR", [1, 1024], F32)
    rB = b.sb("rB", [64, 1024], F32)
    negC = b.sb("negC", [128, 2], F32)
    cmx = b.sb("cmx", [2, 32], F32)
    csb = b.sb("csb", [2, 8], F32)
    csr = b.sb("csr", [1, 2], F32)
    sqb = b.sb("sqb", [128, 2048], BF16)
    bh2 = b.sb("bh2_s", [128, 2], BF16)
    x2s = b.sb("x2s", [128, 2 * 512], F32)
    x3s = b.sb("x3s", [128, 2 * 512], F32)
    wpj = b.sb("wpj", [128, DT8 * DT8 * 128], BF16)
    xs = b.sb("xs", [128, 2 * T], F32)
    xb0 = b.sb("xb0", [128, T // 2], BF16)
    xsq = b.sb("xsq", [128, T // 2], BF16)
    Mt = b.sb("Mt", [128, 512], F32)
    Rt = b.sb("Rt", [128, 512], F32)
    stat_sc = b.sb("stat_sc", [128, 512], F32)
    zsc = b.sb("zsc", [128, 512], F32)
    evz = b.sb("evz", [128, 2 * 512], BF16)

    psA = b.ps("psA", [128, 1024], F32)
    psB = b.ps("psB", [128, 1024], F32)
    psO = b.ps("psO", [128, 1024], F32)
    psP = b.ps("psP", [128, 1024], F32)

    cLd = b.ctr("cLd")
    cTab = b.ctr("cTab")
    cWp = [b.ctr("cWp0"), b.ctr("cWp1")]
    cWpj = b.ctr("cWpj")
    cOtr = [b.ctr("cOtr0"), b.ctr("cOtr1")]
    cX0t = [b.ctr("cX0t0"), b.ctr("cX0t1")]
    cOtd = [b.ctr("cOtd0"), b.ctr("cOtd1")]
    cX1d = b.ctr("cX1d")
    cXs = [b.ctr("cXs0"), b.ctr("cXs1")]
    cOut = b.ctr("cOut")
    cPE = b.ctr("cPE")
    cACT = b.ctr("cACT")
    cDVE = b.ctr("cDVE")
    pe, dv, ac, dq = {}, {}, {}, {}

    block = b.get_block()
    SCALE = 1.0 / math.sqrt(64.0)
    wsrcs = {"q": wq_d, "k": wk_d, "v": wv_d}

    @block.sync
    def _(sy):
        for ap_s, ap_d in ((ones_bt, ones_b_d), (ident_f, ident_f_d), (bh2, bh2_d)):
            cLd.inc(sy.dma_start(ap_s[:], ap_d[:, :]), 16)
        cLd.inc(sy.dma_start(cmask[:], cmask_d[:, :]), 16)
    ld_all = cLd.v

    @block.gpsimd
    def _(g):
        for j in range(DT8):
            cTab.inc(g.dma_start(z2[:, j * T:(j + 1) * T], z2_d[j * 128:(j + 1) * 128, :]), 16)
        for kk, src in ((0, wq_d), (1, wk_d), (2, wv_d)):
            for j in range(DT8):
                cTab.inc(g.dma_start(wpair[:, kk * DT8 * 128 + j * 128: kk * DT8 * 128 + (j + 1) * 128],
                                     src[j * 128:(j + 1) * 128, 0:128]), 16)
    tab_all = cTab.v

    @block.vector
    def _(v):
        v.wait_ge(cLd.h, ld_all)
        v.memset(onesf[:], 1.0)
        v.memset(fc[:, 0:1], 1e-5)
        v.memset(fc[:, 1:2], -0.5)
        v.memset(fc[:, 2:3], SCALE)
        v.memset(fc[:, 3:4], 0.5)
        vv = vst[:].rearrange("p (k c) -> p k c", c=130)
        v.memset(vv[:, :, 64:65], 1.0)
        inst = v.memset(vv[:, :, 129:130], 1.0)
        cDVE.inc(inst)
        v.wait_ge(cDVE.h, cDVE.v)
        inst = v.tensor_copy(ones_bt[:], ones_bt[:])
        cDVE.inc(inst)
        dv["consts"] = cDVE.v

    for hp in range(8):
        wrot = (hp % 2) * 24 * 128

        if hp + 1 < 8:
            @block.gpsimd
            def _(g, hp=hp):
                if hp >= 1:
                    g.wait_ge(cPE.h, pe[("pj", hp - 1, "v", TC8 - 1)])
                nrot = ((hp + 1) % 2) * 24 * 128
                slot = cWp[(hp + 1) % 2]
                for kk, src in ((0, wq_d), (1, wk_d), (2, wv_d)):
                    for j in range(DT8):
                        slot.inc(g.dma_start(
                            wpair[:, nrot + kk * DT8 * 128 + j * 128: nrot + kk * DT8 * 128 + (j + 1) * 128],
                            src[j * 128:(j + 1) * 128, (hp + 1) * 128:(hp + 2) * 128]), 16)
                dq[("wp", hp + 1)] = slot.v

        for which, kk in (("q", 0), ("k", 1), ("v", 2)):
            for tc in range(TC8):
                psx = psA if (tc % 2 == 0) else psB

                @block.tensor
                def _(t, hp=hp, which=which, kk=kk, wrot=wrot, tc=tc, psx=psx):
                    if tc == 0:
                        if hp == 0 and which == "q":
                            t.wait_ge(cTab.h, tab_all)
                            t.wait_ge(cLd.h, ld_all)
                            t.wait_ge(cDVE.h, dv["consts"])
                        elif which == "q":
                            t.wait_ge(cWp[hp % 2].h, dq[("wp", hp)])
                            t.wait_ge(cDVE.h, dv[("norm", hp - 1, 1, 3)])
                            t.wait_ge(cDVE.h, dv[("rbev", hp - 1, 1, 3)])
                    if tc >= 2:
                        t.wait_ge(cDVE.h, dv[("pjev", hp, which, tc - 2)])
                    elif which == "k":
                        t.wait_ge(cDVE.h, dv[("pjev", hp, "q", TC8 - 2 + tc)])
                    elif which == "v":
                        t.wait_ge(cDVE.h, dv[("pjev", hp, "k", TC8 - 2 + tc)])
                    for j in range(DT8):
                        inst = t.matmul(psx[:, 0:512],
                                        wpair[:, wrot + kk * DT8 * 128 + j * 128: wrot + kk * DT8 * 128 + (j + 1) * 128],
                                        z2[:, j * T + tc * 512: j * T + tc * 512 + 512],
                                        start=(j == 0), stop=(j == DT8 - 1))
                    cPE.inc(inst)
                    pe[("pj", hp, which, tc)] = cPE.v

                if which in ("q", "k"):
                    @block.vector
                    def _(v, hp=hp, which=which, tc=tc, psx=psx):
                        dst = QT if which == "q" else KT
                        v.wait_ge(cPE.h, pe[("pj", hp, which, tc)])
                        inst = v.tensor_copy(dst[:, tc * 512:(tc + 1) * 512], psx[:, 0:512])
                        cDVE.inc(inst)
                        dv[("pjev", hp, which, tc)] = cDVE.v
                else:
                    @block.vector
                    def _(v, hp=hp, tc=tc, psx=psx):
                        v.wait_ge(cPE.h, pe[("pj", hp, "v", tc)])
                        if tc >= 2:
                            v.wait_ge(cPE.h, pe[("vt", hp, tc - 2)])
                        inst = v.tensor_copy(VTt[:, (tc % 2) * 512:(tc % 2) * 512 + 512], psx[:, 0:512])
                        cDVE.inc(inst)
                        dv[("pjev", hp, "v", tc)] = cDVE.v

                    @block.tensor
                    def _(t, hp=hp, tc=tc):
                        t.wait_ge(cDVE.h, dv[("pjev", hp, "v", tc)])
                        if tc >= 2:
                            t.wait_ge(cDVE.h, dv[("vtev", hp, tc - 2)])
                        elif hp > 0:
                            t.wait_ge(cDVE.h, dv[("vtev", hp - 1, TC8 - 2 + tc)])
                        for k4 in range(4):
                            inst = t.transpose(psO[:, (tc % 2) * 512 + k4 * 128:(tc % 2) * 512 + (k4 + 1) * 128],
                                               VTt[:, (tc % 2) * 512 + k4 * 128:(tc % 2) * 512 + (k4 + 1) * 128],
                                               ident_f[:])
                        cPE.inc(inst)
                        pe[("vt", hp, tc)] = cPE.v

                    @block.vector
                    def _(v, hp=hp, tc=tc):
                        v.wait_ge(cPE.h, pe[("vt", hp, tc)])
                        src = psO[:, (tc % 2) * 512:(tc % 2) * 512 + 512].rearrange(
                            "p (k two dv) -> p k two dv", two=2, dv=64)
                        dstv = vst[:, tc * 4 * 130:(tc + 1) * 4 * 130].rearrange("p (k c) -> p k c", c=130)
                        v.tensor_copy(dstv[:, :, 0:64], src[:, :, 0, :])
                        inst = v.tensor_copy(dstv[:, :, 65:129], src[:, :, 1, :])
                        cDVE.inc(inst)
                        dv[("vtev", hp, tc)] = cDVE.v

        @block.vector
        def _(v, hp=hp):
            v.wait_ge(cDVE.h, dv[("pjev", hp, "q", TC8 - 1)])
            inst = v.tensor_copy(qsw_dummy := csr[:], csr[:]) if False else None
            inst = v.tensor_copy(csr[:], csr[:])
            cDVE.inc(inst)
            dv[("ropec", hp)] = cDVE.v

        for w8, src in (("q", QT), ("k", KT)):
            @block.scalar
            def _(sc, hp=hp, w8=w8, src=src):
                sc.wait_ge(cDVE.h, dv[("pjev", hp, w8, TC8 - 1)])
                if w8 == "k":
                    sc.wait_ge(cACT.h, ac[("sq", hp, "q", 1)])
                for half in range(2):
                    inst = sc.activation(sqb[:], src[:, half * 2048:(half + 1) * 2048], AF.Square)
                    cACT.inc(inst)
                    ac[("sq", hp, w8, half)] = cACT.v
            for half in range(2):
                for cc in range(4):
                    @block.tensor
                    def _(t, hp=hp, w8=w8, half=half, cc=cc):
                        if cc == 0:
                            t.wait_ge(cACT.h, ac[("sq", hp, w8, half)])
                        idx = (half * 4 + cc)
                        pk = None
                        if idx >= 2:
                            pk = ("nrmred", hp, w8, (idx - 2) // 4, (idx - 2) % 4)
                        elif w8 == "k":
                            pk = ("nrmred", hp, "q", (6 + idx) // 4, (6 + idx) % 4)
                        elif hp > 0:
                            pk = ("negC", hp - 1)
                        if pk is not None and pk in dv:
                            t.wait_ge(cDVE.h, dv[pk])
                        inst = t.matmul(psB[0:2, (cc % 2) * 512:(cc % 2) * 512 + 512],
                                        bh2[:], sqb[:, cc * 512:(cc + 1) * 512],
                                        start=True, stop=True)
                        cPE.inc(inst)
                        pe[("nrm", hp, w8, half, cc)] = cPE.v

                    @block.vector
                    def _(v, hp=hp, w8=w8, half=half, cc=cc):
                        base = 0 if w8 == "q" else 16
                        v.wait_ge(cPE.h, pe[("nrm", hp, w8, half, cc)])
                        inst = v.tensor_reduce(cmx[:, base + half * 4 + cc: base + half * 4 + cc + 1],
                                               psB[0:2, (cc % 2) * 512:(cc % 2) * 512 + 512],
                                               axis=AX.X, op=ALU.max)
                        cDVE.inc(inst)
                        dv[("nrmred", hp, w8, half, cc)] = cDVE.v

        @block.vector
        def _(v, hp=hp):
            v.wait_ge(cDVE.h, dv[("nrmred", hp, "k", 1, 3)])
            v.tensor_reduce(csb[:, 0:1], cmx[:, 0:8], axis=AX.X, op=ALU.max)
            inst = v.tensor_reduce(csb[:, 1:2], cmx[:, 16:24], axis=AX.X, op=ALU.max)
            cDVE.inc(inst)
            v.wait_ge(cDVE.h, cDVE.v)
            inst = v.tensor_tensor(csb[:, 0:1], csb[:, 0:1], csb[:, 1:2], ALU.mult)
            cDVE.inc(inst)
            dv[("csb", hp)] = cDVE.v

        @block.scalar
        def _(sc, hp=hp):
            sc.wait_ge(cDVE.h, dv[("csb", hp)])
            sc.activation(csb[:, 0:1], csb[:, 0:1], AF.Ln)
            inst = sc.activation(csb[:, 0:1], csb[:, 0:1], AF.Exp, scale=fc[0:2, 3:4])
            cACT.inc(inst)
            ac[("csqrt", hp)] = cACT.v

        @block.vector
        def _(v, hp=hp):
            v.wait_ge(cACT.h, ac[("csqrt", hp)])
            inst = v.tensor_scalar(csb[:, 0:1], csb[:, 0:1], -SCALE * 1.05, None, ALU.mult)
            cDVE.inc(inst)
            dv[("csc", hp)] = cDVE.v

        @block.tensor
        def _(t, hp=hp):
            t.wait_ge(cDVE.h, dv[("csc", hp)])
            inst = t.transpose(psB[0:8, 0:2], csb[:, 0:8], ident_f[0:2, 0:2])
            cPE.inc(inst)
            pe[("ct", hp)] = cPE.v

        @block.vector
        def _(v, hp=hp):
            v.wait_ge(cPE.h, pe[("ct", hp)])
            inst = v.tensor_copy(csr[:], psB[0:1, 0:2])
            cDVE.inc(inst)
            dv[("csr", hp)] = cDVE.v

        @block.tensor
        def _(t, hp=hp):
            t.wait_ge(cDVE.h, dv[("csr", hp)])
            inst = t.matmul(psB[:, 0:2], onesf[0:1, :], csr[0:1, 0:2], start=True, stop=True)
            cPE.inc(inst)
            pe[("cbc", hp)] = cPE.v

        @block.vector
        def _(v, hp=hp):
            v.wait_ge(cPE.h, pe[("cbc", hp)])
            inst = v.tensor_copy(negC[:], psB[:, 0:2])
            cDVE.inc(inst)
            dv[("negC", hp)] = cDVE.v

        for h2 in range(2):
            hb = h2 * 64
            vofs = h2 * 65
            for qc in range(4):
                q0 = qc * 1024
                nkt = (qc + 1) * 8
                pso = psO if (qc % 2 == 0) else psP
                prev = (hp, h2, qc - 1) if qc > 0 else ((hp, 0, 3) if h2 > 0 else ((hp - 1, 1, 3) if hp > 0 else None))
                pk = (hp, h2, qc - 2) if qc >= 2 else ((hp, 0, qc + 2) if h2 == 1 else ((hp - 1, 1, qc + 2) if hp > 0 else None))

                for kt in range(nkt):
                    pss = psA if (kt % 2 == 0) else psB
                    diag = (kt >= qc * 8)

                    @block.tensor
                    def _(t, hp=hp, h2=h2, hb=hb, qc=qc, q0=q0, pso=pso, vofs=vofs,
                          prev=prev, pk=pk, kt=kt, pss=pss, nkt=nkt, diag=diag):
                        if kt == 0:
                            if h2 == 0 and qc == 0:
                                t.wait_ge(cDVE.h, dv[("vtev", hp, TC8 - 1)])
                                t.wait_ge(cDVE.h, dv[("negC", hp)])
                            if prev is not None and (("rbev",) + prev) in dv:
                                t.wait_ge(cDVE.h, dv[("rbev",) + prev])
                            if pk is not None and (("norm",) + pk) in dv:
                                t.wait_ge(cDVE.h, dv[("norm",) + pk])
                        if kt >= 2:
                            t.wait_ge(cDVE.h, dv[("pexp", hp, h2, qc, kt - 2)])
                        for half in range(2):
                            inst = t.matmul(pss[:, half * 512:(half + 1) * 512],
                                            KT[hb:hb + 64, kt * 128:(kt + 1) * 128],
                                            QT[hb:hb + 64, q0 + half * 512: q0 + (half + 1) * 512],
                                            start=True, stop=True)
                        cPE.inc(inst)
                        pe[("s", hp, h2, qc, kt)] = cPE.v
                        if kt >= 1:
                            t.wait_ge(cDVE.h, dv[("pexp", hp, h2, qc, kt - 1)])
                            pr = probs[:, ((kt - 1) % 2) * 1024:((kt - 1) % 2) * 1024 + 1024]
                            for half in range(2):
                                inst = t.matmul(pso[0:65, half * 512:(half + 1) * 512],
                                                vst[:, (kt - 1) * 130 + vofs:(kt - 1) * 130 + vofs + 65],
                                                pr[:, half * 512:(half + 1) * 512],
                                                start=(kt - 1 == 0), stop=False)
                            cPE.inc(inst)
                            pe[("v", hp, h2, qc, kt - 1)] = cPE.v

                    @block.scalar
                    def _(s, hp=hp, h2=h2, qc=qc, kt=kt, pss=pss):
                        s.wait_ge(cPE.h, pe[("s", hp, h2, qc, kt)])
                        if kt >= 2:
                            s.wait_ge(cPE.h, pe[("v", hp, h2, qc, kt - 2)])
                        inst = s.activation(probs[:, (kt % 2) * 1024:(kt % 2) * 1024 + 1024],
                                            pss[:], AF.Exp, bias=negC[:, h2:h2 + 1], scale=fc[:, 2:3])
                        cACT.inc(inst)
                        ac[("exp", hp, h2, qc, kt)] = cACT.v

                    @block.vector
                    def _(v, hp=hp, h2=h2, qc=qc, kt=kt, diag=diag):
                        v.wait_ge(cACT.h, ac[("exp", hp, h2, qc, kt)])
                        pr = probs[:, (kt % 2) * 1024:(kt % 2) * 1024 + 1024]
                        if diag:
                            r = kt - qc * 8
                            ofs = 896 - 128 * r
                            inst = v.tensor_tensor(pr[:], pr[:], cmask[:, ofs:ofs + 1024], ALU.mult)
                            cDVE.inc(inst)
                        else:
                            inst = v.tensor_copy(csr[:], csr[:])
                            cDVE.inc(inst)
                        dv[("pexp", hp, h2, qc, kt)] = cDVE.v

                @block.tensor
                def _(t, hp=hp, h2=h2, qc=qc, pso=pso, vofs=vofs, nkt=nkt):
                    t.wait_ge(cDVE.h, dv[("pexp", hp, h2, qc, nkt - 1)])
                    pr = probs[:, ((nkt - 1) % 2) * 1024:((nkt - 1) % 2) * 1024 + 1024]
                    for half in range(2):
                        inst = t.matmul(pso[0:65, half * 512:(half + 1) * 512],
                                        vst[:, (nkt - 1) * 130 + vofs:(nkt - 1) * 130 + vofs + 65],
                                        pr[:, half * 512:(half + 1) * 512],
                                        start=(nkt - 1 == 0), stop=True)
                    cPE.inc(inst)
                    pe[("v", hp, h2, qc, nkt - 1)] = cPE.v

                @block.vector
                def _(v, hp=hp, h2=h2, qc=qc, pso=pso, nkt=nkt):
                    v.wait_ge(cPE.h, pe[("v", hp, h2, qc, nkt - 1)])
                    inst = v.reciprocal(recipR[:], pso[64:65, 0:1024])
                    cDVE.inc(inst)
                    dv[("recip", hp, h2, qc)] = cDVE.v

                @block.tensor
                def _(t, hp=hp, h2=h2, qc=qc):
                    t.wait_ge(cDVE.h, dv[("recip", hp, h2, qc)])
                    t.wait_ge(cDVE.h, dv[("pexp", hp, h2, qc, (qc + 1) * 8 - 1)])
                    for half in range(2):
                        inst = t.matmul(psA[0:64, half * 512:(half + 1) * 512],
                                        onesf[0:1, 0:64],
                                        recipR[0:1, half * 512:(half + 1) * 512],
                                        start=True, stop=True)
                    cPE.inc(inst)
                    pe[("bc", hp, h2, qc)] = cPE.v

                @block.vector
                def _(v, hp=hp, h2=h2, qc=qc, pso=pso):
                    v.wait_ge(cPE.h, pe[("bc", hp, h2, qc)])
                    head = hp * 2 + h2
                    if ("oTdma", hp, h2, qc - 2) in dq:
                        v.wait_ge(cOtd[qc % 2].h, dq[("oTdma", hp, h2, qc - 2)])
                    elif hp > 0 and qc < 2:
                        v.wait_ge(cOtd[qc % 2].h, dq[("oTdma", hp - 1, h2, qc + 2)])
                    inst = v.tensor_copy(rB[:], psA[0:64, 0:1024])
                    cDVE.inc(inst)
                    dv[("rbev", hp, h2, qc)] = cDVE.v
                    ob = oTs[(head % 2) * 64:(head % 2) * 64 + 64, (qc % 2) * 1024:(qc % 2) * 1024 + 1024]
                    inst = v.tensor_tensor(ob, pso[0:64, 0:1024], rB[:], ALU.mult)
                    cDVE.inc(inst)
                    dv[("norm", hp, h2, qc)] = cDVE.v

                @block.sync
                def _(sy, hp=hp, h2=h2, qc=qc, q0=q0):
                    sy.wait_ge(cDVE.h, dv[("norm", hp, h2, qc)])
                    head = hp * 2 + h2
                    row0 = (head // 2) * 128 + (head % 2) * 64
                    cOtd[qc % 2].inc(sy.dma_start(
                        oT_d[row0:row0 + 64, q0:q0 + 1024],
                        oTs[(head % 2) * 64:(head % 2) * 64 + 64, (qc % 2) * 1024:(qc % 2) * 1024 + 1024]), 16)
                    dq[("oTdma", hp, h2, qc)] = cOtd[qc % 2].v

    @block.gpsimd
    def _(g):
        g.wait_ge(cDVE.h, dv[("rbev", 7, 1, 3)])
        for et in range(DT8):
            for j in range(DT8):
                cWpj.inc(g.dma_start(wpj[:, et * D + j * 128: et * D + (j + 1) * 128],
                                     wo_d[j * 128:(j + 1) * 128, et * 128:(et + 1) * 128]), 16)
    wpj_all = cWpj.v

    for tc in range(TC8):
        t0 = tc * 512
        stage = QT if tc % 2 == 0 else KT
        oslot = cOtr[tc % 2]

        @block.sync
        def _(sy, tc=tc, t0=t0, stage=stage, oslot=oslot):
            if tc == 0:
                sy.wait_ge(cDVE.h, dv[("rbev", 7, 1, 3)])
                sy.wait_ge(cOtd[0].h, cOtd[0].v)
                sy.wait_ge(cOtd[1].h, cOtd[1].v)
            if tc >= 2:
                sy.wait_ge(cPE.h, pe[("wp", tc - 2, DT8 - 1)])
            for j in range(DT8):
                oslot.inc(sy.dma_start(stage[:, j * 512:(j + 1) * 512],
                                       oT_d[j * 128:(j + 1) * 128, t0:t0 + 512]), 16)
            dq[("oTr", tc)] = oslot.v

        for et in range(DT8):
            psx = psA if ((tc * DT8 + et) % 2 == 0) else psB
            xslot = cX0t[et % 2]

            @block.sync
            def _(sy, tc=tc, t0=t0, et=et, xslot=xslot):
                if tc >= 1 or et >= 2:
                    prevk = (tc, et - 2) if et >= 2 else (tc - 1, DT8 - 2 + et)
                    sy.wait_ge(cDVE.h, dv[("x3",) + prevk])
                xslot.inc(sy.dma_start(x2s[:, (et % 2) * 512:(et % 2) * 512 + 512],
                                       x2T_d[et * 128:(et + 1) * 128, t0:t0 + 512]), 16)
                dq[("x0t", tc, et)] = xslot.v

            @block.tensor
            def _(t, tc=tc, et=et, psx=psx, stage=stage, oslot=oslot):
                if et == 0:
                    t.wait_ge(oslot.h, dq[("oTr", tc)])
                if tc == 0 and et == 0:
                    t.wait_ge(cWpj.h, wpj_all)
                key = ("x3", tc, et - 2) if et >= 2 else (("x3", tc - 1, DT8 - 2 + et) if tc >= 1 else None)
                if key is not None and key in dv:
                    t.wait_ge(cDVE.h, dv[key])
                for j in range(DT8):
                    inst = t.matmul(psx[:, 0:512], wpj[:, et * D + j * 128: et * D + (j + 1) * 128],
                                    stage[:, j * 512:(j + 1) * 512],
                                    start=(j == 0), stop=(j == DT8 - 1))
                cPE.inc(inst)
                pe[("wp", tc, et)] = cPE.v

            @block.vector
            def _(v, tc=tc, t0=t0, et=et, psx=psx, xslot=xslot):
                v.wait_ge(cPE.h, pe[("wp", tc, et)])
                v.wait_ge(xslot.h, dq[("x0t", tc, et)])
                v.tensor_tensor(x3s[:, (et % 2) * 512:(et % 2) * 512 + 512],
                                psx[:, 0:512], x2s[:, (et % 2) * 512:(et % 2) * 512 + 512], ALU.add)
                inst = v.tensor_tensor(x3s[:, (et % 2) * 512:(et % 2) * 512 + 512],
                                       x3s[:, (et % 2) * 512:(et % 2) * 512 + 512],
                                       z2[:, et * T + t0: et * T + t0 + 512], ALU.add)
                cDVE.inc(inst)
                dv[("x3", tc, et)] = cDVE.v

            @block.sync
            def _(sy, tc=tc, et=et, t0=t0):
                sy.wait_ge(cDVE.h, dv[("x3", tc, et)])
                cX1d.inc(sy.dma_start(x3T_d[et * 128:(et + 1) * 128, t0:t0 + 512],
                                      x3s[:, (et % 2) * 512:(et % 2) * 512 + 512]), 16)
                dq[("x3d", tc, et)] = cX1d.v

    E = LnEnv(block, xs, xb0, xsq, Mt, Rt, stat_sc, zsc, evz, ones_bt, fc, psA,
              cXs, cOut, cPE, cACT, cDVE, pe, dv, ac, dq)
    emit_ln_generic(E, "ln3", x3T_d, z3_d, cX1d, cX1d.v)

    @block.sync
    def _(sy):
        sy.wait_ge(cOut.h, cOut.v)
        sy.wait_ge(cX1d.h, cX1d.v)
    return (b.finish() if standalone else b)



MAGIC = 12582912.0


def build_launch_C3(b=None, debug=False):
    standalone = b is None
    if standalone:
        b = Builder("C3")
    nc = b.nc
    x3T_d = b.din("x3T", [D, T], F32)
    z3_d = b.din("z3", [D, T], BF16)
    w1T_d = b.din("W1T", [D, 4 * D], F32)
    w2T_d = b.din("W2T", [4 * D, D], F32)
    b1_col_d = b.din("b1_col", [128, 32], F32)
    b2_col_d = b.din("b2_col", [128, 8], F32)
    ones_b_d = b.din("ones_b", [128, 128], BF16)
    ident_f_d = b.din("ident_f", [128, 128], F32)
    x4T_d = b.dout("x4T", [D, T], F32)
    z4_d = b.dout("z4", [D, T], BF16)
    h_d = (b.dout if debug else b.dint)("h_scratch", [4 * D, T], BF16)
    if debug:
        w1t_dbg = b.dout("w1t_dbg", [128, 4 * D], BF16)
        sc_dbg = b.dout("sc_dbg", [1, 2], F32)

    z3 = b.sb("z3_s", [128, DT8 * T], BF16)
    wbig = b.sb("wbig", [128, 32 * 1024], BF16)
    xs = b.sb("xs", [128, 2 * T], F32)
    acc = b.sb("acc", [128, 8], F32)
    accs = b.sb("accs", [1, 128], F32)
    sc1 = b.sb("sc1", [1, 1], F32)
    inv128 = b.sb("inv128", [128, 1], F32)
    onesf = b.sb("onesf", [1, 128], F32)
    b1c = b.sb("b1c", [128, 32], F32)
    b2c = b.sb("b2c", [128, 8], F32)
    ones_bt = b.sb("ones_bt", [128, 128], BF16)
    ident_f = b.sb("ident_f_s", [128, 128], F32)
    fc = b.sb("fc", [128, 4], F32)
    evh = b.sb("evh", [128, 2 * 512], BF16)
    x3s = b.sb("x3s", [128, 2 * 512], F32)
    x4s = b.sb("x4s", [128, 2 * 512], F32)
    xb0 = b.sb("xb0", [128, T // 2], BF16)
    xsq = b.sb("xsq", [128, T // 2], BF16)
    Mt = b.sb("Mt", [128, 512], F32)
    Rt = b.sb("Rt", [128, 512], F32)
    stat_sc = b.sb("stat_sc", [128, 512], F32)
    zsc = b.sb("zsc", [128, 512], F32)
    evz = b.sb("evz", [128, 2 * 512], BF16)

    psA = b.ps("psA", [128, 1024], F32)
    psB = b.ps("psB", [128, 1024], F32)

    cLd = b.ctr("cLd")
    cZ = b.ctr("cZ")
    cWt = [b.ctr("cWt0"), b.ctr("cWt1")]
    cHs = [b.ctr("cHs0"), b.ctr("cHs1")]
    cHd = b.ctr("cHd")
    cX0t = [b.ctr("cX0t0"), b.ctr("cX0t1")]
    cX1d = b.ctr("cX1d")
    cXs = [b.ctr("cXs0"), b.ctr("cXs1")]
    cOut = b.ctr("cOut")
    cPE = b.ctr("cPE")
    cACT = b.ctr("cACT")
    cDVE = b.ctr("cDVE")
    pe, dv, ac, dq = {}, {}, {}, {}

    block = b.get_block()

    @block.sync
    def _(sy):
        for ap_s, ap_d in ((b1c, b1_col_d), (b2c, b2_col_d),
                           (ones_bt, ones_b_d), (ident_f, ident_f_d)):
            cLd.inc(sy.dma_start(ap_s[:], ap_d[:, :]), 16)
    ld_all = cLd.v

    @block.gpsimd
    def _(g):
        for j in range(DT8):
            cZ.inc(g.dma_start(z3[:, j * T:(j + 1) * T], z3_d[j * 128:(j + 1) * 128, :]), 16)
    z_all = cZ.v

    @block.vector
    def _(v):
        v.wait_ge(cLd.h, ld_all)
        v.memset(onesf[:], 1.0)
        v.memset(fc[:, 0:1], 1e-5)
        inst = v.memset(fc[:, 1:2], -0.5)
        cDVE.inc(inst)
        v.wait_ge(cDVE.h, cDVE.v)
        inst = v.tensor_copy(ones_bt[:], ones_bt[:])
        cDVE.inc(inst)
        dv["consts"] = cDVE.v

    def emit_ternary(tag, w_dram, nrowtiles, rowlen, nelem, wbig_free_pe=None,
                     wbig_free_act=None):
        for rt in range(nrowtiles):
            slot = cWt[rt % 2]
            rr = (rt % 2) * T

            @block.sync
            def _(sy, rt=rt, slot=slot, rr=rr):
                if (tag, "red", rt - 2) in dv:
                    sy.wait_ge(cDVE.h, dv[(tag, "red", rt - 2)])
                if tag == "w2" and rt < 2:
                    sy.wait_ge(cDVE.h, dv[("w1", "tern", nrowtiles_w1 - 1)]) if False else None
                cw = min(2048, rowlen)
                for cc in range(max(1, rowlen // 2048)):
                    slot.inc(sy.dma_start(xs[:, rr + cc * cw: rr + (cc + 1) * cw],
                                          w_dram[rt * 128:(rt + 1) * 128, cc * cw:(cc + 1) * cw]), 16)
                dq[(tag, "w", rt)] = slot.v

            @block.vector
            def _(v, rt=rt, slot=slot, rr=rr):
                v.wait_ge(slot.h, dq[(tag, "w", rt)])
                inst = v.tensor_reduce(acc[:, rt % 8: rt % 8 + 1] if False else acc[:, 1:2],
                                       xs[:, rr:rr + rowlen], axis=AX.X, op=ALU.add,
                                       apply_absolute_value=True)
                cDVE.inc(inst)
                v.wait_ge(cDVE.h, cDVE.v)
                if rt == 0:
                    inst = v.tensor_copy(acc[:, 0:1], acc[:, 1:2])
                else:
                    inst = v.tensor_tensor(acc[:, 0:1], acc[:, 0:1], acc[:, 1:2], ALU.add)
                cDVE.inc(inst)
                dv[(tag, "red", rt)] = cDVE.v

        @block.tensor
        def _(t):
            t.wait_ge(cDVE.h, dv[(tag, "red", nrowtiles - 1)])
            if wbig_free_pe is not None:
                t.wait_ge(cPE.h, wbig_free_pe)
            if wbig_free_act is not None:
                t.wait_ge(cACT.h, wbig_free_act)
            inst = t.transpose(psB[0:8, 0:128], acc[:, 0:8], ident_f[:])
            cPE.inc(inst)
            pe[(tag, "tr")] = cPE.v

        @block.vector
        def _(v):
            v.wait_ge(cPE.h, pe[(tag, "tr")])
            inst = v.tensor_copy(accs[:], psB[0:1, 0:128])
            cDVE.inc(inst)
            v.wait_ge(cDVE.h, cDVE.v)
            inst = v.tensor_reduce(sc1[:], accs[:], axis=AX.X, op=ALU.add)
            cDVE.inc(inst)
            v.wait_ge(cDVE.h, cDVE.v)
            v.tensor_scalar(sc1[:], sc1[:], 1.0 / nelem, None, ALU.mult)
            v.tensor_scalar(sc1[:], sc1[:], 1e-8, None, ALU.max)
            inst = v.reciprocal(sc1[:], sc1[:])
            cDVE.inc(inst)
            dv[(tag, "sc")] = cDVE.v

        @block.tensor
        def _(t):
            t.wait_ge(cDVE.h, dv[(tag, "sc")])
            inst = t.matmul(psB[:, 0:1], onesf[0:1, :], sc1[0:1, 0:1], start=True, stop=True)
            cPE.inc(inst)
            pe[(tag, "bc")] = cPE.v

        @block.vector
        def _(v):
            v.wait_ge(cPE.h, pe[(tag, "bc")])
            inst = v.tensor_copy(inv128[:], psB[:, 0:1])
            cDVE.inc(inst)
            v.wait_ge(cDVE.h, cDVE.v)
            dv[(tag, "inv")] = cDVE.v

        for rt in range(nrowtiles):
            slot = cWt[rt % 2]
            rr = (rt % 2) * T

            @block.sync
            def _(sy, rt=rt, slot=slot, rr=rr):
                if rt < 2:
                    sy.wait_ge(cDVE.h, dv[(tag, "inv")])
                if (tag, "tern", rt - 2) in dv:
                    sy.wait_ge(cDVE.h, dv[(tag, "tern", rt - 2)])
                cw = min(2048, rowlen)
                for cc in range(max(1, rowlen // 2048)):
                    slot.inc(sy.dma_start(xs[:, rr + cc * cw: rr + (cc + 1) * cw],
                                          w_dram[rt * 128:(rt + 1) * 128, cc * cw:(cc + 1) * cw]), 16)
                dq[(tag, "w2p", rt)] = slot.v

            @block.vector
            def _(v, rt=rt, slot=slot, rr=rr):
                v.wait_ge(slot.h, dq[(tag, "w2p", rt)])
                v.tensor_scalar(xs[:, rr:rr + rowlen], xs[:, rr:rr + rowlen],
                                inv128[:, 0:1], MAGIC, ALU.mult, ALU.add)
                v.tensor_scalar(xs[:, rr:rr + rowlen], xs[:, rr:rr + rowlen],
                                MAGIC, None, ALU.subtract)
                v.tensor_scalar(xs[:, rr:rr + rowlen], xs[:, rr:rr + rowlen], 1.0, None, ALU.min)
                inst = v.tensor_scalar(wbig[:, rt * rowlen:(rt + 1) * rowlen],
                                       xs[:, rr:rr + rowlen], -1.0, None, ALU.max)
                cDVE.inc(inst)
                dv[(tag, "tern", rt)] = cDVE.v
        dv[(tag, "done")] = dv[(tag, "tern", nrowtiles - 1)]

    emit_ternary("w1", w1T_d, DT8, 4 * D, 4 * D * D)

    if debug:
        @block.sync
        def _(sy):
            sy.wait_ge(cDVE.h, dv[("w1", "done")])
            cOut.inc(sy.dma_start(w1t_dbg[:, :], wbig[:, 0:4 * D]), 16)
            cOut.inc(sy.dma_start(sc_dbg[0:1, 0:1], sc1[0:1, 0:1]), 16)
            sy.wait_ge(cOut.h, cOut.v)

    for tc in range(TC8):
        t0 = tc * 512
        for ft in range(32):
            idx = tc * 32 + ft
            psx = psA if (idx % 2 == 0) else psB

            @block.tensor
            def _(t, tc=tc, ft=ft, idx=idx, psx=psx):
                if idx == 0:
                    t.wait_ge(cDVE.h, dv[("w1", "done")])
                    t.wait_ge(cZ.h, z_all)
                if idx >= 2:
                    t.wait_ge(cACT.h, ac[("h", idx - 2)])
                for j in range(DT8):
                    inst = t.matmul(psx[:, 0:512],
                                    wbig[:, j * 4 * D + ft * 128: j * 4 * D + (ft + 1) * 128],
                                    z3[:, j * T + t0: j * T + t0 + 512],
                                    start=(j == 0), stop=(j == DT8 - 1))
                cPE.inc(inst)
                pe[("h", idx)] = cPE.v

            @block.scalar
            def _(s, tc=tc, ft=ft, idx=idx, psx=psx):
                s.wait_ge(cPE.h, pe[("h", idx)])
                if ("hd", idx - 2) in dq:
                    s.wait_ge(cHd.h, dq[("hd", idx - 2)])
                inst = s.activation(evh[:, (idx % 2) * 512:(idx % 2) * 512 + 512],
                                    psx[:, 0:512], AF.Gelu, bias=b1c[:, ft:ft + 1])
                cACT.inc(inst)
                ac[("h", idx)] = cACT.v

            @block.sync
            def _(sy, tc=tc, t0=t0, ft=ft, idx=idx):
                sy.wait_ge(cACT.h, ac[("h", idx)])
                cHd.inc(sy.dma_start(h_d[ft * 128:(ft + 1) * 128, t0:t0 + 512],
                                     evh[:, (idx % 2) * 512:(idx % 2) * 512 + 512]), 16)
                dq[("hd", idx)] = cHd.v

    emit_ternary("w2", w2T_d, 32, D, 4 * D * D, wbig_free_pe=pe[("h", TC8 * 32 - 1)],
                 wbig_free_act=ac[("h", TC8 * 32 - 1)])

    for tc in range(TC8):
        t0 = tc * 512
        hslot = cHs[tc % 2]
        hofs = (tc % 2) * 32 * 512

        @block.sync
        def _(sy, tc=tc, t0=t0, hslot=hslot, hofs=hofs):
            if tc == 0:
                sy.wait_ge(cHd.h, cHd.v)
                sy.wait_ge(cPE.h, pe[("h", TC8 * 32 - 1)])
            if tc >= 2:
                sy.wait_ge(cPE.h, pe[("y", tc - 2, DT8 - 1)])
            for j in range(32):
                hslot.inc(sy.dma_start(z3[:, hofs + j * 512: hofs + (j + 1) * 512],
                                       h_d[j * 128:(j + 1) * 128, t0:t0 + 512]), 16)
            dq[("hs", tc)] = hslot.v

        for et in range(DT8):
            psx = psA if ((tc * DT8 + et) % 2 == 0) else psB
            xslot = cX0t[et % 2]

            @block.sync
            def _(sy, tc=tc, t0=t0, et=et, xslot=xslot):
                if tc >= 1 or et >= 2:
                    prevk = (tc, et - 2) if et >= 2 else (tc - 1, DT8 - 2 + et)
                    sy.wait_ge(cDVE.h, dv[("x4",) + prevk])
                xslot.inc(sy.dma_start(x3s[:, (et % 2) * 512:(et % 2) * 512 + 512],
                                       x3T_d[et * 128:(et + 1) * 128, t0:t0 + 512]), 16)
                dq[("x0t", tc, et)] = xslot.v

            @block.tensor
            def _(t, tc=tc, et=et, psx=psx, hslot=hslot, hofs=hofs):
                if et == 0:
                    t.wait_ge(hslot.h, dq[("hs", tc)])
                    if tc == 0:
                        t.wait_ge(cDVE.h, dv[("w2", "done")])
                key = ("x4", tc, et - 2) if et >= 2 else (("x4", tc - 1, DT8 - 2 + et) if tc >= 1 else None)
                if key is not None and key in dv:
                    t.wait_ge(cDVE.h, dv[key])
                for j in range(32):
                    inst = t.matmul(psx[:, 0:512], wbig[:, j * D + et * 128: j * D + et * 128 + 128],
                                    z3[:, hofs + j * 512: hofs + (j + 1) * 512],
                                    start=(j == 0), stop=(j == 31))
                cPE.inc(inst)
                pe[("y", tc, et)] = cPE.v

            @block.vector
            def _(v, tc=tc, et=et, psx=psx, xslot=xslot):
                v.wait_ge(cPE.h, pe[("y", tc, et)])
                v.wait_ge(xslot.h, dq[("x0t", tc, et)])
                inst = v.scalar_tensor_tensor(x4s[:, (et % 2) * 512:(et % 2) * 512 + 512],
                                              psx[:, 0:512], b2c[:, et:et + 1],
                                              x3s[:, (et % 2) * 512:(et % 2) * 512 + 512],
                                              ALU.add, ALU.add)
                cDVE.inc(inst)
                dv[("x4", tc, et)] = cDVE.v

            @block.sync
            def _(sy, tc=tc, et=et, t0=t0):
                sy.wait_ge(cDVE.h, dv[("x4", tc, et)])
                cX1d.inc(sy.dma_start(x4T_d[et * 128:(et + 1) * 128, t0:t0 + 512],
                                      x4s[:, (et % 2) * 512:(et % 2) * 512 + 512]), 16)
                dq[("x4d", tc, et)] = cX1d.v

    E = LnEnv(block, xs, xb0, xsq, Mt, Rt, stat_sc, zsc, evz, ones_bt, fc, psA,
              cXs, cOut, cPE, cACT, cDVE, pe, dv, ac, dq)
    emit_ln_generic(E, "ln4", x4T_d, z4_d, cX1d, cX1d.v)

    @block.sync
    def _(sy):
        sy.wait_ge(cOut.h, cOut.v)
        sy.wait_ge(cX1d.h, cX1d.v)
    return (b.finish() if standalone else b)



def build_launch_C4(b=None):
    standalone = b is None
    if standalone:
        b = Builder("C4")
    nc = b.nc
    x4T_d = b.din("x4T", [D, T], F32)
    z4_d = b.din("z4", [D, T], BF16)
    wqT_d = b.din("WqT", [D, D], F32)
    wkT_d = b.din("WkT", [D, D], F32)
    wvT_d = b.din("WvT", [D, D], F32)
    woT_d = b.din("WoT", [D, D], F32)
    wd_col_d = b.din("wd_col", [128, 8], F32)
    bd_d = b.din("bd", [1, 1], F32)
    oscale_d = b.din("oscale", [1, 1], F32)
    dmaster_d = b.din("dmaster", [128, 4480], F32)
    dmask_d = b.din("dmask", [128, 896], BF16)
    ones_b_d = b.din("ones_b", [128, 128], BF16)
    ident_f_d = b.din("ident_f", [128, 128], F32)
    x5T_d = b.dout("x5T", [D, T], F32)
    z5_d = b.dout("z5", [D, T], BF16)
    qT_d = b.dint("qT_scratch", [D, T], BF16)
    kT_d = b.dint("kT_scratch", [D, T], BF16)
    vtm_d = b.dint("vtm_scratch", [T, D], BF16)
    ret_d = b.dint("ret_scratch", [D, T], BF16)

    bigz = b.sb("bigz", [128, 32768], BF16)
    sw = b.sb("sw", [128, 32 * 512], BF16)
    xs = b.sb("xs", [128, 2 * T], F32)
    rstage = b.sb("rstage", [128, 2 * 4096], BF16)
    dmaster = b.sb("dmaster_s", [128, 4480], F32)
    dmask = b.sb("dmask_s", [128, 896], BF16)
    wpr = b.sb("wpr", [128, 2 * DT8 * 128], BF16)
    wdc = b.sb("wdc", [128, 8], BF16)
    vt2 = b.sb("vt2", [128, 2 * 512], F32)
    wtile = b.sb("wtile", [128, 2 * 512], BF16)
    evr = b.sb("evr", [128, 2 * 512], BF16)
    dbr2 = b.sb("dbr2", [1, 2 * 512], F32)
    ldc = b.sb("ldc", [128, 32], F32)
    os128 = b.sb("os128", [128, 1], F32)
    bd1 = b.sb("bd1", [1, 1], F32)
    osc1 = b.sb("osc1", [1, 1], F32)
    onesf = b.sb("onesf", [1, 128], F32)
    ones_bt = b.sb("ones_bt", [128, 128], BF16)
    ident_f = b.sb("ident_f_s", [128, 128], F32)
    fc = b.sb("fc", [128, 6], F32)
    x4s = b.sb("x4s", [128, 2 * 512], F32)
    x5s = b.sb("x5s", [128, 2 * 512], F32)
    xb0 = b.sb("xb0", [128, T // 2], BF16)
    xsq = b.sb("xsq", [128, T // 2], BF16)
    Mt = b.sb("Mt", [128, 512], F32)
    Rt = b.sb("Rt", [128, 512], F32)
    stat_sc = b.sb("stat_sc", [128, 512], F32)
    zsc = b.sb("zsc", [128, 512], F32)
    evz = b.sb("evz", [128, 2 * 512], BF16)

    psA = b.ps("psA", [128, 1024], F32)
    psB = b.ps("psB", [128, 1024], F32)
    psO = b.ps("psO", [128, 1024], F32)
    psP = b.ps("psP", [128, 1024], F32)

    cLd = b.ctr("cLd")
    cZ = b.ctr("cZ")
    cWp = [b.ctr("cWp0"), b.ctr("cWp1")]
    cQd = b.ctr("cQd")
    cVtd = b.ctr("cVtd")
    cKs = b.ctr("cKs")
    cVs = b.ctr("cVs")
    cQc = [b.ctr("cQc0"), b.ctr("cQc1")]
    cRd = b.ctr("cRd")
    cRs = [b.ctr("cRs0"), b.ctr("cRs1")]
    cWo = b.ctr("cWo")
    cX0t = [b.ctr("cX0t0"), b.ctr("cX0t1")]
    cX1d = b.ctr("cX1d")
    cXs = [b.ctr("cXs0"), b.ctr("cXs1")]
    cOut = b.ctr("cOut")
    cPE = b.ctr("cPE")
    cACT = b.ctr("cACT")
    cDVE = b.ctr("cDVE")
    pe, dv, ac, dq = {}, {}, {}, {}

    block = b.get_block()

    @block.sync
    def _(sy):
        for ap_s, ap_d in ((ones_bt, ones_b_d), (ident_f, ident_f_d),
                           (dmask, dmask_d), (bd1, bd_d), (osc1, oscale_d)):
            cLd.inc(sy.dma_start(ap_s[:], ap_d[:, :]), 16)
        cLd.inc(sy.dma_start(dmaster[:], dmaster_d[:, :]), 16)
    ld_all = cLd.v

    @block.gpsimd
    def _(g):
        cZ.inc(g.dma_start(wdc[:], wd_col_d[:, :]), 16)
        for j in range(DT8):
            cZ.inc(g.dma_start(bigz[:, j * T:(j + 1) * T], z4_d[j * 128:(j + 1) * 128, :]), 16)
        for j in range(DT8):
            cWp[0].inc(g.dma_start(wpr[:, j * 128:(j + 1) * 128],
                                   wqT_d[j * 128:(j + 1) * 128, 0:128]), 16)
        dq[("wp", 0)] = cWp[0].v
    z_all = cZ.v

    @block.vector
    def _(v):
        v.wait_ge(cLd.h, ld_all)
        v.memset(onesf[:], 1.0)
        v.memset(fc[:, 0:1], 1e-5)
        v.memset(fc[:, 1:2], -0.5)
        v.memset(fc[:, 4:5], 1e-8)
        inst = v.memset(fc[:, 3:4], 0.5)
        cDVE.inc(inst)
        v.wait_ge(cDVE.h, cDVE.v)
        inst = v.tensor_copy(ones_bt[:], ones_bt[:])
        cDVE.inc(inst)
        dv["consts"] = cDVE.v

    for sl in range(24):
        which = sl // 8
        slot = cWp[sl % 2]

        if sl + 1 < 24:
            @block.gpsimd
            def _(g, sl=sl):
                nsl = sl + 1
                nsrc = (wqT_d, wkT_d, wvT_d)[nsl // 8]
                ncol = (nsl % 8) * 128
                nslot = cWp[nsl % 2]
                if sl >= 1:
                    g.wait_ge(cPE.h, pe[("pj", sl - 1, TC8 - 1)])
                for j in range(DT8):
                    nslot.inc(g.dma_start(
                        wpr[:, (nsl % 2) * DT8 * 128 + j * 128:(nsl % 2) * DT8 * 128 + (j + 1) * 128],
                        nsrc[j * 128:(j + 1) * 128, ncol:ncol + 128]), 16)
                dq[("wp", nsl)] = nslot.v

        for tc in range(TC8):
            psx = psA if (tc % 2 == 0) else psB

            @block.tensor
            def _(t, sl=sl, tc=tc, psx=psx, slot=slot):
                if tc == 0:
                    t.wait_ge(slot.h, dq[("wp", sl)])
                    if sl == 0:
                        t.wait_ge(cZ.h, z_all)
                        t.wait_ge(cDVE.h, dv["consts"])
                if tc >= 2:
                    t.wait_ge(cDVE.h, dv[("pjev", sl, tc - 2)])
                elif sl >= 1:
                    t.wait_ge(cDVE.h, dv[("pjev", sl - 1, TC8 - 2 + tc)])
                for j in range(DT8):
                    inst = t.matmul(psx[:, 0:512],
                                    wpr[:, (sl % 2) * DT8 * 128 + j * 128:(sl % 2) * DT8 * 128 + (j + 1) * 128],
                                    bigz[:, j * T + tc * 512: j * T + tc * 512 + 512],
                                    start=(j == 0), stop=(j == DT8 - 1))
                cPE.inc(inst)
                pe[("pj", sl, tc)] = cPE.v

            if which < 2:
                dst = qT_d if which == 0 else kT_d

                @block.vector
                def _(v, sl=sl, tc=tc, psx=psx):
                    v.wait_ge(cPE.h, pe[("pj", sl, tc)])
                    if ("qkd", sl, tc - 2) in dq:
                        v.wait_ge(cQd.h, dq[("qkd", sl, tc - 2)])
                    elif sl >= 1 and ("qkd", sl - 1, TC8 - 2 + tc) in dq:
                        v.wait_ge(cQd.h, dq[("qkd", sl - 1, TC8 - 2 + tc)])
                    inst = v.tensor_copy(evr[:, (tc % 2) * 512:(tc % 2) * 512 + 512], psx[:, 0:512])
                    cDVE.inc(inst)
                    dv[("pjev", sl, tc)] = cDVE.v

                @block.sync
                def _(sy, sl=sl, tc=tc, dst=dst):
                    sy.wait_ge(cDVE.h, dv[("pjev", sl, tc)])
                    cQd.inc(sy.dma_start(dst[(sl % 8) * 128:(sl % 8) * 128 + 128, tc * 512:(tc + 1) * 512],
                                         evr[:, (tc % 2) * 512:(tc % 2) * 512 + 512]), 16)
                    dq[("qkd", sl, tc)] = cQd.v
            else:
                @block.vector
                def _(v, sl=sl, tc=tc, psx=psx):
                    v.wait_ge(cPE.h, pe[("pj", sl, tc)])
                    if tc >= 2:
                        v.wait_ge(cPE.h, pe[("vt", sl, tc - 2)])
                    elif sl > 16:
                        v.wait_ge(cPE.h, pe[("vt", sl - 1, TC8 - 2 + tc)])
                    inst = v.tensor_copy(vt2[:, (tc % 2) * 512:(tc % 2) * 512 + 512], psx[:, 0:512])
                    cDVE.inc(inst)
                    dv[("pjev", sl, tc)] = cDVE.v

                @block.tensor
                def _(t, sl=sl, tc=tc):
                    t.wait_ge(cDVE.h, dv[("pjev", sl, tc)])
                    if tc >= 2:
                        t.wait_ge(cDVE.h, dv[("vtev", sl, tc - 2)])
                    elif sl > 16:
                        t.wait_ge(cDVE.h, dv[("vtev", sl - 1, TC8 - 2 + tc)])
                    for k4 in range(4):
                        inst = t.transpose(psO[:, (tc % 2) * 512 + k4 * 128:(tc % 2) * 512 + (k4 + 1) * 128],
                                           vt2[:, (tc % 2) * 512 + k4 * 128:(tc % 2) * 512 + (k4 + 1) * 128],
                                           ident_f[:])
                    cPE.inc(inst)
                    pe[("vt", sl, tc)] = cPE.v

                @block.vector
                def _(v, sl=sl, tc=tc):
                    v.wait_ge(cPE.h, pe[("vt", sl, tc)])
                    if ("vtd", sl, tc - 2) in dq:
                        v.wait_ge(cVtd.h, dq[("vtd", sl, tc - 2)])
                    elif sl > 16 and ("vtd", sl - 1, TC8 - 2 + tc) in dq:
                        v.wait_ge(cVtd.h, dq[("vtd", sl - 1, TC8 - 2 + tc)])
                    elif sl == 16 and tc < 2:
                        v.wait_ge(cQd.h, dq[("qkd", 15, TC8 - 2 + tc)])
                    inst = v.tensor_copy(evr[:, (tc % 2) * 512:(tc % 2) * 512 + 512],
                                         psO[:, (tc % 2) * 512:(tc % 2) * 512 + 512])
                    cDVE.inc(inst)
                    dv[("vtev", sl, tc)] = cDVE.v

                @block.sync
                def _(sy, sl=sl, tc=tc):
                    sy.wait_ge(cDVE.h, dv[("vtev", sl, tc)])
                    ecol = (sl % 8) * 128
                    for k4 in range(4):
                        s0 = tc * 512 + k4 * 128
                        cVtd.inc(sy.dma_start(vtm_d[s0:s0 + 128, ecol:ecol + 128],
                                              evr[:, (tc % 2) * 512 + k4 * 128:(tc % 2) * 512 + (k4 + 1) * 128]), 16)
                    dq[("vtd", sl, tc)] = cVtd.v

    for tc in range(TC8):
        @block.tensor
        def _(t, tc=tc):
            if tc == 0:
                t.wait_ge(cDVE.h, dv[("pjev", 23, TC8 - 1)])
            if tc >= 2:
                t.wait_ge(cACT.h, ac[("ldr", tc - 2)])
            for j in range(DT8):
                inst = t.matmul(psB[0:1, (tc % 2) * 512:(tc % 2) * 512 + 512],
                                wdc[:, j:j + 1],
                                bigz[:, j * T + tc * 512: j * T + tc * 512 + 512],
                                start=(j == 0), stop=(j == DT8 - 1))
            cPE.inc(inst)
            pe[("d", tc)] = cPE.v

        @block.vector
        def _(v, tc=tc):
            v.wait_ge(cPE.h, pe[("d", tc)])
            inst = v.tensor_scalar(dbr2[0:1, (tc % 2) * 512:(tc % 2) * 512 + 512],
                                   psB[0:1, (tc % 2) * 512:(tc % 2) * 512 + 512],
                                   bd1[0:1, 0:1], None, ALU.add)
            cDVE.inc(inst)
            dv[("dev", tc)] = cDVE.v

        @block.scalar
        def _(s, tc=tc):
            s.wait_ge(cDVE.h, dv[("dev", tc)])
            rr = (tc % 2) * 512
            s.activation(dbr2[0:1, rr:rr + 512], dbr2[0:1, rr:rr + 512], AF.Sigmoid)
            inst = s.activation(dbr2[0:1, rr:rr + 512], dbr2[0:1, rr:rr + 512], AF.Ln,
                                bias=fc[0:1, 4:5])
            cACT.inc(inst)
            ac[("ldr", tc)] = cACT.v

        for m in range(4):
            st = tc * 4 + m

            @block.tensor
            def _(t, tc=tc, m=m, st=st):
                if m == 0:
                    t.wait_ge(cACT.h, ac[("ldr", tc)])
                if ("ldc", st - 2) in dv:
                    t.wait_ge(cDVE.h, dv[("ldc", st - 2)])
                inst = t.transpose(psA[0:128, (st % 2) * 512:(st % 2) * 512 + 1],
                                   dbr2[0:1, (tc % 2) * 512 + m * 128:(tc % 2) * 512 + (m + 1) * 128]
                                   .bitcast(F32) if False else
                                   dbr2[0:1, (tc % 2) * 512 + m * 128:(tc % 2) * 512 + (m + 1) * 128],
                                   ident_f[0:1, 0:1])
                cPE.inc(inst)
                pe[("ldc", st)] = cPE.v

            @block.vector
            def _(v, st=st):
                v.wait_ge(cPE.h, pe[("ldc", st)])
                inst = v.tensor_copy(ldc[:, st:st + 1], psA[:, (st % 2) * 512:(st % 2) * 512 + 1])
                cDVE.inc(inst)
                dv[("ldc", st)] = cDVE.v

    @block.tensor
    def _(t):
        t.wait_ge(cDVE.h, dv[("ldc", 31)])
        inst = t.matmul(psB[:, 0:1], onesf[0:1, :], osc1[0:1, 0:1], start=True, stop=True)
        cPE.inc(inst)
        pe["osb"] = cPE.v

    @block.vector
    def _(v):
        v.wait_ge(cPE.h, pe["osb"])
        inst = v.tensor_copy(os128[:], psB[:, 0:1])
        cDVE.inc(inst)
        dv["os"] = cDVE.v

    NST = {tc: 32 - 4 * tc for tc in range(TC8)}
    for tc in range(TC8):
        t0 = tc * 512
        sts = list(range(4 * tc, 32))
        nst = len(sts)
        qslot = cQc[tc % 2]

        @block.gpsimd
        def _(g, tc=tc, sts=sts):
            if tc == 0:
                g.wait_ge(cQd.h, cQd.v)
                g.wait_ge(cVtd.h, cVtd.v)
                g.wait_ge(cPE.h, pe["osb"])
            else:
                g.wait_ge(cPE.h, pe[("ret", tc - 1, DT8 - 1)])
            for ii, st in enumerate(sts):
                for j in range(DT8):
                    cKs.inc(g.dma_start(bigz[:, (ii * 8 + j) * 128:(ii * 8 + j + 1) * 128],
                                        kT_d[j * 128:(j + 1) * 128, st * 128:(st + 1) * 128]), 16)
            dq[("ks", tc)] = cKs.v

        @block.sync
        def _(sy, tc=tc, t0=t0, qslot=qslot):
            if tc == 0:
                sy.wait_ge(cQd.h, cQd.v)
                sy.wait_ge(cDVE.h, dv["os"])
            if tc >= 2:
                sy.wait_ge(cPE.h, pe[("s", tc - 2, NST[tc - 2] - 1)])
            for j in range(DT8):
                qslot.inc(sy.dma_start(rstage[:, (tc % 2) * 4096 + j * 512:(tc % 2) * 4096 + (j + 1) * 512],
                                       qT_d[j * 128:(j + 1) * 128, t0:t0 + 512]), 16)
            dq[("qc", tc)] = qslot.v

        for ii, st in enumerate(sts):
            pss = psA if (ii % 2 == 0) else psB
            diag = st < 4 * (tc + 1)

            @block.tensor
            def _(t, tc=tc, st=st, ii=ii, pss=pss, qslot=qslot):
                if ii == 0:
                    t.wait_ge(qslot.h, dq[("qc", tc)])
                    t.wait_ge(cKs.h, dq[("ks", tc)])
                if ii >= 2:
                    t.wait_ge(cDVE.h, dv[("sw", tc, ii - 2)])
                elif tc >= 1:
                    t.wait_ge(cDVE.h, dv[("sw", tc - 1, NST[tc - 1] - 2 + ii)])
                for j in range(DT8):
                    inst = t.matmul(pss[:, 0:512],
                                    bigz[:, (ii * 8 + j) * 128:(ii * 8 + j + 1) * 128],
                                    rstage[:, (tc % 2) * 4096 + j * 512:(tc % 2) * 4096 + (j + 1) * 512],
                                    start=(j == 0), stop=(j == DT8 - 1))
                cPE.inc(inst)
                pe[("s", tc, ii)] = cPE.v

            @block.scalar
            def _(s, tc=tc, st=st, ii=ii):
                if ii >= 2:
                    s.wait_ge(cDVE.h, dv[("sw", tc, ii - 2)])
                elif tc >= 1:
                    s.wait_ge(cDVE.h, dv[("sw", tc - 1, NST[tc - 1] - 2 + ii)])
                delta = st * 128 - tc * 512 - 1
                ofs = 3967 - delta
                inst = s.activation(wtile[:, (ii % 2) * 512:(ii % 2) * 512 + 512],
                                    dmaster[:, ofs:ofs + 512], AF.Exp, scale=ldc[:, st:st + 1])
                cACT.inc(inst)
                ac[("wt", tc, ii)] = cACT.v

            @block.vector
            def _(v, tc=tc, st=st, ii=ii, pss=pss, diag=diag):
                v.wait_ge(cPE.h, pe[("s", tc, ii)])
                v.wait_ge(cACT.h, ac[("wt", tc, ii)])
                if ii < 2 and tc >= 1:
                    v.wait_ge(cPE.h, pe[("ret", tc - 1, DT8 - 1)])
                if diag:
                    dpos = st * 128 - tc * 512
                    mofs = 384 - dpos
                    v.tensor_tensor(wtile[:, (ii % 2) * 512:(ii % 2) * 512 + 512],
                                    wtile[:, (ii % 2) * 512:(ii % 2) * 512 + 512],
                                    dmask[:, mofs:mofs + 512], ALU.mult)
                inst = v.scalar_tensor_tensor(sw[:, st * 512:(st + 1) * 512], pss[:, 0:512],
                                              1.0 / 32.0, wtile[:, (ii % 2) * 512:(ii % 2) * 512 + 512],
                                              ALU.mult, ALU.mult)
                cDVE.inc(inst)
                dv[("sw", tc, ii)] = cDVE.v

        @block.gpsimd
        def _(g, tc=tc, sts=sts, nst=nst):
            g.wait_ge(cPE.h, pe[("s", tc, nst - 1)])
            for ii, st in enumerate(sts):
                g.dma_start(bigz[:, ii * 1024:(ii + 1) * 1024],
                            vtm_d[st * 128:(st + 1) * 128, :]).then_inc(cVs.h, 16)
                cVs.v += 16
            dq[("vs", tc)] = cVs.v

        for et in range(DT8):
            pso = psO if (et % 2 == 0) else psP

            @block.tensor
            def _(t, tc=tc, et=et, pso=pso, sts=sts, nst=nst):
                if et == 0:
                    t.wait_ge(cVs.h, dq[("vs", tc)])
                    t.wait_ge(cDVE.h, dv[("sw", tc, nst - 1)])
                if et >= 2:
                    t.wait_ge(cDVE.h, dv[("retev", tc, et - 2)])
                elif tc >= 1:
                    t.wait_ge(cDVE.h, dv[("retev", tc - 1, DT8 - 2 + et)])
                elif tc == 0 and et < 2:
                    t.wait_ge(cDVE.h, dv[("vtev", 23, TC8 - 1)])
                for ii, st in enumerate(sts):
                    inst = t.matmul(pso[:, 0:512],
                                    bigz[:, ii * 1024 + et * 128: ii * 1024 + et * 128 + 128],
                                    sw[:, st * 512:(st + 1) * 512],
                                    start=(ii == 0), stop=(ii == nst - 1))
                cPE.inc(inst)
                pe[("ret", tc, et)] = cPE.v

            @block.vector
            def _(v, tc=tc, et=et, pso=pso):
                v.wait_ge(cPE.h, pe[("ret", tc, et)])
                if ("retd", tc, et - 2) in dq:
                    v.wait_ge(cRd.h, dq[("retd", tc, et - 2)])
                elif tc >= 1 and ("retd", tc - 1, DT8 - 2 + et) in dq:
                    v.wait_ge(cRd.h, dq[("retd", tc - 1, DT8 - 2 + et)])
                inst = v.tensor_copy(evr[:, (et % 2) * 512:(et % 2) * 512 + 512], pso[:, 0:512])
                cDVE.inc(inst)
                dv[("retev", tc, et)] = cDVE.v

            @block.sync
            def _(sy, tc=tc, et=et, t0=t0):
                sy.wait_ge(cDVE.h, dv[("retev", tc, et)])
                cRd.inc(sy.dma_start(ret_d[et * 128:(et + 1) * 128, t0:t0 + 512],
                                     evr[:, (et % 2) * 512:(et % 2) * 512 + 512]), 16)
                dq[("retd", tc, et)] = cRd.v

    @block.gpsimd
    def _(g):
        g.wait_ge(cPE.h, pe[("ret", TC8 - 1, DT8 - 1)])
        for et in range(DT8):
            for j in range(DT8):
                cWo.inc(g.dma_start(bigz[:, (et * DT8 + j) * 128:(et * DT8 + j + 1) * 128],
                                    woT_d[j * 128:(j + 1) * 128, et * 128:(et + 1) * 128]), 16)
    wo_all = cWo.v

    for tc in range(TC8):
        t0 = tc * 512
        rslot = cRs[tc % 2]

        @block.sync
        def _(sy, tc=tc, t0=t0, rslot=rslot):
            if tc == 0:
                sy.wait_ge(cRd.h, cRd.v)
                sy.wait_ge(cPE.h, pe[("ret", TC8 - 1, DT8 - 1)])
            if tc >= 2:
                sy.wait_ge(cPE.h, pe[("wo", tc - 2, DT8 - 1)])
            for j in range(DT8):
                rslot.inc(sy.dma_start(rstage[:, (tc % 2) * 4096 + j * 512:(tc % 2) * 4096 + (j + 1) * 512],
                                       ret_d[j * 128:(j + 1) * 128, t0:t0 + 512]), 16)
            dq[("rs", tc)] = rslot.v

        for et in range(DT8):
            psx = psA if ((tc * DT8 + et) % 2 == 0) else psB
            xslot = cX0t[et % 2]

            @block.sync
            def _(sy, tc=tc, t0=t0, et=et, xslot=xslot):
                if tc >= 1 or et >= 2:
                    prevk = ("x5", tc, et - 2) if et >= 2 else ("x5", tc - 1, DT8 - 2 + et)
                    sy.wait_ge(cDVE.h, dv[prevk])
                xslot.inc(sy.dma_start(x4s[:, (et % 2) * 512:(et % 2) * 512 + 512],
                                       x4T_d[et * 128:(et + 1) * 128, t0:t0 + 512]), 16)
                dq[("x0t", tc, et)] = xslot.v

            @block.tensor
            def _(t, tc=tc, et=et, psx=psx, rslot=rslot):
                if et == 0:
                    t.wait_ge(rslot.h, dq[("rs", tc)])
                    if tc == 0:
                        t.wait_ge(cWo.h, wo_all)
                key = ("x5", tc, et - 2) if et >= 2 else (("x5", tc - 1, DT8 - 2 + et) if tc >= 1 else None)
                if key is not None:
                    t.wait_ge(cDVE.h, dv[key])
                elif tc == 0 and et < 2:
                    t.wait_ge(cDVE.h, dv[("sw", TC8 - 1, NST[TC8 - 1] - 1)])
                for j in range(DT8):
                    inst = t.matmul(psx[:, 0:512],
                                    bigz[:, (et * DT8 + j) * 128:(et * DT8 + j + 1) * 128],
                                    rstage[:, (tc % 2) * 4096 + j * 512:(tc % 2) * 4096 + (j + 1) * 512],
                                    start=(j == 0), stop=(j == DT8 - 1))
                cPE.inc(inst)
                pe[("wo", tc, et)] = cPE.v

            @block.vector
            def _(v, tc=tc, et=et, psx=psx, xslot=xslot):
                v.wait_ge(cPE.h, pe[("wo", tc, et)])
                v.wait_ge(xslot.h, dq[("x0t", tc, et)])
                v.tensor_scalar(x5s[:, (et % 2) * 512:(et % 2) * 512 + 512], psx[:, 0:512],
                                os128[:, 0:1], None, ALU.mult)
                inst = v.tensor_tensor(x5s[:, (et % 2) * 512:(et % 2) * 512 + 512],
                                       x5s[:, (et % 2) * 512:(et % 2) * 512 + 512],
                                       x4s[:, (et % 2) * 512:(et % 2) * 512 + 512], ALU.add)
                cDVE.inc(inst)
                dv[("x5", tc, et)] = cDVE.v

            @block.sync
            def _(sy, tc=tc, et=et, t0=t0):
                sy.wait_ge(cDVE.h, dv[("x5", tc, et)])
                cX1d.inc(sy.dma_start(x5T_d[et * 128:(et + 1) * 128, t0:t0 + 512],
                                      x5s[:, (et % 2) * 512:(et % 2) * 512 + 512]), 16)
                dq[("x5d", tc, et)] = cX1d.v

    E = LnEnv(block, xs, xb0, xsq, Mt, Rt, stat_sc, zsc, evz, ones_bt, fc, psA,
              cXs, cOut, cPE, cACT, cDVE, pe, dv, ac, dq)
    emit_ln_generic(E, "ln5", x5T_d, z5_d, cX1d, cX1d.v)

    @block.sync
    def _(sy):
        sy.wait_ge(cOut.h, cOut.v)
        sy.wait_ge(cX1d.h, cX1d.v)
    return (b.finish() if standalone else b)



FF = 2816
FT = FF // 128
NCH = [(0, 512), (512, 512), (1024, 512), (1536, 512), (2048, 512), (2560, 256)]
MAGIC = 12582912.0


def build_launch_C5(b=None):
    standalone = b is None
    if standalone:
        b = Builder("C5")
    nc = b.nc
    x5T_d = b.din("x5T", [D, T], F32)
    wg_d = b.din("wg", [FF, D], F32)
    wu_d = b.din("wu", [FF, D], F32)
    wdn_d = b.din("wdn", [D, FF], F32)
    ones_b_d = b.din("ones_b", [128, 128], BF16)
    ident_f_d = b.din("ident_f", [128, 128], F32)
    identb_d = b.din("identb", [128, 128], BF16)
    x6T_d = b.dout("x6T", [D, T], F32)
    z6_d = b.dout("z6", [D, T], BF16)
    zq_d = b.dint("zq_scratch", [D, T], BF16)
    hq_d = b.dint("hq_scratch", [FF, T], BF16)

    wgq = b.sb("wgq", [128, 8 * FF], BF16)
    wuq = b.sb("wuq", [128, 8 * FF], BF16)
    wdq = b.sb("wdq", [128, FT * 1024], BF16)
    zqc = b.sb("zqc", [128, 2 * 4096], BF16)
    scr = b.sb("scr", [128, 8192], F32)
    x5tm = b.sb("x5tm", [128, 1024], F32)
    ztm = b.sb("ztm", [128, 1024], F32)
    zqtm = b.sb("zqtm", [128, 1024], BF16)
    wqt = b.sb("wqt", [128, FF], BF16)
    zqs = b.sb("zqs", [128, 1024], BF16)
    hqb = b.sb("hqb", [128, 512], BF16)
    hqs = b.sb("hqs", [128, 2 * 512], BF16)
    bms = b.sb("bms", [128, 32], F32)
    scs = b.sb("scs", [128, 32], F32)
    rss = b.sb("rss", [128, 32], F32)
    mtm = b.sb("mtm", [128, 8], F32)
    x5s = b.sb("x5s", [128, 2 * 512], F32)
    ones_bt = b.sb("ones_bt", [128, 128], BF16)
    ident_f = b.sb("ident_f_s", [128, 128], F32)
    identb = b.sb("identb_s", [128, 128], BF16)
    fc = b.sb("fc", [128, 6], F32)

    psA = b.ps("psA", [128, 1024], F32)
    psB = b.ps("psB", [128, 1024], F32)
    psO = b.ps("psO", [128, 1024], F32)
    psT = b.ps("psT", [128, 2048], BF16)

    cLd = b.ctr("cLd")
    cWt = b.ctr("cWt")
    cXz = [b.ctr("cXz0"), b.ctr("cXz1")]
    cZq = b.ctr("cZq")
    cZc = [b.ctr("cZc0"), b.ctr("cZc1")]
    cHd = b.ctr("cHd")
    cHs = [b.ctr("cHs0"), b.ctr("cHs1")]
    cX0t = [b.ctr("cX0t0"), b.ctr("cX0t1")]
    cX1d = b.ctr("cX1d")
    cXs = [b.ctr("cXs0"), b.ctr("cXs1")]
    cOut = b.ctr("cOut")
    cPE = b.ctr("cPE")
    cACT = b.ctr("cACT")
    cDVE = b.ctr("cDVE")
    pe, dv, ac, dq = {}, {}, {}, {}

    block = b.get_block()

    TQ = 4096
    XN, A_, R_, S0, S1, S2, SG = (TQ, TQ + 512, TQ + 1024, TQ + 1536,
                                  TQ + 2048, TQ + 2560, TQ + 3072)

    @block.sync
    def _(sy):
        for ap_s, ap_d in ((ones_bt, ones_b_d), (ident_f, ident_f_d),
                           (identb, identb_d)):
            cLd.inc(sy.dma_start(ap_s[:], ap_d[:, :]), 16)
    ld_all = cLd.v

    @block.vector
    def _(v):
        v.wait_ge(cLd.h, ld_all)
        v.memset(fc[:, 0:1], 1e-5)
        v.memset(fc[:, 1:2], -0.5)
        v.memset(fc[:, 2:3], 4.0)
        v.memset(fc[:, 3:4], -2.5)
        v.memset(fc[:, 4:5], -3.5)
        inst = v.memset(fc[:, 5:6], -5.0)
        cDVE.inc(inst)
        v.wait_ge(cDVE.h, cDVE.v)
        inst = v.tensor_copy(ones_bt[:], ones_bt[:])
        cDVE.inc(inst)
        dv["consts"] = cDVE.v

    state = {"last_ac": None, "last_dv": None}

    def emit_fp4(key, src, dst, W, first_waits=()):
        nb = W // 16

        @block.vector
        def _(v, key=key, src=src, W=W, nb=nb, first_waits=tuple(first_waits),
              prev_ac=state["last_ac"]):
            for ctr, val in first_waits:
                v.wait_ge(ctr.h, val)
            if prev_ac is not None:
                v.wait_ge(cACT.h, prev_ac)
            sv = src.rearrange("p (k c) -> p k c", c=16)
            inst = v.tensor_reduce(bms[:, 0:nb], sv, axis=AX.X, op=ALU.max,
                                   apply_absolute_value=True)
            cDVE.inc(inst)
            v.wait_ge(cDVE.h, cDVE.v)
            inst = v.tensor_scalar(scs[:, 0:nb], bms[:, 0:nb], 1.0 / 6.0, None, ALU.mult)
            cDVE.inc(inst)
            v.wait_ge(cDVE.h, cDVE.v)
            inst = v.tensor_scalar(scs[:, 0:nb], scs[:, 0:nb], 1e-12, None, ALU.max)
            cDVE.inc(inst)
            v.wait_ge(cDVE.h, cDVE.v)
            inst = v.reciprocal(rss[:, 0:nb], scs[:, 0:nb])
            cDVE.inc(inst)
            v.wait_ge(cDVE.h, cDVE.v)
            rsb = rss[:, 0:nb].unsqueeze(2).broadcast_to([128, nb, 16])
            inst = v.tensor_tensor(scr[:, XN:XN + W].rearrange("p (k c) -> p k c", c=16),
                                   sv, rsb, ALU.mult)
            cDVE.inc(inst)
            dv[("fp4xn",) + key] = cDVE.v

        @block.scalar
        def _(s, key=key, W=W, prev_dv=state["last_dv"]):
            s.wait_ge(cDVE.h, dv[("fp4xn",) + key])
            if prev_dv is not None:
                s.wait_ge(cDVE.h, prev_dv)
            s.activation(scr[:, A_:A_ + W], scr[:, XN:XN + W], AF.Abs)
            s.activation(scr[:, R_:R_ + W], scr[:, A_:A_ + W], AF.Relu,
                         scale=-2.0, bias=fc[:, 2:3])
            s.activation(scr[:, S0:S0 + W], scr[:, A_:A_ + W], AF.Sign, bias=fc[:, 3:4])
            s.activation(scr[:, S1:S1 + W], scr[:, A_:A_ + W], AF.Sign, bias=fc[:, 4:5])
            s.activation(scr[:, S2:S2 + W], scr[:, A_:A_ + W], AF.Sign, bias=fc[:, 5:6])
            inst = s.activation(scr[:, SG:SG + W], scr[:, XN:XN + W], AF.Sign)
            cACT.inc(inst)
            ac[("fp4",) + key] = cACT.v

        @block.vector
        def _(v, key=key, dst=dst, W=W, nb=nb):
            v.wait_ge(cACT.h, ac[("fp4",) + key])
            r = scr[:, R_:R_ + W]
            v.tensor_scalar(r, r, -1.0, MAGIC, ALU.mult, ALU.add)
            v.tensor_scalar(r, r, 0.5, -(MAGIC * 0.5) + 4.0, ALU.mult, ALU.add)
            v.scalar_tensor_tensor(r, scr[:, S0:S0 + W], 0.5, r, ALU.mult, ALU.add)
            v.scalar_tensor_tensor(r, scr[:, S1:S1 + W], 0.5, r, ALU.mult, ALU.add)
            v.scalar_tensor_tensor(r, scr[:, S2:S2 + W], 1.0, r, ALU.mult, ALU.add)
            v.tensor_tensor(r, r, scr[:, SG:SG + W], ALU.mult)
            scb = scs[:, 0:nb].unsqueeze(2).broadcast_to([128, nb, 16])
            inst = v.tensor_tensor(dst.rearrange("p (k c) -> p k c", c=16),
                                   r.rearrange("p (k c) -> p k c", c=16), scb, ALU.mult)
            cDVE.inc(inst)
            dv[("fp4",) + key] = cDVE.v

        state["last_ac"] = ac[("fp4",) + key]
        state["last_dv"] = dv[("fp4",) + key]
        return dv[("fp4",) + key]

    for wsel, (src_d, dstbuf) in enumerate(((wg_d, wgq), (wu_d, wuq))):
        for rt in range(FT):
            fi = wsel * FT + rt
            sl = fi % 2

            @block.sync
            def _(sy, wsel=wsel, src_d=src_d, rt=rt, fi=fi, sl=sl):
                if fi < 2:
                    sy.wait_ge(cLd.h, ld_all)
                else:
                    pw, pr_ = divmod(fi - 2, FT)
                    sy.wait_ge(cDVE.h, dv[("wq", "done", pw, pr_)])
                cWt.inc(sy.dma_start(scr[:, sl * 1024: sl * 1024 + 1024],
                                     src_d[rt * 128:(rt + 1) * 128, :]), 16)
                dq[("wt", wsel, rt)] = cWt.v

            for half in range(2):
                emit_fp4(("w", wsel, rt, half),
                         scr[:, sl * 1024 + half * 512: sl * 1024 + half * 512 + 512],
                         wqt[:, sl * 1024 + half * 512: sl * 1024 + half * 512 + 512], 512,
                         first_waits=((cWt, dq[("wt", wsel, rt)]),) if half == 0 else ())

            @block.tensor
            def _(t, wsel=wsel, rt=rt, fi=fi, sl=sl):
                t.wait_ge(cDVE.h, dv[("fp4", "w", wsel, rt, 1)])
                if fi >= 2:
                    pw, pr_ = divmod(fi - 2, FT)
                    t.wait_ge(cDVE.h, dv[("wq", "ev", pw, pr_)])
                for dch in range(8):
                    inst = t.transpose(psT[:, sl * 1024 + dch * 128: sl * 1024 + (dch + 1) * 128],
                                       wqt[:, sl * 1024 + dch * 128: sl * 1024 + (dch + 1) * 128],
                                       identb[:])
                cPE.inc(inst)
                pe[("wq", wsel, rt)] = cPE.v

            @block.vector
            def _(v, wsel=wsel, rt=rt, dstbuf=dstbuf, sl=sl):
                v.wait_ge(cPE.h, pe[("wq", wsel, rt)])
                for dch in range(8):
                    inst = v.tensor_copy(dstbuf[:, dch * FF + rt * 128: dch * FF + (rt + 1) * 128],
                                         psT[:, sl * 1024 + dch * 128: sl * 1024 + (dch + 1) * 128])
                cDVE.inc(inst)
                dv[("wq", "ev", wsel, rt)] = cDVE.v
                dv[("wq", "done", wsel, rt)] = cDVE.v
    dv[("wgq", "done")] = dv[("wq", "done", 0, FT - 1)]
    dv[("wuq", "done")] = dv[("wq", "done", 1, FT - 1)]

    for rt in range(DT8):
        @block.sync
        def _(sy, rt=rt):
            sy.wait_ge(cDVE.h, dv[("wq", "done", 1, FT - 1) if rt == 0
                                  else ("wd", "done", rt - 1)])
            for cc in range(2):
                cWt.inc(sy.dma_start(scr[:, cc * 1408:(cc + 1) * 1408],
                                     wdn_d[rt * 128:(rt + 1) * 128, cc * 1408:(cc + 1) * 1408]), 16)
            dq[("wdt", rt)] = cWt.v

        for si, (n0, NW) in enumerate(NCH):
            emit_fp4(("wd", rt, si), scr[:, n0:n0 + NW], wqt[:, n0:n0 + NW], NW,
                     first_waits=((cWt, dq[("wdt", rt)]),) if si == 0 else ())

        for grp in range(2):
            lo = grp * 16
            hi = min(FT, lo + 16)

            @block.tensor
            def _(t, rt=rt, grp=grp, lo=lo, hi=hi):
                if grp == 0:
                    t.wait_ge(cDVE.h, dv[("fp4", "wd", rt, len(NCH) - 1)])
                    if ("wd", "ev", rt - 1) in dv:
                        t.wait_ge(cDVE.h, dv[("wd", "ev", rt - 1)])
                else:
                    t.wait_ge(cDVE.h, dv[("wd", "evg", rt, 0)])
                for ft in range(lo, hi):
                    inst = t.transpose(psT[:, (ft % 16) * 128:(ft % 16) * 128 + 128],
                                       wqt[:, ft * 128:(ft + 1) * 128], identb[:])
                cPE.inc(inst)
                pe[("wd", rt, grp)] = cPE.v

            @block.vector
            def _(v, rt=rt, grp=grp, lo=lo, hi=hi):
                v.wait_ge(cPE.h, pe[("wd", rt, grp)])
                for ft in range(lo, hi):
                    inst = v.tensor_copy(wdq[:, ft * 1024 + rt * 128: ft * 1024 + (rt + 1) * 128],
                                         psT[:, (ft % 16) * 128:(ft % 16) * 128 + 128])
                cDVE.inc(inst)
                dv[("wd", "evg", rt, grp)] = cDVE.v
        dv[("wd", "ev", rt)] = dv[("wd", "evg", rt, 1)]
        dv[("wd", "done", rt)] = dv[("wd", "evg", rt, 1)]
    dv[("wdq", "done")] = dv[("wd", "done", DT8 - 1)]
    pe[("wq", "alltr")] = cPE.v

    for tkc in range(TC8):
        slot = cXz[tkc % 2]

        @block.sync
        def _(sy, tkc=tkc, slot=slot):
            if tkc == 0:
                sy.wait_ge(cDVE.h, dv[("wdq", "done")])
                sy.wait_ge(cACT.h, state["last_ac"])
            if tkc >= 2:
                sy.wait_ge(cPE.h, pe[("ztr", tkc - 2, 3)])
            for dch in range(8):
                slot.inc(sy.dma_start(scr[:, dch * 512:(dch + 1) * 512],
                                      x5T_d[dch * 128:(dch + 1) * 128,
                                            tkc * 512:(tkc + 1) * 512]), 16)
            dq[("xz", tkc)] = slot.v

        for tt4 in range(4):
            zt = tkc * 4 + tt4

            @block.tensor
            def _(t, tkc=tkc, tt4=tt4, zt=zt, slot=slot):
                if tt4 == 0:
                    t.wait_ge(slot.h, dq[("xz", tkc)])
                if ("x5tmev", zt - 1) in dv:
                    t.wait_ge(cDVE.h, dv[("x5tmev", zt - 1)])
                for dch in range(8):
                    inst = t.transpose(psO[:, dch * 128:(dch + 1) * 128],
                                       scr[:, dch * 512 + tt4 * 128: dch * 512 + tt4 * 128 + 128],
                                       ident_f[:])
                cPE.inc(inst)
                pe[("ztr", tkc, tt4)] = cPE.v

            @block.vector
            def _(v, zt=zt, tkc=tkc, tt4=tt4):
                v.wait_ge(cPE.h, pe[("ztr", tkc, tt4)])
                if ("zqd", zt - 1) in dq:
                    v.wait_ge(cZq.h, dq[("zqd", zt - 1)])
                inst = v.tensor_copy(x5tm[:], psO[:, 0:1024])
                cDVE.inc(inst)
                dv[("x5tmev", zt)] = cDVE.v
                v.wait_ge(cDVE.h, cDVE.v)
                inst = v.tensor_reduce(mtm[:, 0:1], x5tm[:], axis=AX.X, op=ALU.add)
                cDVE.inc(inst)
                v.wait_ge(cDVE.h, cDVE.v)
                inst = v.tensor_tensor(ztm[:], x5tm[:], x5tm[:], ALU.mult)
                cDVE.inc(inst)
                v.wait_ge(cDVE.h, cDVE.v)
                inst = v.tensor_reduce(mtm[:, 1:2], ztm[:], axis=AX.X, op=ALU.add)
                cDVE.inc(inst)
                v.wait_ge(cDVE.h, cDVE.v)
                inst = v.tensor_scalar(mtm[:, 2:3], mtm[:, 0:1], 1.0 / D, None, ALU.mult)
                cDVE.inc(inst)
                v.wait_ge(cDVE.h, cDVE.v)
                inst = v.tensor_tensor(mtm[:, 3:4], mtm[:, 2:3], mtm[:, 2:3], ALU.mult)
                cDVE.inc(inst)
                v.wait_ge(cDVE.h, cDVE.v)
                inst = v.scalar_tensor_tensor(mtm[:, 4:5], mtm[:, 1:2], 1.0 / D,
                                              mtm[:, 3:4], ALU.mult, ALU.subtract)
                cDVE.inc(inst)
                dv[("zstat", zt)] = cDVE.v

            @block.scalar
            def _(s, zt=zt):
                s.wait_ge(cDVE.h, dv[("zstat", zt)])
                s.activation(mtm[:, 5:6], mtm[:, 4:5], AF.Ln, bias=fc[:, 0:1])
                inst = s.activation(mtm[:, 6:7], mtm[:, 5:6], AF.Exp, scale=fc[:, 1:2])
                cACT.inc(inst)
                ac[("zr", zt)] = cACT.v

            @block.vector
            def _(v, zt=zt):
                v.wait_ge(cACT.h, ac[("zr", zt)])
                v.tensor_scalar(ztm[:], x5tm[:], mtm[:, 2:3], None, ALU.subtract)
                inst = v.tensor_scalar(ztm[:], ztm[:], mtm[:, 6:7], None, ALU.mult)
                cDVE.inc(inst)
                dv[("ztm", zt)] = cDVE.v

            for half in range(2):
                emit_fp4(("z", zt, half), ztm[:, half * 512:half * 512 + 512],
                         zqtm[:, half * 512:half * 512 + 512], 512,
                         first_waits=((cDVE, dv[("ztm", zt)]),) if half == 0 else ())

            @block.tensor
            def _(t, zt=zt):
                t.wait_ge(cDVE.h, dv[("fp4", "z", zt, 1)])
                if ("zqev", zt - 1) in dv:
                    t.wait_ge(cDVE.h, dv[("zqev", zt - 1)])
                for dch in range(8):
                    inst = t.transpose(psT[:, 1024 + dch * 128:1024 + (dch + 1) * 128],
                                       zqtm[:, dch * 128:(dch + 1) * 128], identb[:])
                cPE.inc(inst)
                pe[("zq", zt)] = cPE.v

            @block.vector
            def _(v, zt=zt):
                v.wait_ge(cPE.h, pe[("zq", zt)])
                if ("zqd", zt - 2) in dq:
                    v.wait_ge(cZq.h, dq[("zqd", zt - 2)])
                inst = v.tensor_copy(zqs[:], psT[:, 1024:2048])
                cDVE.inc(inst)
                dv[("zqev", zt)] = cDVE.v

            @block.sync
            def _(sy, zt=zt, tkc=tkc, tt4=tt4):
                sy.wait_ge(cDVE.h, dv[("zqev", zt)])
                tok0 = tkc * 512 + tt4 * 128
                for dch in range(8):
                    cZq.inc(sy.dma_start(zq_d[dch * 128:(dch + 1) * 128, tok0:tok0 + 128],
                                         zqs[:, dch * 128:(dch + 1) * 128]), 16)
                dq[("zqd", zt)] = cZq.v
    zq_all = cZq.v

    for tkc in range(TC8):
        slot = cZc[tkc % 2]

        @block.sync
        def _(sy, tkc=tkc, slot=slot):
            if tkc == 0:
                sy.wait_ge(cZq.h, zq_all)
            if tkc >= 2:
                pekey = ("g", tkc - 2, 3, len(NCH) - 1, "y")
                sy.wait_ge(cPE.h, pe[pekey])
            for dch in range(8):
                slot.inc(sy.dma_start(zqc[:, (tkc % 2) * 4096 + dch * 512:(tkc % 2) * 4096 + (dch + 1) * 512],
                                      zq_d[dch * 128:(dch + 1) * 128, tkc * 512:(tkc + 1) * 512]), 16)
            dq[("zc", tkc)] = slot.v

        for tt4 in range(4):
            for nci, (n0, NW) in enumerate(NCH):
                it = (tkc * 4 + tt4) * len(NCH) + nci
                par = it % 2

                @block.tensor
                def _(t, tkc=tkc, tt4=tt4, nci=nci, n0=n0, NW=NW, it=it, par=par, slot=slot):
                    if tt4 == 0 and nci == 0:
                        t.wait_ge(slot.h, dq[("zc", tkc)])
                        if tkc == 0:
                            t.wait_ge(cDVE.h, dv[("wuq", "done")])
                    if ("gsil", it - 2) in ac:
                        t.wait_ge(cACT.h, ac[("gsil", it - 2)])
                    if ("gh", it - 2) in dv:
                        t.wait_ge(cDVE.h, dv[("gh", it - 2)])
                    lofs = (tkc % 2) * 4096
                    for dch in range(8):
                        t.matmul(psA[:, par * 512:par * 512 + NW],
                                 zqc[:, lofs + dch * 512 + tt4 * 128: lofs + dch * 512 + tt4 * 128 + 128],
                                 wgq[:, dch * FF + n0: dch * FF + n0 + NW],
                                 start=(dch == 0), stop=(dch == 7))
                    pe[("g", tkc, tt4, nci, "g")] = None
                    for dch in range(8):
                        inst = t.matmul(psB[:, par * 512:par * 512 + NW],
                                        zqc[:, lofs + dch * 512 + tt4 * 128: lofs + dch * 512 + tt4 * 128 + 128],
                                        wuq[:, dch * FF + n0: dch * FF + n0 + NW],
                                        start=(dch == 0), stop=(dch == 7))
                    cPE.inc(inst)
                    pe[("g", tkc, tt4, nci, "y")] = cPE.v

                @block.scalar
                def _(s, it=it, par=par, NW=NW, tkc=tkc, tt4=tt4, nci=nci):
                    s.wait_ge(cPE.h, pe[("g", tkc, tt4, nci, "y")])
                    if ("gh", it - 1) in dv:
                        s.wait_ge(cDVE.h, dv[("gh", it - 1)])
                    inst = s.activation(scr[:, 512:512 + NW], psA[:, par * 512:par * 512 + NW],
                                        AF.Silu)
                    cACT.inc(inst)
                    ac[("gsil", it)] = cACT.v

                @block.vector
                def _(v, it=it, par=par, NW=NW):
                    v.wait_ge(cACT.h, ac[("gsil", it)])
                    inst = v.tensor_tensor(scr[:, 0:NW], scr[:, 512:512 + NW],
                                           psB[:, par * 512:par * 512 + NW], ALU.mult)
                    cDVE.inc(inst)
                    dv[("gh", it)] = cDVE.v

                emit_fp4(("h", it), scr[:, 0:NW], hqb[:, 0:NW], NW,
                         first_waits=((cDVE, dv[("gh", it)]),))

                nt = NW // 128

                @block.tensor
                def _(t, it=it, par=par, nt=nt):
                    t.wait_ge(cDVE.h, dv[("fp4", "h", it)])
                    if ("hqev", it - 2) in dv:
                        t.wait_ge(cDVE.h, dv[("hqev", it - 2)])
                    for k in range(nt):
                        inst = t.transpose(psT[:, par * 1024 + k * 128: par * 1024 + (k + 1) * 128],
                                           hqb[:, k * 128:(k + 1) * 128], identb[:])
                    cPE.inc(inst)
                    pe[("hq", it)] = cPE.v

                @block.vector
                def _(v, it=it, par=par, nt=nt):
                    v.wait_ge(cPE.h, pe[("hq", it)])
                    if ("hqd", it - 2) in dq:
                        v.wait_ge(cHd.h, dq[("hqd", it - 2)])
                    inst = v.tensor_copy(hqs[:, par * 512:par * 512 + nt * 128],
                                         psT[:, par * 1024: par * 1024 + nt * 128])
                    cDVE.inc(inst)
                    dv[("hqev", it)] = cDVE.v

                @block.sync
                def _(sy, it=it, par=par, nt=nt, tkc=tkc, tt4=tt4, n0=n0):
                    sy.wait_ge(cDVE.h, dv[("hqev", it)])
                    tok0 = tkc * 512 + tt4 * 128
                    for k in range(nt):
                        cHd.inc(sy.dma_start(hq_d[n0 + k * 128: n0 + (k + 1) * 128, tok0:tok0 + 128],
                                             hqs[:, par * 512 + k * 128: par * 512 + (k + 1) * 128]), 16)
                    dq[("hqd", it)] = cHd.v
    hq_all = cHd.v

    for tkc in range(TC8):
        slot = cHs[tkc % 2]
        hofs = (tkc % 2) * FT * 512

        @block.sync
        def _(sy, tkc=tkc, slot=slot, hofs=hofs):
            if tkc == 0:
                sy.wait_ge(cHd.h, hq_all)
                sy.wait_ge(cPE.h, pe[("g", TC8 - 1, 3, len(NCH) - 1, "y")])
            if tkc >= 2:
                sy.wait_ge(cPE.h, pe[("dn", tkc - 2, DT8 - 1)])
            for ft in range(FT):
                ap = (wgq if hofs + ft * 512 < 8 * FF else wuq)
                ofs = hofs + ft * 512
                if ofs + 512 <= 8 * FF:
                    dst = wgq[:, ofs:ofs + 512]
                else:
                    dst = wuq[:, ofs - 8 * FF: ofs - 8 * FF + 512]
                slot.inc(sy.dma_start(dst, hq_d[ft * 128:(ft + 1) * 128,
                                                tkc * 512:(tkc + 1) * 512]), 16)
            dq[("hs", tkc)] = slot.v

        for dt in range(DT8):
            psx = psA if ((tkc * DT8 + dt) % 2 == 0) else psB
            xslot = cX0t[dt % 2]

            @block.sync
            def _(sy, tkc=tkc, dt=dt, xslot=xslot):
                if tkc >= 1 or dt >= 2:
                    prevk = ("x6d", tkc, dt - 2) if dt >= 2 else ("x6d", tkc - 1, DT8 - 2 + dt)
                    sy.wait_ge(cX1d.h, dq[prevk])
                xslot.inc(sy.dma_start(x5s[:, (dt % 2) * 512:(dt % 2) * 512 + 512],
                                       x5T_d[dt * 128:(dt + 1) * 128,
                                             tkc * 512:(tkc + 1) * 512]), 16)
                dq[("x0t", tkc, dt)] = xslot.v

            @block.tensor
            def _(t, tkc=tkc, dt=dt, psx=psx, slot=slot, hofs=hofs):
                if dt == 0:
                    t.wait_ge(slot.h, dq[("hs", tkc)])
                key = ("x6", tkc, dt - 2) if dt >= 2 else (("x6", tkc - 1, DT8 - 2 + dt) if tkc >= 1 else None)
                if key is not None:
                    t.wait_ge(cDVE.h, dv[key])
                elif tkc == 0 and dt < 2:
                    t.wait_ge(cDVE.h, dv[("gh", TC8 * 4 * len(NCH) - 1)])
                for ft in range(FT):
                    ofs = hofs + ft * 512
                    if ofs + 512 <= 8 * FF:
                        rhs = wgq[:, ofs:ofs + 512]
                    else:
                        rhs = wuq[:, ofs - 8 * FF: ofs - 8 * FF + 512]
                    inst = t.matmul(psx[:, 0:512],
                                    wdq[:, ft * 1024 + dt * 128: ft * 1024 + (dt + 1) * 128],
                                    rhs, start=(ft == 0), stop=(ft == FT - 1))
                cPE.inc(inst)
                pe[("dn", tkc, dt)] = cPE.v

            @block.vector
            def _(v, tkc=tkc, dt=dt, psx=psx, xslot=xslot):
                v.wait_ge(cPE.h, pe[("dn", tkc, dt)])
                v.wait_ge(xslot.h, dq[("x0t", tkc, dt)])
                inst = v.tensor_tensor(x5s[:, (dt % 2) * 512:(dt % 2) * 512 + 512],
                                       psx[:, 0:512],
                                       x5s[:, (dt % 2) * 512:(dt % 2) * 512 + 512], ALU.add)
                cDVE.inc(inst)
                dv[("x6", tkc, dt)] = cDVE.v

            @block.sync
            def _(sy, tkc=tkc, dt=dt):
                sy.wait_ge(cDVE.h, dv[("x6", tkc, dt)])
                cX1d.inc(sy.dma_start(x6T_d[dt * 128:(dt + 1) * 128,
                                            tkc * 512:(tkc + 1) * 512],
                                      x5s[:, (dt % 2) * 512:(dt % 2) * 512 + 512]), 16)
                dq[("x6d", tkc, dt)] = cX1d.v

    E = LnEnv(block, scr, zqc[:, 0:2048], zqc[:, 2048:4096],
              x5tm[:, 0:512], x5tm[:, 512:1024], ztm[:, 0:512], ztm[:, 512:1024],
              zqtm, ones_bt, fc, psA, cXs, cOut, cPE, cACT, cDVE, pe, dv, ac, dq)
    emit_ln_generic(E, "ln6", x6T_d, z6_d, cX1d, cX1d.v)

    @block.sync
    def _(sy):
        sy.wait_ge(cOut.h, cOut.v)
        sy.wait_ge(cX1d.h, cX1d.v)
    return (b.finish() if standalone else b)



def build_launch_C6(b=None):
    standalone = b is None
    if standalone:
        b = Builder("C6")
    nc = b.nc
    SC6 = 1.0 / math.sqrt(128.0)
    x6T_d = b.din("x6T", [D, T], F32)
    z6_d = b.din("z6", [D, T], BF16)
    wqT_d = b.din("WqT", [D, D], F32)
    wkT_d = b.din("WkT", [D, 128], F32)
    wvT_d = b.din("WvT", [D, 128], F32)
    woT_d = b.din("WoT", [D, D], F32)
    peT_d = b.din("peT", [128, T], F32)
    kpar_d = b.din("kpar", [128, 1], F32)
    cmask_d = b.din("cmask", [128, 1920], BF16)
    ones_b_d = b.din("ones_b", [128, 128], BF16)
    identb_d = b.din("identb", [128, 128], BF16)
    x7T_d = b.dout("x7T", [D, T], F32)
    z7_d = b.dout("z7", [D, T], BF16)
    qp_d = b.dint("qp_scratch", [D, T], BF16)
    oT_d = b.dint("oT_scratch", [D, T], BF16)

    z6 = b.sb("z6_s", [128, DT8 * T], BF16)
    kT = b.sb("kT", [128, T], BF16)
    vtm = b.sb("vtm", [128, T], BF16)
    qph = b.sb("qph", [128, 2 * T], BF16)
    peT = b.sb("peT_s", [128, T], F32)
    wpr = b.sb("wpr", [128, 2 * DT8 * 128], BF16)
    vt2 = b.sb("vt2", [128, 2 * 512], BF16)
    qps = b.sb("qps", [128, 2 * 512], BF16)
    pr = b.sb("pr", [128, 4 * 512], BF16)
    oTs = b.sb("oTs", [128, 2 * 512], BF16)
    sqb = b.sb("sqb", [128, 512], BF16)
    dn = b.sb("dn", [1, 512], F32)
    cmq = b.sb("cmq", [1, 64], F32)
    cmk = b.sb("cmk", [1, 8], F32)
    csm = b.sb("csm", [1, 8], F32)
    negCS = b.sb("negCS", [128, 8], F32)
    cmask = b.sb("cmask_s", [128, 1920], BF16)
    kparc = b.sb("kparc", [128, 1], F32)
    onesf = b.sb("onesf", [1, 128], F32)
    ones_bt = b.sb("ones_bt", [128, 128], BF16)
    identb = b.sb("identb_s", [128, 128], BF16)
    fc = b.sb("fc", [128, 4], F32)
    x6s = b.sb("x6s", [128, 2 * 512], F32)
    rBb = x6s[:, 0:512]
    x7s = b.sb("x7s", [128, 2 * 512], F32)
    rstage = b.sb("rstage", [128, 2 * 4096], BF16)
    xs = b.sb("xs", [128, 2 * T], F32)
    xb0 = b.sb("xb0", [128, T // 2], BF16)
    xsq = b.sb("xsq", [128, T // 2], BF16)
    Mt = b.sb("Mt", [128, 512], F32)
    Rt = b.sb("Rt", [128, 512], F32)
    stat_sc = b.sb("stat_sc", [128, 512], F32)
    zsc = b.sb("zsc", [128, 512], F32)
    evz = qps

    psA = b.ps("psA", [128, 1024], F32)
    psB = b.ps("psB", [128, 1024], F32)
    psO = b.ps("psO", [128, 1024], F32)
    psT = b.ps("psT", [128, 1024], BF16)
    psP = b.ps("psP", [128, 512], F32)

    cLd = b.ctr("cLd")
    cZ = b.ctr("cZ")
    cWp = [b.ctr("cWp0"), b.ctr("cWp1")]
    cQd = b.ctr("cQd")
    cQh = [b.ctr("cQh0"), b.ctr("cQh1")]
    cOtd = b.ctr("cOtd")
    cWo = b.ctr("cWo")
    cRs = [b.ctr("cRs0"), b.ctr("cRs1")]
    cX0t = [b.ctr("cX0t0"), b.ctr("cX0t1")]
    cX1d = b.ctr("cX1d")
    cXs = [b.ctr("cXs0"), b.ctr("cXs1")]
    cOut = b.ctr("cOut")
    cPE = b.ctr("cPE")
    cACT = b.ctr("cACT")
    cDVE = b.ctr("cDVE")
    pe, dv, ac, dq = {}, {}, {}, {}

    block = b.get_block()

    @block.sync
    def _(sy):
        for ap_s, ap_d in ((ones_bt, ones_b_d), (identb, identb_d),
                           (cmask, cmask_d), (kparc, kpar_d)):
            cLd.inc(sy.dma_start(ap_s[:], ap_d[:, :]), 16)
        cLd.inc(sy.dma_start(peT[:], peT_d[:, :]), 16)
    ld_all = cLd.v

    @block.gpsimd
    def _(g):
        for j in range(DT8):
            cZ.inc(g.dma_start(z6[:, j * T:(j + 1) * T], z6_d[j * 128:(j + 1) * 128, :]), 16)
        for j in range(DT8):
            cWp[0].inc(g.dma_start(wpr[:, j * 128:(j + 1) * 128],
                                   wkT_d[j * 128:(j + 1) * 128, 0:128]), 16)
        dq[("wp", 0)] = cWp[0].v
    z_all = cZ.v

    @block.vector
    def _(v):
        v.wait_ge(cLd.h, ld_all)
        v.memset(onesf[:], 1.0)
        v.memset(fc[:, 0:1], 1e-5)
        inst = v.memset(fc[:, 1:2], -0.5)
        cDVE.inc(inst)
        v.wait_ge(cDVE.h, cDVE.v)
        inst = v.tensor_copy(ones_bt[:], ones_bt[:])
        cDVE.inc(inst)
        dv["consts"] = cDVE.v

    for sl in range(10):
        slot = cWp[sl % 2]

        if sl + 1 < 10:
            @block.gpsimd
            def _(g, sl=sl):
                nsl = sl + 1
                if nsl == 1:
                    nsrc, ncol = wvT_d, 0
                else:
                    nsrc, ncol = wqT_d, (nsl - 2) * 128
                nslot = cWp[nsl % 2]
                if sl >= 1:
                    g.wait_ge(cPE.h, pe[("pj", sl - 1, TC8 - 1)])
                for j in range(DT8):
                    nslot.inc(g.dma_start(
                        wpr[:, (nsl % 2) * DT8 * 128 + j * 128:(nsl % 2) * DT8 * 128 + (j + 1) * 128],
                        nsrc[j * 128:(j + 1) * 128, ncol:ncol + 128]), 16)
                dq[("wp", nsl)] = nslot.v

        for tc in range(TC8):
            psx = psA if (tc % 2 == 0) else psB
            par = (sl * TC8 + tc) % 2

            @block.tensor
            def _(t, sl=sl, tc=tc, psx=psx, slot=slot):
                if tc == 0:
                    t.wait_ge(slot.h, dq[("wp", sl)])
                    if sl == 0:
                        t.wait_ge(cZ.h, z_all)
                        t.wait_ge(cDVE.h, dv["consts"])
                if tc >= 2:
                    t.wait_ge(cDVE.h, dv[("pjev", sl, tc - 2)])
                elif sl >= 1:
                    t.wait_ge(cDVE.h, dv[("pjev", sl - 1, TC8 - 2 + tc)])
                for j in range(DT8):
                    inst = t.matmul(psx[:, 0:512],
                                    wpr[:, (sl % 2) * DT8 * 128 + j * 128:(sl % 2) * DT8 * 128 + (j + 1) * 128],
                                    z6[:, j * T + tc * 512: j * T + tc * 512 + 512],
                                    start=(j == 0), stop=(j == DT8 - 1))
                cPE.inc(inst)
                pe[("pj", sl, tc)] = cPE.v

            if sl == 0:
                @block.vector
                def _(v, sl=sl, tc=tc, psx=psx):
                    v.wait_ge(cPE.h, pe[("pj", sl, tc)])
                    inst = v.tensor_scalar(kT[:, tc * 512:(tc + 1) * 512], psx[:, 0:512],
                                           kparc[:, 0:1], None, ALU.add)
                    cDVE.inc(inst)
                    dv[("pjev", sl, tc)] = cDVE.v
            elif sl == 1:
                @block.vector
                def _(v, sl=sl, tc=tc, psx=psx):
                    v.wait_ge(cPE.h, pe[("pj", sl, tc)])
                    if tc >= 2:
                        v.wait_ge(cDVE.h, dv[("vtev", tc - 2)])
                    inst = v.tensor_copy(vt2[:, (tc % 2) * 512:(tc % 2) * 512 + 512], psx[:, 0:512])
                    cDVE.inc(inst)
                    dv[("pjev", sl, tc)] = cDVE.v

                @block.tensor
                def _(t, tc=tc):
                    t.wait_ge(cDVE.h, dv[("pjev", 1, tc)])
                    if tc >= 2:
                        t.wait_ge(cDVE.h, dv[("vtev", tc - 2)])
                    for k4 in range(4):
                        inst = t.transpose(psT[:, (tc % 2) * 512 + k4 * 128:(tc % 2) * 512 + (k4 + 1) * 128],
                                           vt2[:, (tc % 2) * 512 + k4 * 128:(tc % 2) * 512 + (k4 + 1) * 128],
                                           identb[:])
                    cPE.inc(inst)
                    pe[("vt", tc)] = cPE.v

                @block.vector
                def _(v, tc=tc):
                    v.wait_ge(cPE.h, pe[("vt", tc)])
                    inst = v.tensor_copy(vtm[:, tc * 512:(tc + 1) * 512],
                                         psT[:, (tc % 2) * 512:(tc % 2) * 512 + 512])
                    cDVE.inc(inst)
                    dv[("vtev", tc)] = cDVE.v
            else:
                h = sl - 2

                @block.vector
                def _(v, sl=sl, tc=tc, psx=psx):
                    v.wait_ge(cPE.h, pe[("pj", sl, tc)])
                    if ("qd", sl, tc - 2) in dq:
                        v.wait_ge(cQd.h, dq[("qd", sl, tc - 2)])
                    elif sl >= 3 and ("qd", sl - 1, TC8 - 2 + tc) in dq:
                        v.wait_ge(cQd.h, dq[("qd", sl - 1, TC8 - 2 + tc)])
                    if ("sq", sl, tc - 1) in ac:
                        v.wait_ge(cACT.h, ac[("sq", sl, tc - 1)])
                    elif sl >= 3 and ("sq", sl - 1, TC8 - 1) in ac:
                        v.wait_ge(cACT.h, ac[("sq", sl - 1, TC8 - 1)])
                    inst = v.tensor_tensor(qps[:, (tc % 2) * 512:(tc % 2) * 512 + 512],
                                           psx[:, 0:512], peT[:, tc * 512:(tc + 1) * 512], ALU.add)
                    cDVE.inc(inst)
                    dv[("pjev", sl, tc)] = cDVE.v

                @block.sync
                def _(sy, sl=sl, tc=tc, h=h):
                    sy.wait_ge(cDVE.h, dv[("pjev", sl, tc)])
                    cQd.inc(sy.dma_start(qp_d[h * 128:(h + 1) * 128, tc * 512:(tc + 1) * 512],
                                         qps[:, (tc % 2) * 512:(tc % 2) * 512 + 512]), 16)
                    dq[("qd", sl, tc)] = cQd.v

                @block.scalar
                def _(s, sl=sl, tc=tc):
                    s.wait_ge(cDVE.h, dv[("pjev", sl, tc)])
                    if ("nrm", sl, tc - 1) in pe:
                        s.wait_ge(cPE.h, pe[("nrm", sl, tc - 1)])
                    elif sl >= 3 and ("nrm", sl - 1, TC8 - 1) in pe:
                        s.wait_ge(cPE.h, pe[("nrm", sl - 1, TC8 - 1)])
                    inst = s.activation(sqb[:], qps[:, (tc % 2) * 512:(tc % 2) * 512 + 512], AF.Square)
                    cACT.inc(inst)
                    ac[("sq", sl, tc)] = cACT.v

                @block.tensor
                def _(t, sl=sl, tc=tc, par=par):
                    t.wait_ge(cACT.h, ac[("sq", sl, tc)])
                    if ("nred", sl, tc - 2) in dv:
                        t.wait_ge(cDVE.h, dv[("nred", sl, tc - 2)])
                    elif sl >= 3 and ("nred", sl - 1, TC8 - 2 + tc) in dv:
                        t.wait_ge(cDVE.h, dv[("nred", sl - 1, TC8 - 2 + tc)])
                    inst = t.matmul(psO[0:1, par * 512:par * 512 + 512], ones_bt[:, 0:1],
                                    sqb[:], start=True, stop=True)
                    cPE.inc(inst)
                    pe[("nrm", sl, tc)] = cPE.v

                @block.vector
                def _(v, sl=sl, tc=tc, par=par, h=h):
                    v.wait_ge(cPE.h, pe[("nrm", sl, tc)])
                    inst = v.tensor_reduce(cmq[0:1, h * 8 + tc: h * 8 + tc + 1],
                                           psO[0:1, par * 512:par * 512 + 512],
                                           axis=AX.X, op=ALU.max)
                    cDVE.inc(inst)
                    v.wait_ge(cDVE.h, cDVE.v)
                    dv[("nred", sl, tc)] = cDVE.v

    for tc in range(TC8):
        par = tc % 2

        @block.scalar
        def _(s, tc=tc):
            if tc == 0:
                s.wait_ge(cACT.h, ac[("sq", 9, TC8 - 1)])
                s.wait_ge(cDVE.h, dv[("pjev", 0, TC8 - 1)])
                s.wait_ge(cPE.h, pe[("nrm", 9, TC8 - 1)])
            if ("knrm", tc - 1) in pe:
                s.wait_ge(cPE.h, pe[("knrm", tc - 1)])
            inst = s.activation(sqb[:], kT[:, tc * 512:(tc + 1) * 512], AF.Square)
            cACT.inc(inst)
            ac[("ksq", tc)] = cACT.v

        @block.tensor
        def _(t, tc=tc, par=par):
            t.wait_ge(cACT.h, ac[("ksq", tc)])
            if tc == 0:
                t.wait_ge(cDVE.h, dv[("nred", 9, TC8 - 1)])
            if ("knred", tc - 2) in dv:
                t.wait_ge(cDVE.h, dv[("knred", tc - 2)])
            inst = t.matmul(psO[0:1, par * 512:par * 512 + 512], ones_bt[:, 0:1],
                            sqb[:], start=True, stop=True)
            cPE.inc(inst)
            pe[("knrm", tc)] = cPE.v

        @block.vector
        def _(v, tc=tc, par=par):
            v.wait_ge(cPE.h, pe[("knrm", tc)])
            inst = v.tensor_reduce(cmk[0:1, tc:tc + 1],
                                   psO[0:1, par * 512:par * 512 + 512],
                                   axis=AX.X, op=ALU.max)
            cDVE.inc(inst)
            v.wait_ge(cDVE.h, cDVE.v)
            dv[("knred", tc)] = cDVE.v

    @block.vector
    def _(v):
        v.wait_ge(cDVE.h, dv[("knred", TC8 - 1)])
        inst = v.tensor_reduce(csm[0:1, 0:1], cmk[0:1, 0:8], axis=AX.X, op=ALU.max)
        cDVE.inc(inst)
        v.wait_ge(cDVE.h, cDVE.v)
        dv["kmax"] = cDVE.v

    for h in range(8):
        @block.vector
        def _(v, h=h):
            if h == 0:
                v.wait_ge(cDVE.h, dv["kmax"])
            else:
                v.wait_ge(cDVE.h, dv[("negc", h - 1)])
            inst = v.tensor_reduce(csm[0:1, 1:2], cmq[0:1, h * 8:(h + 1) * 8],
                                   axis=AX.X, op=ALU.max)
            cDVE.inc(inst)
            v.wait_ge(cDVE.h, cDVE.v)
            inst = v.tensor_tensor(csm[0:1, 2:3], csm[0:1, 1:2], csm[0:1, 0:1], ALU.mult)
            cDVE.inc(inst)
            dv[("cprod", h)] = cDVE.v

        @block.scalar
        def _(s, h=h):
            s.wait_ge(cDVE.h, dv[("cprod", h)])
            s.activation(csm[0:1, 3:4], csm[0:1, 2:3], AF.Ln)
            inst = s.activation(csm[0:1, 4:5], csm[0:1, 3:4], AF.Exp, scale=0.5)
            cACT.inc(inst)
            ac[("csqrt", h)] = cACT.v

        @block.vector
        def _(v, h=h):
            v.wait_ge(cACT.h, ac[("csqrt", h)])
            inst = v.tensor_scalar(csm[0:1, 5:6], csm[0:1, 4:5], -1.05 * SC6, None, ALU.mult)
            cDVE.inc(inst)
            v.wait_ge(cDVE.h, cDVE.v)
            dv[("cneg", h)] = cDVE.v

        @block.tensor
        def _(t, h=h):
            t.wait_ge(cDVE.h, dv[("cneg", h)])
            inst = t.matmul(psO[:, 512:513], onesf[0:1, :], csm[0:1, 5:6],
                            start=True, stop=True)
            cPE.inc(inst)
            pe[("cbc", h)] = cPE.v

        @block.vector
        def _(v, h=h):
            v.wait_ge(cPE.h, pe[("cbc", h)])
            inst = v.tensor_copy(negCS[:, h:h + 1], psO[:, 512:513])
            cDVE.inc(inst)
            v.wait_ge(cDVE.h, cDVE.v)
            dv[("negc", h)] = cDVE.v

    for h in range(8):
        qslot = cQh[h % 2]

        @block.gpsimd
        def _(g, h=h, qslot=qslot):
            if h == 0:
                g.wait_ge(cQd.h, cQd.v)
                g.wait_ge(cDVE.h, dv[("negc", 7)])
            if h >= 2:
                g.wait_ge(cPE.h, pe[("rdg", h - 2, TC8 - 1, TC8 - 1)])
            qslot.inc(g.dma_start(qph[:, (h % 2) * T:(h % 2) * T + T],
                                  qp_d[h * 128:(h + 1) * 128, :]), 16)
            dq[("qh", h)] = qslot.v

        for tc in range(TC8):
            ngr = tc + 1

            def prevg(tc, gi, h=h):
                if gi >= 1:
                    return ("pbg", h, tc, gi - 1)
                if tc >= 1:
                    return ("pbg", h, tc - 1, tc - 1)
                if h >= 1:
                    return ("pbg", h - 1, TC8 - 1, TC8 - 1)
                return None

            def prevrd(tc, gi, h=h):
                if gi >= 1:
                    return ("rdg", h, tc, gi - 1)
                if tc >= 1:
                    return ("rdg", h, tc - 1, tc - 1)
                if h >= 1:
                    return ("rdg", h - 1, TC8 - 1, TC8 - 1)
                return None

            for gi in range(ngr):
                sts = [gi * 4 + m for m in range(4)]
                diagg = (gi == tc)

                @block.tensor
                def _(t, h=h, tc=tc, gi=gi, sts=sts, qslot=qslot):
                    if tc == 0 and gi == 0:
                        t.wait_ge(qslot.h, dq[("qh", h)])
                        if h == 0:
                            t.wait_ge(cDVE.h, dv[("vtev", TC8 - 1)])
                    pk = prevg(tc, gi)
                    if pk is not None:
                        t.wait_ge(cACT.h, ac[pk])
                    for m, st in enumerate(sts):
                        psx = psA if m < 2 else psB
                        half = (m % 2) * 512
                        inst = t.matmul(psx[:, half:half + 512],
                                        kT[:, st * 128:(st + 1) * 128],
                                        qph[:, (h % 2) * T + tc * 512:(h % 2) * T + tc * 512 + 512],
                                        start=True, stop=True)
                    cPE.inc(inst)
                    pe[("sg", h, tc, gi)] = cPE.v

                @block.scalar
                def _(s, h=h, tc=tc, gi=gi):
                    s.wait_ge(cPE.h, pe[("sg", h, tc, gi)])
                    pk = prevrd(tc, gi)
                    if pk is not None:
                        s.wait_ge(cPE.h, pe[pk])
                    for m in range(4):
                        psx = psA if m < 2 else psB
                        half = (m % 2) * 512
                        inst = s.activation(pr[:, m * 512:(m + 1) * 512],
                                            psx[:, half:half + 512],
                                            AF.Exp, bias=negCS[:, h:h + 1], scale=SC6)
                    cACT.inc(inst)
                    ac[("pbg", h, tc, gi)] = cACT.v

                if diagg:
                    @block.vector
                    def _(v, h=h, tc=tc, gi=gi, sts=sts):
                        v.wait_ge(cACT.h, ac[("pbg", h, tc, gi)])
                        for m, st in enumerate(sts):
                            dpos = st * 128 - tc * 512
                            mofs = 896 - dpos
                            inst = v.tensor_tensor(pr[:, m * 512:(m + 1) * 512],
                                                   pr[:, m * 512:(m + 1) * 512],
                                                   cmask[:, mofs:mofs + 512], ALU.mult)
                        cDVE.inc(inst)
                        dv[("pbg", h, tc, gi)] = cDVE.v

                @block.tensor
                def _(t, h=h, tc=tc, gi=gi, sts=sts, diagg=diagg, ngr=ngr):
                    if diagg:
                        t.wait_ge(cDVE.h, dv[("pbg", h, tc, gi)])
                    else:
                        t.wait_ge(cACT.h, ac[("pbg", h, tc, gi)])
                    if gi == 0:
                        pk = ("oev", h, tc - 2) if tc >= 2 else (("oev", h - 1, TC8 - 2 + tc) if h >= 1 else None)
                        if pk is not None and pk in dv:
                            t.wait_ge(cDVE.h, dv[pk])
                        pk2 = ("oev", h, tc - 1) if tc >= 1 else (("oev", h - 1, TC8 - 1) if h >= 1 else None)
                        if pk2 is not None and pk2 in dv:
                            t.wait_ge(cDVE.h, dv[pk2])
                    for m, st in enumerate(sts):
                        t.matmul(psO[:, (tc % 2) * 512:(tc % 2) * 512 + 512],
                                 vtm[:, st * 128:(st + 1) * 128],
                                 pr[:, m * 512:(m + 1) * 512],
                                 start=(gi == 0 and m == 0), stop=(gi == ngr - 1 and m == 3))
                        inst = t.matmul(psP[0:1, 0:512],
                                        ones_bt[:, 0:1],
                                        pr[:, m * 512:(m + 1) * 512],
                                        start=(gi == 0 and m == 0), stop=(gi == ngr - 1 and m == 3))
                    cPE.inc(inst)
                    pe[("rdg", h, tc, gi)] = cPE.v

            @block.vector
            def _(v, h=h, tc=tc, ngr=ngr):
                v.wait_ge(cPE.h, pe[("rdg", h, tc, ngr - 1)])
                inst = v.tensor_copy(dn[:], psP[0:1, 0:512])
                cDVE.inc(inst)
                v.wait_ge(cDVE.h, cDVE.v)
                inst = v.reciprocal(dn[:], dn[:])
                cDVE.inc(inst)
                v.wait_ge(cDVE.h, cDVE.v)
                dv[("dn", h, tc)] = cDVE.v

            @block.tensor
            def _(t, h=h, tc=tc):
                t.wait_ge(cDVE.h, dv[("dn", h, tc)])
                inst = t.matmul(psP[:, 0:512], onesf[0:1, :], dn[0:1, :],
                                start=True, stop=True)
                cPE.inc(inst)
                pe[("bc", h, tc)] = cPE.v

            @block.vector
            def _(v, h=h, tc=tc):
                v.wait_ge(cPE.h, pe[("bc", h, tc)])
                if ("otd", h, tc - 2) in dq:
                    v.wait_ge(cOtd.h, dq[("otd", h, tc - 2)])
                elif h >= 1 and ("otd", h - 1, TC8 - 2 + tc) in dq:
                    v.wait_ge(cOtd.h, dq[("otd", h - 1, TC8 - 2 + tc)])
                v.tensor_copy(rBb, psP[:, 0:512])
                inst = v.tensor_tensor(oTs[:, (tc % 2) * 512:(tc % 2) * 512 + 512],
                                       psO[:, (tc % 2) * 512:(tc % 2) * 512 + 512],
                                       rBb, ALU.mult)
                cDVE.inc(inst)
                dv[("oev", h, tc)] = cDVE.v

            @block.sync
            def _(sy, h=h, tc=tc):
                sy.wait_ge(cDVE.h, dv[("oev", h, tc)])
                cOtd.inc(sy.dma_start(oT_d[h * 128:(h + 1) * 128, tc * 512:(tc + 1) * 512],
                                      oTs[:, (tc % 2) * 512:(tc % 2) * 512 + 512]), 16)
                dq[("otd", h, tc)] = cOtd.v

    @block.gpsimd
    def _(g):
        g.wait_ge(cPE.h, pe[("rdg", 7, TC8 - 1, TC8 - 1)])
        for et in range(DT8):
            for j in range(DT8):
                cWo.inc(g.dma_start(z6[:, (et * DT8 + j) * 128:(et * DT8 + j + 1) * 128],
                                    woT_d[j * 128:(j + 1) * 128, et * 128:(et + 1) * 128]), 16)
    wo_all = cWo.v

    for tc in range(TC8):
        t0 = tc * 512
        rslot = cRs[tc % 2]

        @block.sync
        def _(sy, tc=tc, t0=t0, rslot=rslot):
            if tc == 0:
                sy.wait_ge(cOtd.h, cOtd.v)
            if tc >= 2:
                sy.wait_ge(cPE.h, pe[("wo", tc - 2, DT8 - 1)])
            for j in range(DT8):
                rslot.inc(sy.dma_start(rstage[:, (tc % 2) * 4096 + j * 512:(tc % 2) * 4096 + (j + 1) * 512],
                                       oT_d[j * 128:(j + 1) * 128, t0:t0 + 512]), 16)
            dq[("rs", tc)] = rslot.v

        for et in range(DT8):
            psx = psA if ((tc * DT8 + et) % 2 == 0) else psB
            xslot = cX0t[et % 2]

            @block.sync
            def _(sy, tc=tc, t0=t0, et=et, xslot=xslot):
                if tc >= 1 or et >= 2:
                    prevk = ("x7", tc, et - 2) if et >= 2 else ("x7", tc - 1, DT8 - 2 + et)
                    sy.wait_ge(cDVE.h, dv[prevk])
                xslot.inc(sy.dma_start(x6s[:, (et % 2) * 512:(et % 2) * 512 + 512],
                                       x6T_d[et * 128:(et + 1) * 128, t0:t0 + 512]), 16)
                dq[("x0t", tc, et)] = xslot.v

            @block.tensor
            def _(t, tc=tc, et=et, psx=psx, rslot=rslot):
                if et == 0:
                    t.wait_ge(rslot.h, dq[("rs", tc)])
                    if tc == 0:
                        t.wait_ge(cWo.h, wo_all)
                key = ("x7", tc, et - 2) if et >= 2 else (("x7", tc - 1, DT8 - 2 + et) if tc >= 1 else None)
                if key is not None:
                    t.wait_ge(cDVE.h, dv[key])
                elif tc == 0 and et < 2:
                    t.wait_ge(cDVE.h, dv[("oev", 7, TC8 - 1)])
                for j in range(DT8):
                    inst = t.matmul(psx[:, 0:512],
                                    z6[:, (et * DT8 + j) * 128:(et * DT8 + j + 1) * 128],
                                    rstage[:, (tc % 2) * 4096 + j * 512:(tc % 2) * 4096 + (j + 1) * 512],
                                    start=(j == 0), stop=(j == DT8 - 1))
                cPE.inc(inst)
                pe[("wo", tc, et)] = cPE.v

            @block.vector
            def _(v, tc=tc, et=et, psx=psx, xslot=xslot):
                v.wait_ge(cPE.h, pe[("wo", tc, et)])
                v.wait_ge(xslot.h, dq[("x0t", tc, et)])
                inst = v.tensor_tensor(x7s[:, (et % 2) * 512:(et % 2) * 512 + 512], psx[:, 0:512],
                                       x6s[:, (et % 2) * 512:(et % 2) * 512 + 512], ALU.add)
                cDVE.inc(inst)
                dv[("x7", tc, et)] = cDVE.v

            @block.sync
            def _(sy, tc=tc, et=et, t0=t0):
                sy.wait_ge(cDVE.h, dv[("x7", tc, et)])
                cX1d.inc(sy.dma_start(x7T_d[et * 128:(et + 1) * 128, t0:t0 + 512],
                                      x7s[:, (et % 2) * 512:(et % 2) * 512 + 512]), 16)
                dq[("x7d", tc, et)] = cX1d.v

    E = LnEnv(block, xs, xb0, xsq, Mt, Rt, stat_sc, zsc, evz, ones_bt, fc, psA,
              cXs, cOut, cPE, cACT, cDVE, pe, dv, ac, dq)
    emit_ln_generic(E, "ln7", x7T_d, z7_d, cX1d, cX1d.v)

    @block.sync
    def _(sy):
        sy.wait_ge(cOut.h, cOut.v)
        sy.wait_ge(cX1d.h, cX1d.v)
    return (b.finish() if standalone else b)



FF4 = 4096


def build_launch_C7(b=None):
    standalone = b is None
    if standalone:
        b = Builder("C7")
    nc = b.nc
    x7T_d = b.din("x7T", [D, T], F32)
    z7_d = b.din("z7", [D, T], BF16)
    wexpT_d = b.din("WexpT", [FF4, D], F32)
    wcon_d = b.din("Wcon", [FF4, D], F32)
    bexp_col_d = b.din("bexp_col", [128, 32], F32)
    upb_col_d = b.din("upb_col", [128, 32], F32)
    bcon_col_d = b.din("bcon_col", [128, 8], F32)
    dnb_col_d = b.din("dnb_col", [128, 8], F32)
    cdU_d = b.din("cdU", [128, 7 * 32], F32)
    coU_d = b.din("coU", [128, 7 * 32], F32)
    coPU_d = b.din("coPU", [128, 7 * 32], F32)
    cdD_d = b.din("cdD", [128, 7 * 32], F32)
    coD_d = b.din("coD", [128, 7 * 32], F32)
    coPD_d = b.din("coPD", [128, 7 * 32], F32)
    cxU_d = b.din("cxU", [128, 5 * 64], F32)
    cxD_d = b.din("cxD", [128, 5 * 64], F32)
    xs7_d = b.din("Xs7", [128, 7 * 128], BF16)
    identb_d = b.din("identb", [128, 128], BF16)
    x8T_d = b.dout("x8T", [D, T], F32)
    dS_d = b.dint("dS_scratch", [128, 10 * 64 * 128], BF16)
    h3_d = b.dint("h3_scratch", [FF4, T], BF16)

    wbig = b.sb("wbig", [128, 32 * 1024], BF16)
    slab = b.sb("slab", [128, 32 * 512], BF16)
    dSb = b.sb("dSb", [128, 2 * 64 * 128], BF16)
    z7c = b.sb("z7c", [128, 2 * 8 * 512], BF16)
    wrowb = b.sb("wrowb", [128, 1024], BF16)
    bexb = b.sb("bexb", [128, 32], BF16)
    b1f = b.sb("b1f", [128, 32], F32)
    cdc = b.sb("cdc", [128, 7 * 32], F32)
    coc = b.sb("coc", [128, 7 * 32], F32)
    copc = b.sb("copc", [128, 7 * 32], F32)
    cxc = b.sb("cxc", [128, 2 * 5 * 64], F32)
    xs7 = b.sb("xs7_s", [128, 7 * 128], BF16)
    identb = b.sb("identb_s", [128, 128], BF16)
    mS = b.sb("mS", [128, 128], BF16)
    mT = b.sb("mT", [128, 128], BF16)
    cur = b.sb("cur", [128, 128], BF16)
    btUp = b.sb("btUp", [128, 32 * 128], BF16)
    btDn = b.sb("btDn", [128, 32 * 128], BF16)
    b1c = b.sb("b1c", [128, 32], F32)
    upbc = b.sb("upbc", [128, 32], F32)
    bc2 = b.sb("bc2", [128, 8], F32)
    ev1 = b.sb("ev1", [128, 2 * 512], BF16)
    x7s = b.sb("x7s", [128, 2 * 512], F32)
    x8s = b.sb("x8s", [128, 2 * 512], F32)

    psA = b.ps("psA", [128, 1024], F32)
    psB = b.ps("psB", [128, 1024], F32)
    psO = b.ps("psO", [128, 1024], F32)
    psT = b.ps("psT", [128, 2048], BF16)

    cLd = b.ctr("cLd")
    cWt = b.ctr("cWt")
    cDs = b.ctr("cDs")
    cDl = [b.ctr("cDl0"), b.ctr("cDl1")]
    cZc = [b.ctr("cZc0"), b.ctr("cZc1")]
    cH3 = b.ctr("cH3")
    cHl = [b.ctr("cHl0"), b.ctr("cHl1")]
    cX0t = [b.ctr("cX0t0"), b.ctr("cX0t1")]
    cX1d = b.ctr("cX1d")
    cPE = b.ctr("cPE")
    cACT = b.ctr("cACT")
    cDVE = b.ctr("cDVE")
    pe, dv, ac, dq = {}, {}, {}, {}

    block = b.get_block()

    @block.sync
    def _(sy):
        for ap_s, ap_d in ((identb, identb_d), (cdc, cdU_d), (coc, coU_d),
                           (copc, coPU_d), (b1c, bexp_col_d), (upbc, upb_col_d)):
            cLd.inc(sy.dma_start(ap_s[:], ap_d[:, :]), 16)
        cLd.inc(sy.dma_start(xs7[:], xs7_d[:, :]), 16)
        cLd.inc(sy.dma_start(cxc[:, 0:320], cxU_d[:, :]), 16)
        cLd.inc(sy.dma_start(cxc[:, 320:640], cxD_d[:, :]), 16)
    ld_all = cLd.v

    def emit_b06(side, tile, dst_col):
        dstbuf = btUp if side == 0 else btDn

        @block.vector
        def _(v, side=side, tile=tile):
            key = ("b06", side, tile)
            if side == 0 and tile == 0:
                v.wait_ge(cLd.h, ld_all)
            else:
                v.wait_ge(cDVE.h, dv[("b06done", side, tile - 1) if tile >= 1
                                     else ("b06done", side - 1, 31)])
            inst = v.tensor_scalar(cur[:], identb[:], cdc[:, tile:tile + 1], None, ALU.mult)
            cDVE.inc(inst)
            v.wait_ge(cDVE.h, cDVE.v)
            inst = v.tensor_scalar(mT[:], xs7[:, 0:128], coc[:, tile:tile + 1], None, ALU.mult)
            cDVE.inc(inst)
            v.wait_ge(cDVE.h, cDVE.v)
            inst = v.tensor_tensor(cur[:], cur[:], mT[:], ALU.add)
            cDVE.inc(inst)
            v.wait_ge(cDVE.h, cDVE.v)
            dv[("seed", side, tile)] = cDVE.v

        for s in range(1, 7):
            @block.vector
            def _(v, side=side, tile=tile, s=s):
                if s == 1:
                    v.wait_ge(cDVE.h, dv[("seed", side, tile)])
                else:
                    v.wait_ge(cPE.h, pe[("compm", side, tile, s - 1)])
                col = s * 32 + tile
                inst = v.tensor_scalar(mS[:], identb[:], cdc[:, col:col + 1], None, ALU.mult)
                cDVE.inc(inst)
                v.wait_ge(cDVE.h, cDVE.v)
                inst = v.tensor_scalar(mT[:], xs7[:, s * 128:(s + 1) * 128],
                                       copc[:, col:col + 1], None, ALU.mult)
                cDVE.inc(inst)
                v.wait_ge(cDVE.h, cDVE.v)
                inst = v.tensor_tensor(mS[:], mS[:], mT[:], ALU.add)
                cDVE.inc(inst)
                v.wait_ge(cDVE.h, cDVE.v)
                dv[("m", side, tile, s)] = cDVE.v

            @block.tensor
            def _(t, side=side, tile=tile, s=s):
                t.wait_ge(cDVE.h, dv[("m", side, tile, s)])
                inst = t.matmul(psO[:, 0:128], mS[:], cur[:], start=True, stop=True)
                cPE.inc(inst)
                pe[("compm", side, tile, s)] = cPE.v

            @block.vector
            def _(v, side=side, tile=tile, s=s):
                v.wait_ge(cPE.h, pe[("compm", side, tile, s)])
                inst = v.tensor_copy(cur[:], psO[:, 0:128])
                cDVE.inc(inst)
                v.wait_ge(cDVE.h, cDVE.v)
                dv[("c", side, tile, s)] = cDVE.v

        @block.tensor
        def _(t, side=side, tile=tile):
            t.wait_ge(cDVE.h, dv[("c", side, tile, 6)])
            inst = t.transpose(psT[:, 256:384], cur[:], identb[:])
            cPE.inc(inst)
            pe[("b06t", side, tile)] = cPE.v

        @block.vector
        def _(v, side=side, tile=tile, dst_col=dst_col, dstbuf=dstbuf):
            v.wait_ge(cPE.h, pe[("b06t", side, tile)])
            inst = v.tensor_copy(dstbuf[:, dst_col:dst_col + 128], psT[:, 256:384])
            cDVE.inc(inst)
            v.wait_ge(cDVE.h, cDVE.v)
            dv[("b06done", side, tile)] = cDVE.v

    for tile in range(32):
        emit_b06(0, tile, tile * 128)

    dv[("btUp", "done")] = dv[("b06done", 0, 31)]

    @block.sync
    def _(sy):
        sy.wait_ge(cDVE.h, dv[("b06done", 0, 31)])
        cLd.inc(sy.dma_start(cdc[:], cdD_d[:, :]), 16)
        cLd.inc(sy.dma_start(coc[:], coD_d[:, :]), 16)
        cLd.inc(sy.dma_start(copc[:], coPD_d[:, :]), 16)
    ld2 = cLd.v

    @block.vector
    def _(v):
        v.wait_ge(cLd.h, ld2)
        inst = v.tensor_copy(mT[:], identb[:])
        cDVE.inc(inst)
        dv[("b06done", 0, 32)] = cDVE.v
    dv[("b06done", 0, 32 - 1)] = dv[("b06done", 0, 32)]
    for tile in range(32):
        if tile == 0:
            dv[("b06done", 0, 31)] = dv[("b06done", 0, 32)]
        emit_b06(1, tile, tile * 128)

    for k in range(10):
        @block.vector
        def _(v, k=k):
            if k == 0:
                v.wait_ge(cDVE.h, dv[("b06done", 1, 31)])
            else:
                v.wait_ge(cDs.h, dq[("ds", k - 1)])
            for idx in range(64):
                col = k * 64 + idx if k < 5 else 320 + (k - 5) * 64 + idx
                inst = v.tensor_scalar(dSb[:, idx * 128:(idx + 1) * 128], identb[:],
                                       cxc[:, col:col + 1], None, ALU.mult)
            cDVE.inc(inst)
            dv[("dsb", k)] = cDVE.v

        @block.sync
        def _(sy, k=k):
            sy.wait_ge(cDVE.h, dv[("dsb", k)])
            cDs.inc(sy.dma_start(dS_d[:, k * 8192:(k + 1) * 8192], dSb[:, 0:8192]), 16)
            dq[("ds", k)] = cDs.v

    @block.vector
    def _(v):
        v.wait_ge(cDs.h, dq[("ds", 9)])
        inst = v.tensor_copy(bexb[:], b1c[:])
        cDVE.inc(inst)
        v.wait_ge(cDVE.h, cDVE.v)
        dv["bexb"] = cDVE.v

    for t32 in range(32):
        @block.gpsimd
        def _(g, t32=t32):
            if t32 == 0:
                g.wait_ge(cDVE.h, dv[("btUp", "done")])
            else:
                g.wait_ge(cPE.h, pe[("fold", t32 - 1, DT8 - 1)])
            cWt.inc(g.dma_start(wrowb[:], wexpT_d[t32 * 128:(t32 + 1) * 128, :]), 16)
            dq[("wr", t32)] = cWt.v

        for rc in range(DT8):
            psx = psA if (rc % 2 == 0) else psB

            @block.tensor
            def _(t, t32=t32, rc=rc, psx=psx):
                if rc == 0:
                    t.wait_ge(cWt.h, dq[("wr", t32)])
                    if t32 == 0:
                        t.wait_ge(cDVE.h, dv["bexb"])
                if ("foldev", t32, rc - 2) in dv:
                    t.wait_ge(cDVE.h, dv[("foldev", t32, rc - 2)])
                elif t32 >= 1 and ("foldev", t32 - 1, DT8 - 2 + rc) in dv:
                    t.wait_ge(cDVE.h, dv[("foldev", t32 - 1, DT8 - 2 + rc)])
                inst = t.matmul(psx[:, 0:128], wrowb[:, rc * 128:(rc + 1) * 128],
                                btUp[:, t32 * 128:(t32 + 1) * 128], start=True, stop=True)
                cPE.inc(inst)
                pe[("fold", t32, rc)] = cPE.v

            @block.vector
            def _(v, t32=t32, rc=rc, psx=psx):
                v.wait_ge(cPE.h, pe[("fold", t32, rc)])
                inst = v.tensor_copy(wbig[:, t32 * 1024 + rc * 128: t32 * 1024 + (rc + 1) * 128],
                                     psx[:, 0:128])
                cDVE.inc(inst)
                dv[("foldev", t32, rc)] = cDVE.v

        @block.tensor
        def _(t, t32=t32):
            t.wait_ge(cPE.h, pe[("fold", t32, DT8 - 1)])
            inst = t.matmul(psO[:, (t32 % 2) * 512:(t32 % 2) * 512 + 1],
                            btUp[:, t32 * 128:(t32 + 1) * 128],
                            bexb[:, t32:t32 + 1], start=True, stop=True)
            cPE.inc(inst)
            pe[("bf", t32)] = cPE.v

        @block.vector
        def _(v, t32=t32):
            v.wait_ge(cPE.h, pe[("bf", t32)])
            inst = v.tensor_copy(b1f[:, t32:t32 + 1], psO[:, (t32 % 2) * 512:(t32 % 2) * 512 + 1])
            cDVE.inc(inst)
            v.wait_ge(cDVE.h, cDVE.v)
            dv[("b1f", t32)] = cDVE.v
    dv[("wexpF", "done")] = dv[("foldev", 31, DT8 - 1)]

    @block.sync
    def _(sy):
        sy.wait_ge(cLd.h, ld_all)
        cLd.inc(sy.dma_start(bc2[:], bcon_col_d[:, :]), 16)
        cLd.inc(sy.dma_start(x8s[:].bitcast(F32)[:, 0:8] if False else x7s[:, 0:8], dnb_col_d[:, :]), 16)
    ld3 = cLd.v

    @block.vector
    def _(v):
        v.wait_ge(cLd.h, ld3)
        inst = v.tensor_tensor(bc2[:], bc2[:], x7s[:, 0:8], ALU.add)
        cDVE.inc(inst)
        v.wait_ge(cDVE.h, cDVE.v)
        dv["bc2"] = cDVE.v

    STRIDE_T = [1, 2, 4, 8, 16]

    def pairs_of(s5):
        st = STRIDE_T[s5]
        return [(ta, ta + st) for ta in range(32) if (ta // st) % 2 == 0]

    for tkc in range(TC8):
        zslot = cZc[tkc % 2]

        @block.sync
        def _(sy, tkc=tkc, zslot=zslot):
            if tkc == 0:
                sy.wait_ge(cDVE.h, dv["bc2"])
            if tkc >= 2:
                sy.wait_ge(cPE.h, pe[("exp", tkc - 2, 31)])
            for dch in range(DT8):
                zslot.inc(sy.dma_start(z7c[:, (tkc % 2) * 4096 + dch * 512:(tkc % 2) * 4096 + (dch + 1) * 512],
                                       z7_d[dch * 128:(dch + 1) * 128, tkc * 512:(tkc + 1) * 512]), 16)
            dq[("zc", tkc)] = zslot.v

        for t32 in range(32):
            psx = psA if (t32 % 2 == 0) else psB

            @block.tensor
            def _(t, tkc=tkc, t32=t32, psx=psx, zslot=zslot):
                if t32 == 0:
                    t.wait_ge(zslot.h, dq[("zc", tkc)])
                    if tkc == 0:
                        t.wait_ge(cDVE.h, dv[("wexpF", "done")])
                        t.wait_ge(cDVE.h, dv[("b1f", 31)])
                if ("expev", tkc, t32 - 2) in dv:
                    t.wait_ge(cDVE.h, dv[("expev", tkc, t32 - 2)])
                elif tkc >= 1 and t32 < 2:
                    t.wait_ge(cDVE.h, dv[("stev", tkc - 1, 9, 14 + t32)])
                for rc in range(DT8):
                    inst = t.matmul(psx[:, 0:512],
                                    wbig[:, t32 * 1024 + rc * 128: t32 * 1024 + (rc + 1) * 128],
                                    z7c[:, (tkc % 2) * 4096 + rc * 512:(tkc % 2) * 4096 + (rc + 1) * 512],
                                    start=(rc == 0), stop=(rc == DT8 - 1))
                cPE.inc(inst)
                pe[("exp", tkc, t32)] = cPE.v

            @block.vector
            def _(v, tkc=tkc, t32=t32, psx=psx):
                v.wait_ge(cPE.h, pe[("exp", tkc, t32)])
                if ("h3d", tkc - 1, t32) in dq:
                    v.wait_ge(cH3.h, dq[("h3d", tkc - 1, t32)])
                inst = v.tensor_scalar(slab[:, t32 * 512:(t32 + 1) * 512], psx[:, 0:512],
                                       b1f[:, t32:t32 + 1], None, ALU.add)
                cDVE.inc(inst)
                dv[("expev", tkc, t32)] = cDVE.v
                dv[("sl", tkc, "exp", t32)] = cDVE.v

        last_stage_tag = "exp"
        for phase, sets in (("up", range(0, 5)), ("dn", range(5, 10))):
            if phase == "dn":
                for t32 in range(32):
                    @block.scalar
                    def _(s, tkc=tkc, t32=t32, lst=last_stage_tag):
                        s.wait_ge(cDVE.h, dv[("sl", tkc, lst, t32)])
                        if ("itev", tkc, t32 - 2) in dv:
                            s.wait_ge(cDVE.h, dv[("itev", tkc, t32 - 2)])
                        inst = s.activation(slab[:, t32 * 512:(t32 + 1) * 512],
                                            slab[:, t32 * 512:(t32 + 1) * 512],
                                            AF.Gelu, bias=upbc[:, t32:t32 + 1])
                        cACT.inc(inst)
                        ac[("gel", tkc, t32)] = cACT.v

                    psx = psA if (t32 % 2 == 0) else psB

                    @block.tensor
                    def _(t, tkc=tkc, t32=t32, psx=psx):
                        t.wait_ge(cACT.h, ac[("gel", tkc, t32)])
                        if ("itev", tkc, t32 - 2) in dv:
                            t.wait_ge(cDVE.h, dv[("itev", tkc, t32 - 2)])
                        elif t32 < 2:
                            t.wait_ge(cDVE.h, dv[("stev", tkc, 4, 14 + t32)])
                        inst = t.matmul(psx[:, 512:1024], btDn[:, t32 * 128:(t32 + 1) * 128],
                                        slab[:, t32 * 512:(t32 + 1) * 512], start=True, stop=True)
                        cPE.inc(inst)
                        pe[("it", tkc, t32)] = cPE.v

                    @block.vector
                    def _(v, tkc=tkc, t32=t32, psx=psx):
                        v.wait_ge(cPE.h, pe[("it", tkc, t32)])
                        inst = v.tensor_copy(slab[:, t32 * 512:(t32 + 1) * 512], psx[:, 512:1024])
                        cDVE.inc(inst)
                        dv[("itev", tkc, t32)] = cDVE.v
                        dv[("sl", tkc, "it", t32)] = cDVE.v
                last_stage_tag = "it"

            for si, k in enumerate(sets):
                dslot = cDl[k % 2]
                tag = ("st", tkc, k)

                @block.sync
                def _(sy, tkc=tkc, k=k, dslot=dslot):
                    if ("dsu", tkc, k - 2) in pe:
                        sy.wait_ge(cPE.h, pe[("dsu", tkc, k - 2)])
                    elif tkc >= 1 and ("dsu", tkc - 1, 8 + (k % 2)) in pe:
                        sy.wait_ge(cPE.h, pe[("dsu", tkc - 1, 8 + (k % 2))])
                    elif tkc == 0 and k < 2:
                        sy.wait_ge(cDs.h, dq[("ds", 9)])
                    for half in range(2):
                        dslot.inc(sy.dma_start(
                            dSb[:, (k % 2) * 8192 + half * 4096:(k % 2) * 8192 + (half + 1) * 4096],
                            dS_d[:, k * 8192 + half * 4096: k * 8192 + (half + 1) * 4096]), 16)
                    dq[("dsl", tkc, k)] = dslot.v

                for pi, (ta, tb) in enumerate(pairs_of(k % 5)):
                    psx = psA if (pi % 2 == 0) else psB

                    @block.tensor
                    def _(t, tkc=tkc, k=k, pi=pi, ta=ta, tb=tb, psx=psx,
                          dslot=dslot, lst=last_stage_tag):
                        if pi == 0:
                            t.wait_ge(dslot.h, dq[("dsl", tkc, k)])
                        t.wait_ge(cDVE.h, dv[("sl", tkc, lst, ta)])
                        t.wait_ge(cDVE.h, dv[("sl", tkc, lst, tb)])
                        if ("stev", tkc, k, pi - 2) in dv:
                            t.wait_ge(cDVE.h, dv[("stev", tkc, k, pi - 2)])
                        elif pi < 2 and k == 0:
                            t.wait_ge(cDVE.h, dv[("expev", tkc, 30 + pi)])
                        elif pi < 2 and k == 5:
                            t.wait_ge(cDVE.h, dv[("itev", tkc, 30 + pi)])
                        elif pi < 2 and k in (1, 2, 3, 4, 6, 7, 8, 9):
                            t.wait_ge(cDVE.h, dv[("stev", tkc, k - 1, 14 + pi)])
                        dof = (k % 2) * 8192
                        inst = t.matmul(psx[:, 0:512], dSb[:, dof + (ta * 2) * 128: dof + (ta * 2 + 1) * 128],
                                        slab[:, ta * 512:(ta + 1) * 512], start=True, stop=False)
                        inst = t.matmul(psx[:, 0:512], dSb[:, dof + (ta * 2 + 1) * 128: dof + (ta * 2 + 2) * 128],
                                        slab[:, tb * 512:(tb + 1) * 512], start=False, stop=True)
                        inst = t.matmul(psx[:, 512:1024], dSb[:, dof + (tb * 2) * 128: dof + (tb * 2 + 1) * 128],
                                        slab[:, tb * 512:(tb + 1) * 512], start=True, stop=False)
                        inst = t.matmul(psx[:, 512:1024], dSb[:, dof + (tb * 2 + 1) * 128: dof + (tb * 2 + 2) * 128],
                                        slab[:, ta * 512:(ta + 1) * 512], start=False, stop=True)
                        cPE.inc(inst)
                        pe[("stp", tkc, k, pi)] = cPE.v
                        if pi == len(pairs_of(k % 5)) - 1:
                            pe[("dsu", tkc, k)] = cPE.v

                    @block.vector
                    def _(v, tkc=tkc, k=k, pi=pi, ta=ta, tb=tb, psx=psx):
                        v.wait_ge(cPE.h, pe[("stp", tkc, k, pi)])
                        v.tensor_copy(slab[:, ta * 512:(ta + 1) * 512], psx[:, 0:512])
                        inst = v.tensor_copy(slab[:, tb * 512:(tb + 1) * 512], psx[:, 512:1024])
                        cDVE.inc(inst)
                        dv[("stev", tkc, k, pi)] = cDVE.v
                        dv[("sl", tkc, ("st", k), ta)] = cDVE.v
                        dv[("sl", tkc, ("st", k), tb)] = cDVE.v
                last_stage_tag = ("st", k)

        for t32 in range(32):
            @block.sync
            def _(sy, tkc=tkc, t32=t32, lst=last_stage_tag):
                sy.wait_ge(cDVE.h, dv[("sl", tkc, lst, t32)])
                cH3.inc(sy.dma_start(h3_d[t32 * 128:(t32 + 1) * 128, tkc * 512:(tkc + 1) * 512],
                                     slab[:, t32 * 512:(t32 + 1) * 512]), 16)
                dq[("h3d", tkc, t32)] = cH3.v
    h3_all = cH3.v

    @block.gpsimd
    def _(g):
        g.wait_ge(cPE.h, pe[("exp", TC8 - 1, 31)])
        for t32 in range(32):
            cWt.inc(g.dma_start(wbig[:, t32 * 1024:(t32 + 1) * 1024],
                                wcon_d[t32 * 128:(t32 + 1) * 128, :]), 16)
    wcon_all = cWt.v

    for tkc in range(TC8):
        hslot = cHl[tkc % 2]
        hbuf_ofs = 0 if tkc % 2 == 0 else None

        @block.sync
        def _(sy, tkc=tkc, hslot=hslot):
            if tkc == 0:
                sy.wait_ge(cH3.h, h3_all)
                sy.wait_ge(cPE.h, pe[("stp", TC8 - 1, 9, len(pairs_of(4)) - 1)])
            if tkc >= 2:
                sy.wait_ge(cPE.h, pe[("dn", tkc - 2, DT8 - 1)])
            for t32 in range(32):
                dst = (slab[:, t32 * 512:(t32 + 1) * 512] if tkc % 2 == 0
                       else dSb[:, t32 * 512:(t32 + 1) * 512])
                hslot.inc(sy.dma_start(dst, h3_d[t32 * 128:(t32 + 1) * 128,
                                                 tkc * 512:(tkc + 1) * 512]), 16)
            dq[("hl", tkc)] = hslot.v

        for dt in range(DT8):
            psx = psA if ((tkc * DT8 + dt) % 2 == 0) else psB
            xslot = cX0t[dt % 2]

            @block.sync
            def _(sy, tkc=tkc, dt=dt, xslot=xslot):
                if tkc >= 1 or dt >= 2:
                    prevk = ("x8d", tkc, dt - 2) if dt >= 2 else ("x8d", tkc - 1, DT8 - 2 + dt)
                    sy.wait_ge(cX1d.h, dq[prevk])
                xslot.inc(sy.dma_start(x7s[:, (dt % 2) * 512:(dt % 2) * 512 + 512],
                                       x7T_d[dt * 128:(dt + 1) * 128,
                                             tkc * 512:(tkc + 1) * 512]), 16)
                dq[("x0t", tkc, dt)] = xslot.v

            @block.tensor
            def _(t, tkc=tkc, dt=dt, psx=psx, hslot=hslot):
                if dt == 0:
                    t.wait_ge(hslot.h, dq[("hl", tkc)])
                    if tkc == 0:
                        t.wait_ge(cWt.h, wcon_all)
                key = ("x8", tkc, dt - 2) if dt >= 2 else (("x8", tkc - 1, DT8 - 2 + dt) if tkc >= 1 else None)
                if key is not None:
                    t.wait_ge(cDVE.h, dv[key])
                for t32 in range(32):
                    src = (slab[:, t32 * 512:(t32 + 1) * 512] if tkc % 2 == 0
                           else dSb[:, t32 * 512:(t32 + 1) * 512])
                    inst = t.matmul(psx[:, 0:512],
                                    wbig[:, t32 * 1024 + dt * 128: t32 * 1024 + (dt + 1) * 128],
                                    src, start=(t32 == 0), stop=(t32 == 31))
                cPE.inc(inst)
                pe[("dn", tkc, dt)] = cPE.v

            @block.vector
            def _(v, tkc=tkc, dt=dt, psx=psx, xslot=xslot):
                v.wait_ge(cPE.h, pe[("dn", tkc, dt)])
                v.wait_ge(xslot.h, dq[("x0t", tkc, dt)])
                pk = ("x8d", tkc, dt - 2) if dt >= 2 else (("x8d", tkc - 1, DT8 - 2 + dt) if tkc >= 1 else None)
                if pk is not None and pk in dq:
                    v.wait_ge(cX1d.h, dq[pk])
                v.tensor_scalar(x8s[:, (dt % 2) * 512:(dt % 2) * 512 + 512], psx[:, 0:512],
                                bc2[:, dt:dt + 1], None, ALU.add)
                inst = v.tensor_tensor(x8s[:, (dt % 2) * 512:(dt % 2) * 512 + 512],
                                       x8s[:, (dt % 2) * 512:(dt % 2) * 512 + 512],
                                       x7s[:, (dt % 2) * 512:(dt % 2) * 512 + 512], ALU.add)
                cDVE.inc(inst)
                dv[("x8", tkc, dt)] = cDVE.v

            @block.sync
            def _(sy, tkc=tkc, dt=dt):
                sy.wait_ge(cDVE.h, dv[("x8", tkc, dt)])
                cX1d.inc(sy.dma_start(x8T_d[dt * 128:(dt + 1) * 128,
                                            tkc * 512:(tkc + 1) * 512],
                                      x8s[:, (dt % 2) * 512:(dt % 2) * 512 + 512]), 16)
                dq[("x8d", tkc, dt)] = cX1d.v

    @block.sync
    def _(sy):
        sy.wait_ge(cX1d.h, cX1d.v)
    return (b.finish() if standalone else b)


def _emit_barrier(b):
    block = b.get_block()
    vals = [(c.h, c.v) for c in b.ctrs.values()]
    for eng in ("tensor", "vector", "scalar", "gpsimd", "sync"):
        @getattr(block, eng)
        def _(e, vals=tuple(vals)):
            for h, v in vals:
                e.wait_ge(h, v)


def build_launch_Cmerged():
    b = Builder("C", fused=True)
    phases = [(build_launch_C1, "c1_"), (build_launch_C2, "c2_"),
              (build_launch_C3, "c3_"), (build_launch_C4, "c4_"),
              (build_launch_C5, "c5_"), (build_launch_C6, "c6_"),
              (build_launch_C7, "c7_")]
    for i, (fn, pfx) in enumerate(phases):
        b.phase = pfx
        b.scope = ExitStack()
        fn(b)
        b.scope.close()
        b.scope = None
        if i + 1 < len(phases):
            _emit_barrier(b)
    return b.finish()


def prep_Cmerged(inputs, x1T, z1, uterm):
    import ml_dtypes
    ones_b = bfc(np.ones((128, 128)))
    ident_f = f32c(np.eye(128))
    identb = bfc(np.eye(128))
    cmask = np.ascontiguousarray(
        (np.arange(1920)[None, :] - 896 >= np.arange(128)[:, None]).astype(ml_dtypes.bfloat16))
    bh2 = bfc(np.concatenate([
        np.concatenate([np.ones((64, 1)), np.zeros((64, 1))], axis=0),
        np.concatenate([np.zeros((64, 1)), np.ones((64, 1))], axis=0)], axis=1))
    p = np.arange(128)[:, None]
    dmaster = np.maximum(p + 3967.0 - np.arange(4480)[None, :], 0.0).astype(np.float32)
    dmask = np.ascontiguousarray((np.arange(896)[None, :] <= p + 383).astype(ml_dtypes.bfloat16))
    cdU, coU, coPU = _intile_cols(inputs["L3_ffn_up_weights"])
    cdD, coD, coPD = _intile_cols(inputs["L3_ffn_down_weights"])
    shared = {}
    shared.update({"c1_" + k: v for k, v in {
        "woT": f32c(np.asarray(inputs["L0_ffn_wo"]).T),
        "bo_col": col_layout(inputs["L0_ffn_bo"], 8),
        "ones_b": ones_b, "uterm": uterm}.items()})
    shared.update({"c2_" + k: v for k, v in {
        "Wq": f32c(inputs["L1_attn_Wq"]), "Wk": f32c(inputs["L1_attn_Wk"]),
        "Wv": f32c(inputs["L1_attn_Wv"]), "Wo": f32c(inputs["L1_attn_Wo"]),
        "ones_b": ones_b, "ident_f": ident_f, "bh2": bh2, "cmask": cmask}.items()})
    shared.update({"c3_" + k: v for k, v in {
        "W1T": f32c(np.asarray(inputs["L1_ffn_W1"]).T),
        "W2T": f32c(np.asarray(inputs["L1_ffn_W2"]).T),
        "b1_col": col_layout(inputs["L1_ffn_b1"], 32),
        "b2_col": col_layout(inputs["L1_ffn_b2"], 8),
        "ones_b": ones_b, "ident_f": ident_f}.items()})
    shared.update({"c4_" + k: v for k, v in {
        "WqT": f32c(np.asarray(inputs["L2_attn_Wq"]).T),
        "WkT": f32c(np.asarray(inputs["L2_attn_Wk"]).T),
        "WvT": f32c(np.asarray(inputs["L2_attn_Wv"]).T),
        "WoT": f32c(np.asarray(inputs["L2_attn_Wo"]).T),
        "wd_col": f32c(np.asarray(inputs["L2_attn_Wd"]).reshape(8, 128).T),
        "bd": np.asarray(inputs["L2_attn_bd"], np.float32).reshape(1, 1),
        "oscale": np.asarray(inputs["L2_attn_out_scale"], np.float32).reshape(1, 1),
        "dmaster": dmaster, "dmask": dmask,
        "ones_b": ones_b, "ident_f": ident_f}.items()})
    shared.update({"c5_" + k: v for k, v in {
        "wg": f32c(inputs["L2_ffn_w_gate"]), "wu": f32c(inputs["L2_ffn_w_up"]),
        "wdn": f32c(inputs["L2_ffn_w_down"]),
        "ones_b": ones_b, "ident_f": ident_f, "identb": identb}.items()})
    shared.update({"c6_" + k: v for k, v in {
        "WqT": f32c(np.asarray(inputs["L3_attn_Wq"]).T),
        "WkT": f32c(np.asarray(inputs["L3_attn_Wk"]).T),
        "WvT": f32c(np.asarray(inputs["L3_attn_Wv"]).T),
        "WoT": f32c(np.asarray(inputs["L3_attn_Wo"]).T),
        "peT": f32c(_sin_pe(4096, 128).T), "kpar": f32c(_sin_pe(1, 128).T),
        "cmask": cmask, "ones_b": ones_b, "identb": identb}.items()})
    import ml_dtypes as mld
    shared.update({"c7_" + k: v for k, v in {
        "WexpT": f32c(np.asarray(inputs["L3_ffn_W_exp"]).T),
        "Wcon": f32c(inputs["L3_ffn_W_con"]),
        "bexp_col": col_layout(inputs["L3_ffn_b_exp"], 32),
        "upb_col": col_layout(inputs["L3_ffn_up_bias"], 32),
        "bcon_col": col_layout(inputs["L3_ffn_b_con"], 8),
        "dnb_col": col_layout(inputs["L3_ffn_down_bias"], 8),
        "cdU": cdU, "coU": coU, "coPU": coPU,
        "cdD": cdD, "coD": coD, "coPD": coPD,
        "cxU": _cross_cols(inputs["L3_ffn_up_weights"]),
        "cxD": _cross_cols(inputs["L3_ffn_down_weights"]),
        "Xs7": np.ascontiguousarray(_xs7_mats().astype(mld.bfloat16)),
        "identb": identb}.items()})
    maps = []
    for t in range(NCORES):
        m = dict(shared)
        m["c1_x1T"] = x1T[t]
        m["c1_z1"] = z1[t]
        maps.append(m)
    return maps
